# version 06
# baseline (speedup 1.0000x reference)
import sys
import numpy as np
import ml_dtypes

sys.path.insert(0, "/opt/trn_rl_repo")
import concourse.bass as bass
import concourse.mybir as mybir
from concourse.bass_utils import run_bass_kernel_spmd

F32 = mybir.dt.float32
BF16 = mybir.dt.bfloat16
I32 = mybir.dt.int32
AF = mybir.ActivationFunctionType
ALU = mybir.AluOpType
AX = mybir.AxisListType
bf16 = ml_dtypes.bfloat16

T = 2048
D = 1024
NT = T // 128
ND = D // 128
NC2 = T // 512
LN_EPS = 1e-5
DEBUG_SKIP = set()
_NAME_CTR = [0]


def uname(s):
    _NAME_CTR[0] += 1
    return "%s_%d" % (s, _NAME_CTR[0])
NDMA_SEMS = 16
STRICT_DMA = True
FFN_BAR = 0


class Prog:
    def __init__(self, nc, sems):
        self.nc = nc
        self.sems = sems
        self.cnt = {k: 0 for k in sems}
        self.hw = {}

    def eng(self, name):
        return getattr(self.nc, name)

    ENG_OF = {"dve": "vector", "act": "scalar", "gp": "gpsimd"}

    def inc(self, instr, sem, n=1):
        instr.then_inc(self.sems[sem], n)
        self.cnt[sem] += n
        eng = self.ENG_OF.get(sem)
        if eng is not None:
            self.wait(eng, sem, self.cnt[sem])
        return self.cnt[sem]

    def wait(self, eng, sem, tgt):
        if not tgt or tgt <= 0:
            return
        key = (eng, sem)
        if self.hw.get(key, -1) >= tgt:
            return
        self.eng(eng).wait_ge(self.sems[sem], tgt)
        self.hw[key] = tgt

    def barrier_into(self, engines):
        for c in engines:
            for s in self.cnt:
                self.wait(c, s, self.cnt[s])

    def dgrp(self):
        g = self.rr % NDMA_SEMS
        self.rr += 1
        return g

    def dma(self, q, out, in_, g):
        instr = self.eng(q).dma_start(out=out, in_=in_)
        self.inc(instr, "d%d" % g, 16)
        return g

    def dwait(self, eng, g):
        self.wait(eng, "d%d" % g, self.cnt["d%d" % g])


def ts(i, n=128):
    return slice(i * n, (i + 1) * n)


def rs(a, b):
    return slice(a, b)



def host_prep(inputs):
    c = {}
    f32 = np.float32

    def asf(x):
        return np.ascontiguousarray(np.asarray(x), dtype=f32)

    c["c_ident"] = np.eye(128, dtype=np.float32).astype(bf16)
    c["c_identf"] = np.eye(128, dtype=np.float32)
    c["c_triu"] = np.triu(np.ones((128, 128), dtype=np.float32)).astype(bf16)
    c["c_ones"] = np.ones((128, 128), dtype=np.float32).astype(bf16)
    c["c_onesf"] = np.ones((8, 128), dtype=np.float32)
    c["c_iota256"] = np.broadcast_to(np.arange(256, dtype=f32), (128, 256)).copy()
    dup = np.zeros((64, 128), dtype=np.float32)
    for j in range(128):
        dup[j % 64, j] = 1.0
    c["c_dup"] = dup.astype(bf16)

    c["w_l0_wq"] = asf(inputs["L0_attn_Wq"])
    c["w_l0_wk"] = asf(inputs["L0_attn_Wk"])
    c["w_l0_wv"] = asf(inputs["L0_attn_Wv"])
    for nm, key in (("w_l0_bq", "L0_attn_bq"), ("w_l0_bk", "L0_attn_bk"),
                    ("w_l0_bv", "L0_attn_bv")):
        c[nm] = np.ascontiguousarray(asf(inputs[key]).reshape(8, 128).T)
    c["b_l0_bo_row"] = asf(inputs["L0_attn_bo"]).reshape(1, 1024).astype(bf16)
    for nm, key in (("w_l0_aq", "L0_attn_alpha_q"), ("w_l0_ak", "L0_attn_alpha_k"),
                    ("w_l0_av", "L0_attn_alpha_v")):
        a = asf(inputs[key])
        ar = np.repeat(a.transpose(1, 2, 0), 64, axis=2)
        c[nm] = np.ascontiguousarray(
            ar.reshape(16, 2, 8, 128).transpose(3, 0, 1, 2).reshape(128, 256))
    e64 = np.eye(64, dtype=np.float32)
    c["c_p64"] = np.tile(np.concatenate([e64, e64], axis=1), (16, 1)).astype(bf16)
    col = asf(inputs["L0_attn_collapse"])
    c["w_l0_col"] = np.ascontiguousarray(
        np.repeat(col[:, :, None], 64, axis=2).transpose(2, 0, 1).reshape(64, 32))
    c["w_l0_woT"] = np.ascontiguousarray(asf(inputs["L0_attn_Wo"]).T)
    c["w_l0_gT"] = np.ascontiguousarray(np.asarray(inputs["L0_ffn_w_gate"]).T)
    c["w_l0_uT"] = np.ascontiguousarray(np.asarray(inputs["L0_ffn_w_up"]).T)
    c["w_l0_dT"] = np.ascontiguousarray(np.asarray(inputs["L0_ffn_w_down"]).T)

    c["w_l1_attnT"] = np.ascontiguousarray(asf(inputs["L1_attn_W_attn"]).T)
    c["w_l1_projT"] = np.ascontiguousarray(asf(inputs["L1_attn_W_proj"]).T)
    c["w_l1_win"] = asf(inputs["L1_ffn_Win"])
    c["w_l1_bin"] = asf(inputs["L1_ffn_bin_"]).reshape(64, 1)
    R = np.zeros((64, 1024), dtype=np.float32)
    for m in range(64):
        R[m, m * 16:(m + 1) * 16] = 1.0
    c["c_l1_R"] = R.astype(bf16)
    c["c_l1_w2mask"] = np.ascontiguousarray(R.T).astype(bf16)
    c["w_l1_w1col"] = np.ascontiguousarray(
        asf(inputs["L1_ffn_W1"]).reshape(8, 128).T)
    c["w_l1_b1col"] = np.ascontiguousarray(
        asf(inputs["L1_ffn_b1"]).reshape(8, 128).T)
    c["w_l1_w2col"] = np.ascontiguousarray(
        asf(inputs["L1_ffn_W2"]).reshape(8, 128).T)
    c["w_l1_b2"] = asf(inputs["L1_ffn_b2"]).reshape(64, 1)
    c["w_l1_wout"] = asf(inputs["L1_ffn_Wout"])
    c["b_l1_bout_row"] = asf(inputs["L1_ffn_bout"]).reshape(1, 1024).astype(bf16)

    c["w_l2_qT"] = np.ascontiguousarray(
        asf(inputs["L2_attn_WQ"]).transpose(2, 0, 1).reshape(1024, 1024))
    c["w_l2_kT"] = np.ascontiguousarray(
        asf(inputs["L2_attn_WK"]).transpose(2, 0, 1).reshape(1024, 1024))
    c["w_l2_vT"] = np.ascontiguousarray(
        asf(inputs["L2_attn_WV"]).transpose(2, 0, 1).reshape(1024, 1024))
    c["w_l2_oT"] = np.ascontiguousarray(
        asf(inputs["L2_attn_WO"]).transpose(0, 2, 1).reshape(1024, 1024))
    c["w_l2_w1"] = asf(inputs["L2_ffn_w1"][0])
    c["w_l2_w2"] = asf(inputs["L2_ffn_w2"][0])
    c["w_l2_b1col"] = np.ascontiguousarray(
        asf(inputs["L2_ffn_b1"][0]).reshape(32, 128).T)
    c["b_l2_b2_row"] = asf(inputs["L2_ffn_b2"][0]).reshape(1, 1024).astype(bf16)
    c["b_l2_gamma_row"] = asf(inputs["L2_ffn_gamma"]).reshape(1, 1024).astype(bf16)
    c["b_l2_beta_row"] = asf(inputs["L2_ffn_beta"]).reshape(1, 1024).astype(bf16)

    c["w_l3_qT"] = np.ascontiguousarray(asf(inputs["L3_attn_Wq"]).T)
    c["w_l3_kT"] = np.ascontiguousarray(asf(inputs["L3_attn_Wk"]).T)
    c["w_l3_vT"] = np.ascontiguousarray(asf(inputs["L3_attn_Wv"]).T)
    c["w_l3_oT"] = np.ascontiguousarray(asf(inputs["L3_attn_Wo"]).T)
    c["w_l3_gate"] = asf(inputs["L3_ffn_gate_w"][0])
    c["w_l3_experts"] = asf(inputs["L3_ffn_experts"])

    def sinus(length, dim):
        pos = np.arange(length, dtype=np.float64)[:, None]
        inv = np.exp(-np.log(10000.0) *
                     (np.arange(0, dim, 2, dtype=np.float64) / dim))
        ang = pos * inv[None, :]
        pe = np.zeros((length, dim), dtype=np.float64)
        pe[:, 0::2] = np.sin(ang)
        pe[:, 1::2] = np.cos(ang)
        return pe.astype(np.float32)
    c["c_l3_peT"] = np.ascontiguousarray(sinus(T, 128).T).astype(bf16)
    c["c_l3_pe1"] = sinus(1, 128).reshape(128, 1)

    x = np.ascontiguousarray(np.asarray(inputs["L0_attn_hidden_states"]),
                             dtype=f32)
    per_core = [{"x_in": x[b]} for b in range(8)]
    return c, per_core



class Bufs:
    pass


NEEDED_PREFIX = {0: ("w_l0",), 1: ("w_l1",), 2: ("w_l2",), 3: ("w_l3",)}


def declare_io(nc, common, layers):
    d = {}
    need = set()
    for k in common:
        if k.startswith(("c_", "b_")):
            need.add(k)
        for l in layers:
            for pre in NEEDED_PREFIX[l]:
                if k.startswith(pre):
                    need.add(k)
    for k in sorted(need):
        a = common[k]
        dt = {np.dtype(np.float32): F32, np.dtype(np.int32): I32,
              np.dtype(bf16): BF16}[a.dtype]
        d[k] = nc.dram_tensor(k, list(a.shape), dt, kind="ExternalInput").ap()
    d["x_in"] = nc.dram_tensor("x_in", [T, D], F32, kind="ExternalInput").ap()
    d["x_out"] = nc.dram_tensor("x_out", [T, D], F32, kind="ExternalOutput").ap()
    if 0 in layers:
        d["l0_wmix"] = nc.dram_tensor("l0_wmix", [48, 1024, 128], BF16,
                                      kind="Internal").ap()
    return d


def build_program(common, layers):
    import contextlib
    nc = bass.Bass("TRN2", target_bir_lowering=False)
    d = declare_io(nc, common, layers)
    _stk = contextlib.ExitStack()
    _dsems = [_stk.enter_context(nc.semaphore("d%d" % i))
              for i in range(NDMA_SEMS)]

    with (
        nc.sbuf_tensor(uname("X"), [128, NT * D], F32) as X,
        nc.sbuf_tensor(uname("xnT"), [128, ND * T], BF16) as xnT,
        nc.sbuf_tensor(uname("arena"), [128, 24576], BF16) as arena,
        nc.sbuf_tensor(uname("stats"), [128, 64], F32) as stats,
        nc.sbuf_tensor(uname("consts"), [128, 512], BF16) as consts,
        nc.sbuf_tensor(uname("identf"), [128, 128], F32) as identf,
        nc.sbuf_tensor(uname("onesf"), [8, 128], F32) as onesf,
        nc.sbuf_tensor(uname("iota256"), [128, 256], F32) as iota256,
        nc.sbuf_tensor(uname("colbuf"), [128, 128], F32) as colbuf,
        nc.sbuf_tensor(uname("scr_bf"), [128, 2048], BF16) as scr_bf,
        nc.sbuf_tensor(uname("scr_f1"), [128, 1024], F32) as scr_f1,
        nc.sbuf_tensor(uname("scr_f2"), [128, 1024], F32) as scr_f2,
        nc.semaphore(uname("dW")) as _s0,
        nc.semaphore(uname("dX")) as _s1,
        nc.semaphore(uname("pe")) as _s2,
        nc.semaphore(uname("act")) as _s3,
        nc.semaphore(uname("dve")) as _s4,
        nc.semaphore(uname("gp")) as _s5,
        nc.Block() as block,
    ):
        sems = {"dW": _s0, "dX": _s1, "pe": _s2, "act": _s3,
                "dve": _s4, "gp": _s5}
        for gi, h in enumerate(_dsems):
            sems["d%d" % gi] = h
        p = Prog(nc, sems)
        p.rr = 0
        B = Bufs()
        B.X, B.xnT, B.arena, B.stats = X, xnT, arena, stats
        B.consts, B.iota256, B.colbuf, B.onesf = consts, iota256, colbuf, onesf
        B.identf = identf
        B.scr_bf, B.scr_f1, B.scr_f2 = scr_bf, scr_f1, scr_f2
        B.d = d
        B.ident = consts[:, 0:128]
        B.triu = consts[:, 128:256]
        B.ones = consts[:, 256:384]
        B.dup = consts[0:64, 384:512]
        B._tok_ctx_evac = 0
        B._stage_toks = {}
        B._stage_grps = {}

        B.gC = p.dgrp()
        for nm, dst in (("c_ident", B.ident), ("c_triu", B.triu),
                        ("c_ones", B.ones), ("c_dup", B.dup),
                        ("c_identf", identf[:]), ("c_onesf", onesf[:]),
                        ("c_iota256", iota256[:])):
            p.dma("sync", dst, d[nm], B.gC)
        B.epscol = colbuf[:, 127:128]
        te = p.inc(nc.vector.memset(B.epscol, LN_EPS), "dve")
        B.tok_eps = te
        B.gX = p.dgrp()
        xr = d["x_in"].rearrange("(n p) d -> n p d", p=128)
        for i in range(NT):
            p.dma("sync", X[:, ts(i, D)], xr[i], B.gX)

        for li in layers:
            p.barrier_into(("vector", "scalar", "tensor", "gpsimd", "sync"))
            emit_layernorm(p, B)
            if li == 0:
                if "l0ffn" not in DEBUG_SKIP:
                    emit_l0_ffn(p, B)
                if "l0attn" not in DEBUG_SKIP:
                    emit_l0_attn(p, B)
            elif li == 1:
                emit_l1_ffn(p, B)
                emit_l1_attn(p, B)
            elif li == 2:
                emit_l2_ffn(p, B)
                emit_l2_attn(p, B)
            else:
                emit_l3_ffn(p, B)
                emit_l3_attn(p, B)

        p.barrier_into(("sync",))
        gOut = p.dgrp()
        xo = d["x_out"].rearrange("(n p) d -> n p d", p=128)
        for i in range(NT):
            p.dma("sync", xo[i], X[:, ts(i, D)], gOut)
        p.dwait("sync", gOut)
    _stk.close()
    return nc



def emit_layernorm(p, B):
    nc = p.nc
    X, stats = B.X, B.stats
    p.dwait("scalar", B.gX)
    p.dwait("scalar", B.gC)
    p.dwait("tensor", B.gC)
    p.dwait("vector", B.gC)
    mean, rstd = stats[:, 0:16], stats[:, 16:32]
    sums, negmean = stats[:, 32:48], stats[:, 48:64]
    for i in range(NT):
        t = p.inc(nc.scalar.activation(B.scr_bf[:, 0:1024], X[:, ts(i, D)],
                                       AF.Identity, accum_out=sums[:, i:i + 1]),
                  "act")
        p.wait("scalar", "act", t)
    p.wait("vector", "act", t)
    t = p.inc(nc.vector.tensor_scalar(mean, sums, 1.0 / D, None, ALU.mult), "dve")
    p.wait("vector", "dve", t)
    t = p.inc(nc.vector.tensor_scalar(negmean, mean, -1.0, None, ALU.mult), "dve")
    p.wait("scalar", "dve", t)
    for i in range(NT):
        t = p.inc(nc.scalar.activation(B.scr_bf[:, 0:1024], X[:, ts(i, D)],
                                       AF.Square, bias=negmean[:, i:i + 1],
                                       accum_out=sums[:, i:i + 1]), "act")
        p.wait("scalar", "act", t)
    p.wait("vector", "act", t)
    t = p.inc(nc.vector.tensor_scalar(rstd, sums, 1.0 / D, None, ALU.mult), "dve")
    p.wait("scalar", "dve", t)
    p.wait("scalar", "dve", B.tok_eps)
    t = p.inc(nc.scalar.activation(rstd, rstd, AF.Ln, bias=B.epscol), "act")
    p.wait("scalar", "act", t)
    t = p.inc(nc.scalar.activation(rstd, rstd, AF.Exp, scale=-0.5), "act")
    B.tok_lnstats = t
    p.wait("vector", "act", t)

    with nc.psum_tensor([128, 1024], BF16) as pt:
        tv2 = 0
        for i in range(NT):
            tv = p.inc(nc.vector.tensor_scalar(
                B.scr_bf[:, 0:1024], X[:, ts(i, D)], mean[:, i:i + 1],
                rstd[:, i:i + 1], ALU.subtract, ALU.mult), "dve")
            p.wait("tensor", "dve", tv)
            for j in range(ND):
                tp = p.inc(nc.tensor.transpose(pt[:, ts(j, 128)],
                                               B.scr_bf[:, ts(j, 128)],
                                               B.ident), "pe")
            p.wait("vector", "pe", tp)
            dst = B.xnT[:].rearrange("p (j t) -> p j t", t=T)[:, :, ts(i, 128)]
            src = pt[:].rearrange("p (j t) -> p j t", t=128)
            tv2 = p.inc(nc.vector.tensor_copy(dst, src), "dve")
            p.wait("tensor", "dve", tv2)
    B.tok_xnT = tv2


def xnt(B, dtile, cols):
    return B.xnT[:, dtile * T + cols.start: dtile * T + cols.stop]



def stream_cast(p, B, dram, rows0, nrt, cols0, ncols, dst, dst_off, staging,
                war_pe=None):
    nc = p.nc
    half = staging.shape[1] // 2
    nbuf = 2 if ncols <= half else 1
    assert ncols <= staging.shape[1]
    key = id(staging)
    toks = B._stage_toks.get(key, [0, 0])
    grps = B._stage_grps.get(key)
    if grps is None:
        grps = [p.dgrp(), p.dgrp()]
        B._stage_grps[key] = grps
    tok = 0
    p.wait("gpsimd", "pe", p.cnt["pe"] if war_pe is None else war_pe)
    for j in range(nrt):
        par = j % nbuf
        st = staging[:, par * half: par * half + ncols]
        p.wait("sync", "gp", toks[par] if nbuf == 2 else max(toks))
        src = dram[rows0 + j * 128: rows0 + (j + 1) * 128, cols0:cols0 + ncols]
        p.dma("sync", st, src, grps[par])
        p.dwait("gpsimd", grps[par])
        tok = p.inc(nc.gpsimd.tensor_copy(
            dst[:, dst_off + j * ncols: dst_off + (j + 1) * ncols], st), "gp")
        toks[par] = tok
    B._stage_toks[key] = toks
    return tok



def emit_negB_pair(p, B, QT, KT, ps, scale, col_q, tok_ready_pe, tok_ready_dve):
    nc = p.nc
    tdv = 0
    for h in range(2):
        hs = ts(h, 64)
        maxs = []
        tm_prev = tok_ready_pe
        for w, MT in ((0, QT), (1, KT)):
            p.wait("scalar", "pe", tm_prev)
            p.wait("scalar", "dve", max(tok_ready_dve, tdv))
            ta = p.inc(nc.scalar.activation(B.scr_bf[0:64, 0:2048], MT[hs, :],
                                            AF.Square), "act")
            p.wait("tensor", "act", ta)
            p.wait("tensor", "dve", tdv)
            for cc in range(4):
                tm = p.inc(nc.tensor.matmul(
                    ps[:, cc * 512:(cc + 1) * 512],
                    B.ones[0:64, :],
                    B.scr_bf[0:64, cc * 512:(cc + 1) * 512],
                    start=True, stop=True), "pe")
            tm_prev = tm
            p.wait("vector", "pe", tm)
            for cc in range(4):
                tdv = p.inc(nc.vector.tensor_reduce(
                    B.colbuf[:, 122 + cc: 123 + cc],
                    ps[:, cc * 512:(cc + 1) * 512], AX.X, ALU.max), "dve")
            p.wait("vector", "dve", tdv)
            cdst = B.colbuf[:, 120 + w: 121 + w]
            tdv = p.inc(nc.vector.tensor_reduce(cdst, B.colbuf[:, 122:126],
                                                AX.X, ALU.max), "dve")
            p.wait("vector", "dve", tdv)
            if getattr(B, "dbg_negb", None):
                B.dbg_negb(p, B, h, w, ps)
        c0 = B.colbuf[:, 120:121]
        c1 = B.colbuf[:, 121:122]
        tdv = p.inc(nc.vector.tensor_tensor(c0, c0, c1, ALU.mult), "dve")
        p.wait("scalar", "dve", tdv)
        tac = p.inc(nc.scalar.activation(c0, c0, AF.Ln), "act")
        p.wait("scalar", "act", tac)
        tac = p.inc(nc.scalar.activation(c0, c0, AF.Exp, scale=0.5), "act")
        p.wait("vector", "act", tac)
        tdv = p.inc(nc.vector.tensor_scalar(
            B.colbuf[:, col_q + h: col_q + h + 1], c0, -scale, None, ALU.mult),
            "dve")
    return tdv


def emit_vtok(p, B, VTf, vtok, pOf, nheads, vdim, width, tok_src_dve):
    nc = p.nc
    tdv = 0
    for h in range(nheads):
        p.wait("tensor", "dve", max(tok_src_dve, tdv))
        for j in range(NT):
            tp = p.inc(nc.tensor.transpose(
                pOf[:, j * vdim:(j + 1) * vdim],
                VTf[h * vdim:(h + 1) * vdim, ts(j, 128)],
                B.identf[h * vdim: h * vdim + vdim,
                         h * vdim: h * vdim + vdim]), "pe")
        p.wait("vector", "pe", tp)
        if getattr(B, "dbg_vtok", None):
            B.dbg_vtok(p, B, h, pOf)
        base = h * NT * width
        nj_half = (512 // vdim)
        for bk in range((NT * vdim) // 512):
            dst = vtok[:, base + bk * nj_half * width:
                       base + (bk + 1) * nj_half * width].rearrange(
                "p (j c) -> p j c", c=width)[:, :, 0:vdim]
            srcb = pOf[:, bk * 512:(bk + 1) * 512].rearrange(
                "p (j c) -> p j c", c=vdim)
            tdv = p.inc(nc.vector.tensor_copy(dst, srcb), "dve")
        if width > vdim:
            ocol = vtok[:, base: base + NT * width].rearrange(
                "p (j c) -> p j c", c=width)[:, :, vdim:vdim + 1]
            tdv = p.inc(nc.vector.memset(ocol, 1.0), "dve")
    return tdv


def attn_chunks_pair(p, B, QT, KT, vtok, attnT, pS, pO, col_q, scale,
                     ctx_store, vwidth=66):
    nc = p.nc
    tok_exp = {}
    tok_av = {}
    for c in range(NC2):
        q0 = c * 512
        jmax = 4 * c + 3
        t_prev_ep = B._tok_ctx_evac
        for j in range(jmax + 1):
            par = j % 2
            qlo = max(0, 128 * j - q0)
            p.wait("tensor", "act", tok_exp.get((par, j - 2), 0))
            if j == 0:
                p.wait("tensor", "dve", t_prev_ep)
            for h in range(2):
                tm = p.inc(nc.tensor.matmul(
                    pS[:, (par * 2 + h) * 512 + qlo: (par * 2 + h) * 512 + 512],
                    KT[ts(h, 64), ts(j, 128)],
                    QT[ts(h, 64), rs(q0 + qlo, q0 + 512)],
                    start=True, stop=True,
                    tile_position=(h * 64, 0)), "pe")
            tok_s = tm
            p.wait("scalar", "pe", tok_s)
            pin = pS[:, par * 1024: par * 1024 + 1024].rearrange(
                "p (h q) -> p h q", q=512)[:, :, qlo:512]
            pout = attnT[:, par * 1024: par * 1024 + 1024].rearrange(
                "p (h q) -> p h q", q=512)[:, :, qlo:512]
            te = p.inc(nc.scalar.activation(
                pout, pin, AF.Exp, scale=scale,
                bias=B.colbuf[:, col_q: col_q + 1]), "act")
            tok_exp[(par, j)] = te
            tmask = 0
            if 128 * j >= q0:
                p.wait("vector", "act", te)
                for h in range(2):
                    sl = rs((par * 2 + h) * 512 + qlo,
                            (par * 2 + h) * 512 + qlo + 128)
                    tmask = p.inc(nc.vector.tensor_tensor(
                        attnT[:, sl], attnT[:, sl], B.triu, ALU.mult), "dve")
            p.wait("tensor", "act", te)
            if tmask:
                p.wait("tensor", "dve", tmask)
            for h in range(2):
                for strip in range(2):
                    ta = p.inc(nc.tensor.matmul(
                        pO[0:65, (h * 2 + strip) * 512 + qlo:
                           (h * 2 + strip) * 512 + 512],
                        vtok[ts(strip, 64),
                             h * NT * vwidth + j * vwidth:
                             h * NT * vwidth + j * vwidth + 65],
                        attnT[ts(strip, 64), (par * 2 + h) * 512 + qlo:
                              (par * 2 + h) * 512 + 512],
                        start=(j == 0), stop=(j == jmax),
                        tile_position=(strip * 64, 0)), "pe")
            tok_av[(par, j)] = ta
        p.wait("vector", "pe", ta)
        for h in range(2):
            tv = p.inc(nc.vector.tensor_copy(
                B.scr_f1[0:65, 0:512],
                pO[0:65, (h * 2) * 512:(h * 2) * 512 + 512]), "dve")
            p.wait("vector", "dve", tv)
            tv = p.inc(nc.vector.tensor_tensor(
                B.scr_f1[0:65, 0:512], B.scr_f1[0:65, 0:512],
                pO[0:65, (h * 2 + 1) * 512:(h * 2 + 1) * 512 + 512], ALU.add),
                "dve")
            p.wait("vector", "dve", tv)
            tv = p.inc(nc.vector.reciprocal(B.scr_f2[0:1, 0:512],
                                            B.scr_f1[64:65, 0:512]), "dve")
            p.wait("tensor", "dve", tv)
            p.wait("tensor", "act", tok_exp[(jmax % 2, jmax)])
            tb = p.inc(nc.tensor.matmul(
                pS[0:64, 0:512], B.onesf[0:1, 0:64], B.scr_f2[0:1, 0:512],
                start=True, stop=True), "pe")
            p.wait("vector", "pe", tb)
            tv = p.inc(nc.vector.tensor_tensor(
                B.scr_f1[0:64, 0:512], B.scr_f1[0:64, 0:512], pS[0:64, 0:512],
                ALU.mult), "dve")
            p.wait("vector", "dve", tv)
            ctx_store(h, c)
            p.wait("vector", "dve", p.cnt["dve"])
        B._tok_ctx_evac = p.cnt["dve"]



def emit_l1_attn(p, B):
    emit_std_attn(p, B,
                  qkv=[("w_l1_attnT", lambda hp: hp * 128),
                       ("w_l1_attnT", lambda hp: 1024 + hp * 128),
                       ("w_l1_attnT", lambda hp: 2048 + hp * 128)],
                  wo_key="w_l1_projT")


def emit_l2_attn(p, B):
    emit_std_attn(p, B,
                  qkv=[("w_l2_qT", lambda hp: hp * 128),
                       ("w_l2_kT", lambda hp: hp * 128),
                       ("w_l2_vT", lambda hp: hp * 128)],
                  wo_key="w_l2_oT")


def emit_std_attn(p, B, qkv, wo_key):
    nc = p.nc
    d = B.d
    scale = 0.125
    with (
        nc.sbuf_tensor(uname("sa_qt"), [128, 2048], BF16) as QT,
        nc.sbuf_tensor(uname("sa_kt"), [128, 2048], BF16) as KT,
        nc.sbuf_tensor(uname("sa_vtf"), [128, 2048], F32) as VTf,
        nc.sbuf_tensor(uname("sa_vtok"), [128, 2 * NT * 66], BF16) as vtok,
        nc.sbuf_tensor(uname("sa_attnT"), [128, 2048], BF16) as attnT,
        nc.sbuf_tensor(uname("sa_wbuf"), [128, 3 * 1024], BF16) as wbuf,
        nc.sbuf_tensor(uname("sa_stage"), [128, 2048], F32) as stage,
        nc.psum_tensor([128, 2048], F32) as pS,
        nc.psum_tensor([128, 2048], F32) as pO,
    ):
        tok_proj_prev = 0
        for hp in range(8):
            tok_w = 0
            for w in range(3):
                key, coloff = qkv[w]
                tok_w = stream_cast(p, B, d[key], 0, 8, coloff(hp), 128, wbuf,
                                    w * 1024, stage, war_pe=tok_proj_prev)
            p.wait("tensor", "gp", tok_w)
            p.wait("tensor", "dve", B._tok_ctx_evac)
            tv = 0
            for w in range(3):
                for cc in range(4):
                    for dt in range(ND):
                        tm = p.inc(nc.tensor.matmul(
                            pS[:, cc * 512:(cc + 1) * 512],
                            wbuf[:, w * 1024 + dt * 128: w * 1024 + (dt + 1) * 128],
                            xnt(B, dt, rs(cc * 512, (cc + 1) * 512)),
                            start=(dt == 0), stop=(dt == ND - 1)), "pe")
                    p.wait("vector", "pe", tm)
                    if w < 2:
                        dstT = QT if w == 0 else KT
                        tv = p.inc(nc.vector.tensor_copy(
                            dstT[:, cc * 512:(cc + 1) * 512],
                            pS[:, cc * 512:(cc + 1) * 512]), "dve")
                    else:
                        tv = p.inc(nc.vector.tensor_copy(
                            VTf[:, cc * 512:(cc + 1) * 512],
                            pS[:, cc * 512:(cc + 1) * 512]), "dve")
                p.wait("tensor", "dve", tv)
            tok_qkv_pe = p.cnt["pe"]
            tok_proj_prev = tok_qkv_pe
            tok_qkv = tv
            tvt = emit_vtok(p, B, VTf, vtok, pO, 2, 64, 66, tok_qkv)
            tnb = emit_negB_pair(p, B, QT, KT, pS, scale, 60, tok_qkv_pe,
                                 tok_qkv)
            p.wait("vector", "dve", tnb)
            tnb = p.inc(nc.vector.tensor_tensor(
                B.colbuf[:, 62:63], B.colbuf[:, 60:61], B.colbuf[:, 61:62],
                ALU.min), "dve")
            p.wait("vector", "dve", tnb)
            p.wait("tensor", "dve", max(tnb, tvt))
            p.wait("scalar", "dve", max(tnb, tvt))

            def store(h, c, hp=hp):
                dst = B.arena[ts(h, 64),
                              hp * 2048 + c * 512: hp * 2048 + (c + 1) * 512]
                p.inc(nc.vector.tensor_copy(dst, B.scr_f1[0:64, 0:512]), "dve")
            if getattr(B, "dbg_attn_pre", None):
                B.dbg_attn_pre(p, B, hp, QT, KT, vtok)
            attn_chunks_pair(p, B, QT, KT, vtok, attnT, pS, pO, 62, scale,
                             store)
            if getattr(B, "dbg_attn_post", None):
                B.dbg_attn_post(p, B, hp)

        tok_x = {}
        tok_wq_prev = p.cnt["pe"]
        for quarter in range(4):
            tok_wp = stream_cast(p, B, d[wo_key], quarter * 256, 2, 0,
                                 1024, wbuf, 0, stage, war_pe=tok_wq_prev)
            p.wait("tensor", "gp", tok_wp)
            p.wait("tensor", "dve", B._tok_ctx_evac)
            for i in range(NT):
                par = i % 2
                p.wait("tensor", "dve", tok_x.get((quarter, i - 2), 0))
                for oc in range(2):
                    for hpq in range(2):
                        hp = quarter * 2 + hpq
                        tm = p.inc(nc.tensor.matmul(
                            pS[:, par * 1024 + oc * 512:
                               par * 1024 + oc * 512 + 512],
                            B.arena[:, hp * 2048 + i * 128: hp * 2048 + i * 128 + 128],
                            wbuf[:, hpq * 1024 + oc * 512: hpq * 1024 + (oc + 1) * 512],
                            start=(hpq == 0), stop=(hpq == 1)), "pe")
                p.wait("vector", "pe", tm)
                for oc in range(2):
                    t_ = p.inc(nc.vector.tensor_tensor(
                        B.X[:, i * D + oc * 512: i * D + (oc + 1) * 512],
                        B.X[:, i * D + oc * 512: i * D + (oc + 1) * 512],
                        pS[:, par * 1024 + oc * 512: par * 1024 + (oc + 1) * 512],
                        ALU.add), "dve")
                tok_x[(quarter, i)] = t_
            tok_wq_prev = p.cnt["pe"]
            p.wait("tensor", "dve", t_)
    B._tok_ctx_evac = 0
    p.barrier_into(("vector", "scalar", "tensor", "gpsimd", "sync"))


def emit_l1_ffn(p, B):
    nc = p.nc
    d = B.d
    with (
        nc.sbuf_tensor(uname("f1_w"), [128, 1536], BF16) as wsm,
        nc.sbuf_tensor(uname("f1_wout"), [64, 1024], BF16) as wout,
        nc.sbuf_tensor(uname("f1_cols"), [128, 16], F32) as cols,
        nc.sbuf_tensor(uname("f1_rows"), [1, 1024], BF16) as brow,
        nc.sbuf_tensor(uname("f1_st"), [64, 2048], BF16) as sT,
        nc.sbuf_tensor(uname("f1_gt"), [64, 2048], BF16) as gT,
        nc.sbuf_tensor(uname("f1_stage"), [128, 2048], F32) as stage,
        nc.psum_tensor([128, 2048], F32) as ps,
        nc.psum_tensor([128, 512], F32) as ps2,
        nc.psum_tensor([128, 1024], F32) as ps3,
    ):
        tok_w = stream_cast(p, B, d["w_l1_win"], 0, 8, 0, 64, wsm, 0, stage)
        gF = p.dgrp()
        p.dma("sync", wsm[0:64, 512:1536], d["c_l1_R"], gF)
        p.wait("sync", "act", p.cnt["act"])
        p.dma("sync", B.scr_bf[:, 0:512].rearrange("p (j c) -> p j c", c=64),
              d["c_l1_w2mask"].rearrange("(j p) c -> p j c", p=128), gF)
        p.dma("sync", cols[:, 0:8], d["w_l1_w1col"], gF)
        p.dma("sync", cols[:, 8:16], d["w_l1_b1col"], gF)
        p.dma("sync", B.colbuf[:, 96:104], d["w_l1_w2col"], gF)
        p.dma("sync", B.colbuf[0:64, 104:105], d["w_l1_bin"], gF)
        p.dma("sync", B.colbuf[0:64, 105:106], d["w_l1_b2"], gF)
        p.dma("sync", brow[:], d["b_l1_bout_row"], gF)
        gW2 = p.dgrp()
        p.wait("sync", "gp", max(B._stage_toks.get(id(stage), [0, 0])))
        p.dma("sync", stage[0:64, 0:1024], d["w_l1_wout"], gW2)
        p.dwait("gpsimd", gW2)
        tw2 = p.inc(nc.gpsimd.tensor_copy(wout[:], stage[0:64, 0:1024]), "gp")
        p.dwait("vector", gF)
        w2b = B.arena[:, 23 * 1024: 23 * 1024 + 512]
        for j in range(8):
            tbd = p.inc(nc.vector.tensor_scalar(
                w2b[:, j * 64:(j + 1) * 64],
                B.scr_bf[:, j * 64:(j + 1) * 64],
                B.colbuf[:, 96 + j: 97 + j], None, ALU.mult), "dve")

        p.wait("tensor", "gp", tok_w)
        p.wait("tensor", "dve", B.tok_xnT)
        for cc in range(4):
            for dt in range(ND):
                tm = p.inc(nc.tensor.matmul(
                    ps[0:64, cc * 512:(cc + 1) * 512],
                    wsm[:, dt * 64:(dt + 1) * 64],
                    xnt(B, dt, rs(cc * 512, (cc + 1) * 512)),
                    start=(dt == 0), stop=(dt == ND - 1)), "pe")
        p.wait("scalar", "pe", tm)
        p.dwait("scalar", gF)
        tst = p.inc(nc.scalar.activation(sT[:], ps[0:64, 0:2048], AF.Identity,
                                         bias=B.colbuf[0:64, 104:105]), "act")
        p.wait("tensor", "act", tst)
        p.dwait("tensor", gF)
        p.wait("tensor", "dve", tbd)
        tg = 0
        th = 0
        for cc in range(4):
            csl = rs(cc * 512, (cc + 1) * 512)
            for j in range(8):
                p.wait("tensor", "dve", th)
                tmr = p.inc(nc.tensor.matmul(
                    ps2[:, 0:512],
                    wsm[0:64, 512 + j * 128: 512 + (j + 1) * 128],
                    sT[0:64, csl], start=True, stop=True), "pe")
                p.wait("vector", "pe", tmr)
                p.wait("vector", "pe", tg)
                th = p.inc(nc.vector.tensor_scalar(
                    B.scr_bf[:, 512:1024], ps2[:, 0:512], cols[:, j:j + 1],
                    cols[:, 8 + j: 8 + j + 1], ALU.mult, ALU.add), "dve")
                p.wait("vector", "dve", th)
                th = p.inc(nc.vector.tensor_scalar(
                    B.scr_bf[:, 512:1024], B.scr_bf[:, 512:1024], 0.0, None,
                    ALU.max), "dve")
                p.wait("tensor", "dve", th)
                tg = p.inc(nc.tensor.matmul(
                    ps[0:64, csl],
                    w2b[:, j * 64:(j + 1) * 64],
                    B.scr_bf[:, 512:1024],
                    start=(j == 0), stop=(j == 7)), "pe")
        p.wait("scalar", "pe", tg)
        tgt = p.inc(nc.scalar.activation(gT[:], ps[0:64, 0:2048], AF.Identity,
                                         bias=B.colbuf[0:64, 105:106]), "act")
        p.wait("tensor", "act", tgt)
        p.wait("tensor", "gp", tw2)
        p.dwait("tensor", gF)
        tx = 0
        for i in range(NT):
            p.wait("tensor", "dve", tx)
            for oc in range(2):
                osl = rs(oc * 512, (oc + 1) * 512)
                tm = p.inc(nc.tensor.matmul(
                    ps3[:, osl], B.ones[0:1, :], brow[0:1, osl],
                    start=True, stop=False), "pe")
                tm = p.inc(nc.tensor.matmul(
                    ps3[:, osl], gT[0:64, ts(i, 128)], wout[0:64, osl],
                    start=False, stop=True), "pe")
            p.wait("vector", "pe", tm)
            for oc in range(2):
                tx = p.inc(nc.vector.tensor_tensor(
                    B.X[:, i * D + oc * 512: i * D + (oc + 1) * 512],
                    B.X[:, i * D + oc * 512: i * D + (oc + 1) * 512],
                    ps3[:, oc * 512:(oc + 1) * 512], ALU.add), "dve")
    p.barrier_into(("vector", "scalar", "tensor", "gpsimd", "sync"))


def emit_l0_ffn(p, B):
    nc = p.nc
    d = B.d
    S2 = float(1.0 / (127.0 * 127.0))
    with (
        nc.sbuf_tensor(uname("f0_wg"), [128, 2048], BF16) as wgbuf,
        nc.sbuf_tensor(uname("f0_wu"), [128, 2048], BF16) as wubuf,
        nc.sbuf_tensor(uname("f0_silu"), [128, 1024], BF16) as silu,
        nc.sbuf_tensor(uname("f0_stage"), [128, 2048], I32) as stage,
        nc.psum_tensor([128, 4096], F32) as pEO,
    ):
        tok_ep = [0, 0, 0, 0]
        tok_g2 = 0
        tok_hh = [0, 0]
        tok_g2h = [0, 0]
        tok_gmm = [0, 0]
        tok_umm = [0, 0]
        for tc in range(4):
            csl = rs(tc * 512, (tc + 1) * 512)
            tok_h = 0
            tok_gate_evac = 0
            for f in range(24):
                par = f % 2
                tok_w = stream_cast(p, B, d["w_l0_gT"], 0, 8, f * 128, 128,
                                    wgbuf, par * 1024, stage,
                                    war_pe=tok_gmm[par])
                p.wait("tensor", "gp", tok_w)
                p.wait("tensor", "dve", B.tok_xnT)
                p.wait("tensor", "act", tok_g2h[par])
                p.wait("tensor", "dve", max(tok_ep))
                for dt in range(ND):
                    tm = p.inc(nc.tensor.matmul(
                        pEO[:, par * 512: par * 512 + 512],
                        wgbuf[:, par * 1024 + dt * 128: par * 1024 + (dt + 1) * 128],
                        xnt(B, dt, csl),
                        start=(dt == 0), stop=(dt == ND - 1)), "pe")
                tok_gmm[par] = tm
                p.wait("scalar", "pe", tm)
                p.wait("scalar", "dve", tok_hh[par])
                tok_g1 = p.inc(nc.scalar.activation(
                    B.scr_f1[:, par * 512: par * 512 + 512],
                    pEO[:, par * 512: par * 512 + 512], AF.Tanh,
                    scale=float(0.5 / 127.0)), "act")
                tok_g2 = p.inc(nc.scalar.activation(
                    B.scr_f2[:, par * 512: par * 512 + 512],
                    pEO[:, par * 512: par * 512 + 512], AF.Identity,
                    scale=float(0.5 / 127.0)), "act")
                tok_g2h[par] = tok_g2
                p.wait("vector", "act", tok_g2)
                tok_gate_evac = p.inc(nc.vector.scalar_tensor_tensor(
                    silu[:, par * 512: par * 512 + 512],
                    B.scr_f1[:, par * 512: par * 512 + 512], 1.0,
                    B.scr_f2[:, par * 512: par * 512 + 512],
                    ALU.add, ALU.mult), "dve")
                tok_wu = stream_cast(p, B, d["w_l0_uT"], 0, 8, f * 128, 128,
                                     wubuf, par * 1024, stage,
                                     war_pe=tok_umm[par])
                p.wait("tensor", "gp", tok_wu)
                p.wait("tensor", "dve", tok_hh[par])
                for dt in range(ND):
                    tm = p.inc(nc.tensor.matmul(
                        pEO[:, 1024 + par * 512: 1024 + (par + 1) * 512],
                        wubuf[:, par * 1024 + dt * 128: par * 1024 + (dt + 1) * 128],
                        xnt(B, dt, csl),
                        start=(dt == 0), stop=(dt == ND - 1)), "pe")
                tok_umm[par] = tm
                p.wait("vector", "pe", tm)
                p.wait("vector", "dve", tok_gate_evac)
                tok_h = p.inc(nc.vector.tensor_tensor(
                    B.arena[:, f * 512:(f + 1) * 512],
                    silu[:, par * 512: par * 512 + 512],
                    pEO[:, 1024 + par * 512: 1024 + (par + 1) * 512],
                    ALU.mult), "dve")
                tok_hh[par] = tok_h
            if getattr(B, "dbg_l0h", None):
                B.dbg_l0h(p, B, tc)
            if FFN_BAR >= 1:
                p.barrier_into(("vector", "scalar", "tensor", "gpsimd", "sync"))
            p.wait("tensor", "dve", tok_h)
            tok_dmm = [max(tok_gmm), max(tok_gmm)]
            for f in range(24):
                par = f % 2
                tok_wd = stream_cast(p, B, d["w_l0_dT"], f * 128, 1, 0, 1024,
                                     wgbuf, par * 1024, stage,
                                     war_pe=tok_dmm[par])
                p.wait("tensor", "gp", tok_wd)
                for i in range(4):
                    for oc in range(2):
                        tm = p.inc(nc.tensor.matmul(
                            pEO[:, i * 1024 + oc * 512:
                                i * 1024 + (oc + 1) * 512],
                            B.arena[:, f * 512 + i * 128: f * 512 + (i + 1) * 128],
                            wgbuf[:, par * 1024 + oc * 512:
                                  par * 1024 + (oc + 1) * 512],
                            start=(f == 0), stop=(f == 23)), "pe")
                tok_dmm[par] = tm
            tok_gmm = [tm, tm]
            if FFN_BAR >= 2:
                p.barrier_into(("vector", "scalar", "tensor", "gpsimd", "sync"))
            p.wait("vector", "pe", tm)
            for i in range(4):
                it = tc * 4 + i
                for oc in range(2):
                    xs = rs(it * D + oc * 512, it * D + (oc + 1) * 512)
                    tv = p.inc(nc.vector.scalar_tensor_tensor(
                        B.X[:, xs],
                        pEO[:, i * 1024 + oc * 512: i * 1024 + (oc + 1) * 512],
                        S2, B.X[:, xs], ALU.mult, ALU.add), "dve")
                tok_ep[i] = tv
            if FFN_BAR >= 3:
                p.barrier_into(("vector", "scalar", "tensor", "gpsimd", "sync"))
    p.barrier_into(("vector", "scalar", "tensor", "gpsimd", "sync"))


def emit_l0_attn(p, B):
    nc = p.nc
    d = B.d
    scale = 0.125
    wmix_d = d["l0_wmix"]
    with (
        nc.sbuf_tensor(uname("a0_wfull"), [128, 8192], BF16) as wfull,
        nc.sbuf_tensor(uname("a0_stage"), [128, 2048], F32) as stage,
        nc.sbuf_tensor(uname("a0_p64"), [128, 1024], BF16) as p64s,
        nc.sbuf_tensor(uname("a0_acol"), [128, 768], F32) as acol,
        nc.sbuf_tensor(uname("a0_bcolf"), [128, 24], F32) as bcolf,
        nc.sbuf_tensor(uname("a0_bcol"), [128, 24], BF16) as bcol,
        nc.sbuf_tensor(uname("a0_abuf"), [128, 1024], BF16) as abuf,
        nc.sbuf_tensor(uname("a0_wmixT"), [128, 1024], BF16) as wmixT,
        nc.psum_tensor([128, 1024], F32) as pW,
        nc.psum_tensor([128, 512], F32) as pB2,
        nc.psum_tensor([128, 1024], BF16) as pT,
    ):
        gA = p.dgrp()
        p.dma("sync", p64s[:].rearrange("p (a c) -> p a c", c=128),
              d["c_p64"].rearrange("(a p) c -> p a c", p=128), gA)
        for w, (ak, bk) in enumerate((("w_l0_aq", "w_l0_bq"),
                                      ("w_l0_ak", "w_l0_bk"),
                                      ("w_l0_av", "w_l0_bv"))):
            p.dma("sync", acol[:, w * 256:(w + 1) * 256], d[ak], gA)
            p.dma("sync", bcolf[:, w * 8:(w + 1) * 8], d[bk], gA)
        p.dwait("vector", gA)
        p.dwait("tensor", gA)
        tcv = p.inc(nc.vector.tensor_copy(bcol[:], bcolf[:]), "dve")
        p.wait("tensor", "dve", tcv)
        tok_mm = 0
        tok_st = 0
        gWS = p.dgrp()
        for w, wk in enumerate(("w_l0_wq", "w_l0_wk", "w_l0_wv")):
            tok_w = stream_cast(p, B, d[wk], 0, 8, 0, 1024, wfull, 0, stage,
                                war_pe=tok_mm)
            for h in range(16):
                p.wait("vector", "pe", tok_mm)
                for ot in range(8):
                    for ph in range(2):
                        ta = p.inc(nc.vector.tensor_scalar(
                            abuf[:, ot * 128 + ph * 64: ot * 128 + ph * 64 + 64],
                            p64s[:, ot * 128 + ph * 64: ot * 128 + ph * 64 + 64],
                            acol[:, w * 256 + h * 16 + ph * 8 + ot:
                                 w * 256 + h * 16 + ph * 8 + ot + 1],
                            None, ALU.mult), "dve")
                p.wait("tensor", "dve", ta)
                p.wait("tensor", "gp", tok_w)
                p.wait("tensor", "dve", tok_wm_ev if (w or h) else 0)
                for ot in range(8):
                    for oc in range(2):
                        tok_mm = p.inc(nc.tensor.matmul(
                            pW[:, oc * 512:(oc + 1) * 512],
                            abuf[:, ot * 128:(ot + 1) * 128],
                            wfull[:, ot * 1024 + oc * 512:
                                  ot * 1024 + (oc + 1) * 512],
                            start=(ot == 0), stop=(ot == 7)), "pe")
                    tok_mm = p.inc(nc.tensor.matmul(
                        pB2[:, 0:1], abuf[:, ot * 128:(ot + 1) * 128],
                        bcol[:, w * 8 + ot: w * 8 + ot + 1],
                        start=(ot == 0), stop=(ot == 7)), "pe")
                p.wait("vector", "pe", tok_mm)
                for oc in range(2):
                    tv = p.inc(nc.vector.tensor_copy(
                        B.scr_bf[:, oc * 512:(oc + 1) * 512],
                        pW[:, oc * 512:(oc + 1) * 512]), "dve")
                tv = p.inc(nc.vector.tensor_copy(
                    B.colbuf[:, w * 16 + h: w * 16 + h + 1], pB2[:, 0:1]),
                    "dve")
                tok_wm_ev = tv
                p.wait("tensor", "dve", tv)
                for it in range(8):
                    tp = p.inc(nc.tensor.transpose(
                        pT[:, it * 128:(it + 1) * 128],
                        B.scr_bf[:, it * 128:(it + 1) * 128], B.ident), "pe")
                p.wait("vector", "pe", tp)
                p.dwait("vector", gWS)
                tv = p.inc(nc.vector.tensor_copy(wmixT[:], pT[:]), "dve")
                p.wait("sync", "dve", tv)
                p.dma("sync",
                      wmix_d[w * 16 + h].rearrange("(a p) c -> p a c", p=128),
                      wmixT[:].rearrange("p (a c) -> p a c", c=128), gWS)
                tok_st = p.cnt["dve"]
        p.dwait("sync", gWS)
    p.barrier_into(("vector", "scalar", "tensor", "gpsimd", "sync"))

    with (
        nc.sbuf_tensor(uname("b0_wmix"), [128, 3 * 1024], BF16) as wmix,
        nc.sbuf_tensor(uname("b0_qtm"), [128, 2048], BF16) as QTm,
        nc.sbuf_tensor(uname("b0_qtd"), [128, 2048], BF16) as QTd,
        nc.sbuf_tensor(uname("b0_ktm"), [128, 2048], BF16) as KTm,
        nc.sbuf_tensor(uname("b0_vtf"), [128, 2048], F32) as VTmf,
        nc.sbuf_tensor(uname("b0_vtok"), [128, 2 * NT * 66], BF16) as vtok,
        nc.sbuf_tensor(uname("b0_attnT"), [128, 2048], BF16) as attnT,
        nc.sbuf_tensor(uname("b0_n0"), [64, 2048], BF16) as n0buf,
        nc.sbuf_tensor(uname("b0_ccol"), [64, 32], F32) as ccol,
        nc.sbuf_tensor(uname("b0_wo"), [128, 2048], BF16) as wobuf,
        nc.sbuf_tensor(uname("b0_worow"), [1, 1024], BF16) as worow,
        nc.psum_tensor([128, 2048], F32) as pS,
        nc.psum_tensor([128, 2048], F32) as pO,
    ):
        gB0 = p.dgrp()
        p.dma("sync", ccol[:], d["w_l0_col"], gB0)
        p.dma("sync", worow[:], d["b_l0_bo_row"], gB0)
        p.dwait("vector", gB0)
        p.dwait("tensor", gB0)
        for h in range(16):
            gH = p.dgrp()
            p.wait("sync", "pe", getattr(B, "_l0_projpe", 0))
            for w in range(3):
                p.dma("sync",
                      wmix[:, w * 1024:(w + 1) * 1024].rearrange(
                          "p (a c) -> p a c", c=128),
                      wmix_d[w * 16 + h].rearrange("(a p) c -> p a c", p=128),
                      gH)
            p.dwait("tensor", gH)
            p.wait("tensor", "dve", B._tok_ctx_evac)
            tv = 0
            for w in range(3):
                for cc in range(4):
                    for dt in range(ND):
                        tm = p.inc(nc.tensor.matmul(
                            pS[:, cc * 512:(cc + 1) * 512],
                            wmix[:, w * 1024 + dt * 128: w * 1024 + (dt + 1) * 128],
                            xnt(B, dt, rs(cc * 512, (cc + 1) * 512)),
                            start=(dt == 0), stop=(dt == ND - 1)), "pe")
                    p.wait("vector", "pe", tm)
                    if w < 2:
                        dstT = QTm if w == 0 else KTm
                        tv = p.inc(nc.vector.tensor_scalar(
                            dstT[:, cc * 512:(cc + 1) * 512],
                            pS[:, cc * 512:(cc + 1) * 512],
                            B.colbuf[:, w * 16 + h: w * 16 + h + 1],
                            None, ALU.add), "dve")
                    else:
                        tv = p.inc(nc.vector.tensor_scalar(
                            VTmf[:, cc * 512:(cc + 1) * 512],
                            pS[:, cc * 512:(cc + 1) * 512],
                            B.colbuf[:, w * 16 + h: w * 16 + h + 1],
                            None, ALU.add), "dve")
                p.wait("tensor", "dve", tv)
            B._l0_projpe = p.cnt["pe"]
            tok_qkv_pe = p.cnt["pe"]
            tok_qkv = tv
            tvt = emit_vtok(p, B, VTmf, vtok, pO, 2, 64, 66, tok_qkv)
            tnb = emit_negB_full(p, B, QTm, KTm, pS, scale, 100, tok_qkv_pe,
                                 tok_qkv)
            p.wait("tensor", "dve", max(tnb, tvt))
            p.wait("scalar", "dve", max(tnb, tvt))
            for p1 in range(2):
                gQ = p.dgrp()
                p.wait("sync", "dve", tok_qkv)
                p.wait("sync", "pe", p.cnt["pe"])
                p.dma("sync", QTd[0:64, :], QTm[p1 * 64: p1 * 64 + 64, :], gQ)
                p.dma("sync", QTd[64:128, :], QTm[p1 * 64: p1 * 64 + 64, :], gQ)
                p.dwait("tensor", gQ)
                attn_chunks_l0(p, B, QTd, KTm, vtok, attnT, pS, pO, 100, scale,
                               h, p1, n0buf, ccol)
        tok_x = 0
        for quarter in range(4):
            tok_wp = stream_cast(p, B, d["w_l0_woT"], quarter * 256, 2, 0,
                                 1024, wobuf, 0, VTmf)
            p.wait("tensor", "gp", tok_wp)
            p.wait("tensor", "dve", B._tok_ctx_evac)
            for i in range(NT):
                for oc in range(2):
                    if quarter == 0:
                        tm = p.inc(nc.tensor.matmul(
                            pS[:, oc * 512: oc * 512 + 512], B.ones[0:1, :],
                            worow[0:1, oc * 512:(oc + 1) * 512],
                            start=True, stop=False), "pe")
                    for hpq in range(2):
                        hp = quarter * 2 + hpq
                        tm = p.inc(nc.tensor.matmul(
                            pS[:, oc * 512: oc * 512 + 512],
                            B.arena[:, hp * 2048 + i * 128: hp * 2048 + i * 128 + 128],
                            wobuf[:, hpq * 1024 + oc * 512: hpq * 1024 + (oc + 1) * 512],
                            start=(hpq == 0 and quarter != 0),
                            stop=(hpq == 1)), "pe")
                    p.wait("vector", "pe", tm)
                    tok_x = p.inc(nc.vector.tensor_tensor(
                        B.X[:, i * D + oc * 512: i * D + (oc + 1) * 512],
                        B.X[:, i * D + oc * 512: i * D + (oc + 1) * 512],
                        pS[:, oc * 512: oc * 512 + 512], ALU.add), "dve")
                    p.wait("tensor", "dve", tok_x)
    B._tok_ctx_evac = 0
    B._l0_projpe = 0
    p.barrier_into(("vector", "scalar", "tensor", "gpsimd", "sync"))


def emit_negB_full(p, B, QT, KT, ps, scale, col_q, tok_ready_pe, tok_ready_dve):
    nc = p.nc
    tdv = 0
    tm_prev = tok_ready_pe
    for w, MT in ((0, QT), (1, KT)):
        p.wait("scalar", "pe", tm_prev)
        p.wait("scalar", "dve", max(tok_ready_dve, tdv))
        ta = p.inc(nc.scalar.activation(B.scr_bf[:, 0:2048], MT[:],
                                        AF.Square), "act")
        p.wait("tensor", "act", ta)
        p.wait("tensor", "dve", tdv)
        for cc in range(4):
            tm = p.inc(nc.tensor.matmul(
                ps[:, cc * 512:(cc + 1) * 512], B.ones[:, :],
                B.scr_bf[:, cc * 512:(cc + 1) * 512],
                start=True, stop=True), "pe")
        tm_prev = tm
        p.wait("vector", "pe", tm)
        for cc in range(4):
            tdv = p.inc(nc.vector.tensor_reduce(
                B.colbuf[:, 122 + cc: 123 + cc],
                ps[:, cc * 512:(cc + 1) * 512], AX.X, ALU.max), "dve")
        p.wait("vector", "dve", tdv)
        tdv = p.inc(nc.vector.tensor_reduce(
            B.colbuf[:, 120 + w: 121 + w], B.colbuf[:, 122:126],
            AX.X, ALU.max), "dve")
        p.wait("vector", "dve", tdv)
    c0 = B.colbuf[:, 120:121]
    c1 = B.colbuf[:, 121:122]
    tdv = p.inc(nc.vector.tensor_tensor(c0, c0, c1, ALU.mult), "dve")
    p.wait("scalar", "dve", tdv)
    tac = p.inc(nc.scalar.activation(c0, c0, AF.Ln), "act")
    p.wait("scalar", "act", tac)
    tac = p.inc(nc.scalar.activation(c0, c0, AF.Exp, scale=0.5), "act")
    p.wait("vector", "act", tac)
    tdv = p.inc(nc.vector.tensor_scalar(
        B.colbuf[:, col_q: col_q + 1], c0, -scale, None, ALU.mult), "dve")
    return tdv


def attn_chunks_l0(p, B, QTd, KTm, vtok, attnT, pS, pO, col_b, scale,
                   h, p1, n0buf, ccol):
    nc = p.nc
    tok_exp = {}
    for c in range(NC2):
        q0 = c * 512
        jmax = 4 * c + 3
        t_prev_ep = B._tok_ctx_evac
        for j in range(jmax + 1):
            par = j % 2
            qlo = max(0, 128 * j - q0)
            p.wait("tensor", "act", tok_exp.get((par, j - 2), 0))
            if j == 0:
                p.wait("tensor", "dve", t_prev_ep)
            for p2 in range(2):
                tm = p.inc(nc.tensor.matmul(
                    pS[:, (par * 2 + p2) * 512 + qlo: (par * 2 + p2) * 512 + 512],
                    KTm[ts(p2, 64), ts(j, 128)],
                    QTd[ts(p2, 64), rs(q0 + qlo, q0 + 512)],
                    start=True, stop=True,
                    tile_position=(p2 * 64, 0)), "pe")
            tok_s = tm
            p.wait("scalar", "pe", tok_s)
            pin = pS[:, par * 1024: par * 1024 + 1024].rearrange(
                "p (h q) -> p h q", q=512)[:, :, qlo:512]
            pout = attnT[:, par * 1024: par * 1024 + 1024].rearrange(
                "p (h q) -> p h q", q=512)[:, :, qlo:512]
            te = p.inc(nc.scalar.activation(
                pout, pin, AF.Exp, scale=scale,
                bias=B.colbuf[:, col_b: col_b + 1]), "act")
            tok_exp[(par, j)] = te
            tmask = 0
            if 128 * j >= q0:
                p.wait("vector", "act", te)
                for p2 in range(2):
                    sl = rs((par * 2 + p2) * 512 + qlo,
                            (par * 2 + p2) * 512 + qlo + 128)
                    tmask = p.inc(nc.vector.tensor_tensor(
                        attnT[:, sl], attnT[:, sl], B.triu, ALU.mult), "dve")
            p.wait("tensor", "act", te)
            if tmask:
                p.wait("tensor", "dve", tmask)
            for p2 in range(2):
                for strip in range(2):
                    ta = p.inc(nc.tensor.matmul(
                        pO[0:65, strip * 512 + qlo: strip * 512 + 512],
                        vtok[ts(strip, 64),
                             p2 * NT * 66 + j * 66: p2 * NT * 66 + j * 66 + 65],
                        attnT[ts(strip, 64), (par * 2 + p2) * 512 + qlo:
                              (par * 2 + p2) * 512 + 512],
                        start=(j == 0 and p2 == 0),
                        stop=(j == jmax and p2 == 1),
                        tile_position=(strip * 64, 0)), "pe")
        p.wait("vector", "pe", ta)
        tv = p.inc(nc.vector.tensor_copy(
            B.scr_f1[0:65, 0:512], pO[0:65, 0:512]), "dve")
        p.wait("vector", "dve", tv)
        tv = p.inc(nc.vector.tensor_tensor(
            B.scr_f1[0:65, 0:512], B.scr_f1[0:65, 0:512],
            pO[0:65, 512:1024], ALU.add), "dve")
        p.wait("vector", "dve", tv)
        tv = p.inc(nc.vector.reciprocal(B.scr_f2[0:1, 0:512],
                                        B.scr_f1[64:65, 0:512]), "dve")
        p.wait("tensor", "dve", tv)
        p.wait("tensor", "act", tok_exp[(jmax % 2, jmax)])
        tb = p.inc(nc.tensor.matmul(
            pS[0:64, 0:512], B.onesf[0:1, 0:64], B.scr_f2[0:1, 0:512],
            start=True, stop=True), "pe")
        p.wait("vector", "pe", tb)
        tv = p.inc(nc.vector.tensor_tensor(
            B.scr_f1[0:64, 0:512], B.scr_f1[0:64, 0:512], pS[0:64, 0:512],
            ALU.mult), "dve")
        p.wait("vector", "dve", tv)
        if p1 == 0:
            tv = p.inc(nc.vector.tensor_copy(
                n0buf[0:64, c * 512:(c + 1) * 512], B.scr_f1[0:64, 0:512]),
                "dve")
        else:
            tv = p.inc(nc.vector.tensor_scalar(
                B.scr_f2[0:64, 0:512], B.scr_f1[0:64, 0:512],
                ccol[0:64, h * 2 + 1: h * 2 + 2], None, ALU.mult), "dve")
            p.wait("vector", "dve", tv)
            dst = B.arena[(h % 2) * 64: (h % 2) * 64 + 64,
                          (h // 2) * 2048 + c * 512: (h // 2) * 2048 + (c + 1) * 512]
            tv = p.inc(nc.vector.scalar_tensor_tensor(
                dst, n0buf[0:64, c * 512:(c + 1) * 512],
                ccol[0:64, h * 2: h * 2 + 1], B.scr_f2[0:64, 0:512],
                ALU.mult, ALU.add), "dve")
        B._tok_ctx_evac = p.cnt["dve"]


def emit_l2_ffn(p, B):
    nc = p.nc
    d = B.d
    with (
        nc.sbuf_tensor(uname("f2_w1"), [128, 2048], BF16) as w1buf,
        nc.sbuf_tensor(uname("f2_w2"), [128, 2048], BF16) as w2buf,
        nc.sbuf_tensor(uname("f2_b1"), [128, 32], F32) as b1col,
        nc.sbuf_tensor(uname("f2_rows"), [1, 3 * 1024], BF16) as rows3,
        nc.sbuf_tensor(uname("f2_gb"), [128, 2048], BF16) as gb,
        nc.sbuf_tensor(uname("f2_stage"), [128, 2048], F32) as stage,
        nc.psum_tensor([128, 4096], F32) as pEO,
    ):
        gL2 = p.dgrp()
        p.dma("sync", b1col[:], d["w_l2_b1col"], gL2)
        p.dma("sync", rows3[0:1, 0:1024], d["b_l2_b2_row"], gL2)
        p.dma("sync", rows3[0:1, 1024:2048], d["b_l2_gamma_row"], gL2)
        p.dma("sync", rows3[0:1, 2048:3072], d["b_l2_beta_row"], gL2)
        p.dwait("tensor", gL2)
        for gi in range(2):
            for oc in range(2):
                tm = p.inc(nc.tensor.matmul(
                    pEO[:, oc * 512:(oc + 1) * 512], B.ones[0:1, :],
                    rows3[0:1, (1 + gi) * 1024 + oc * 512:
                          (1 + gi) * 1024 + (oc + 1) * 512],
                    start=True, stop=True), "pe")
            p.wait("vector", "pe", tm)
            for oc in range(2):
                tgb = p.inc(nc.vector.tensor_copy(
                    gb[:, gi * 1024 + oc * 512: gi * 1024 + (oc + 1) * 512],
                    pEO[:, oc * 512:(oc + 1) * 512]), "dve")
            p.wait("tensor", "dve", tgb)

        mean, rstd = B.stats[:, 0:16], B.stats[:, 16:32]
        c_sum = B.colbuf[:, 110:111]
        c_m2 = B.colbuf[:, 108:109]
        c_nm2 = B.colbuf[:, 111:112]
        c_rs2 = B.colbuf[:, 109:110]
        tok_ep = [0, 0, 0, 0]
        tok_w1mm = [0, 0]
        tok_w2mm = [0, 0]
        for tc in range(4):
            csl = rs(tc * 512, (tc + 1) * 512)
            tok_h0 = 0
            for f in range(32):
                tok_w1 = stream_cast(p, B, d["w_l2_w1"], 0, 8, f * 128, 128,
                                     w1buf, (f % 2) * 1024, stage,
                                     war_pe=tok_w1mm[f % 2])
                p.wait("tensor", "gp", tok_w1)
                p.wait("tensor", "dve", B.tok_xnT)
                p.wait("tensor", "act", tok_h0)
                p.wait("tensor", "dve", max(tok_ep))
                for dt in range(ND):
                    tm = p.inc(nc.tensor.matmul(
                        pEO[:, (f % 2) * 512:(f % 2) * 512 + 512],
                        w1buf[:, (f % 2) * 1024 + dt * 128:
                              (f % 2) * 1024 + (dt + 1) * 128],
                        xnt(B, dt, csl),
                        start=(dt == 0), stop=(dt == ND - 1)), "pe")
                tok_w1mm[f % 2] = tm
                p.wait("scalar", "pe", tm)
                p.dwait("scalar", gL2)
                tok_h0 = p.inc(nc.scalar.activation(
                    B.arena[:, f * 512:(f + 1) * 512],
                    pEO[:, (f % 2) * 512:(f % 2) * 512 + 512],
                    AF.Relu, bias=b1col[:, f:f + 1]), "act")
            p.wait("tensor", "act", tok_h0)
            for i in range(4):
                for oc in range(2):
                    tm = p.inc(nc.tensor.matmul(
                        pEO[:, i * 1024 + oc * 512: i * 1024 + (oc + 1) * 512],
                        B.ones[0:1, :],
                        rows3[0:1, oc * 512:(oc + 1) * 512],
                        start=True, stop=False), "pe")
            for f in range(32):
                tok_w2 = stream_cast(p, B, d["w_l2_w2"], f * 128, 1, 0, 1024,
                                     w2buf, (f % 2) * 1024, stage,
                                     war_pe=tok_w2mm[f % 2])
                p.wait("tensor", "gp", tok_w2)
                for i in range(4):
                    for oc in range(2):
                        tm = p.inc(nc.tensor.matmul(
                            pEO[:, i * 1024 + oc * 512:
                                i * 1024 + (oc + 1) * 512],
                            B.arena[:, f * 512 + i * 128: f * 512 + (i + 1) * 128],
                            w2buf[:, (f % 2) * 1024 + oc * 512:
                                  (f % 2) * 1024 + (oc + 1) * 512],
                            start=False, stop=(f == 31)), "pe")
                tok_w2mm[f % 2] = tm
            tok_eo = tm
            for i in range(4):
                it = tc * 4 + i
                p.wait("vector", "pe", tok_eo)
                tv = p.inc(nc.vector.tensor_scalar(
                    B.scr_f1[:, 0:1024], B.X[:, ts(it, D)],
                    mean[:, it:it + 1], rstd[:, it:it + 1],
                    ALU.subtract, ALU.mult), "dve")
                p.wait("vector", "dve", tv)
                for oc in range(2):
                    tv = p.inc(nc.vector.tensor_tensor(
                        B.scr_f2[:, oc * 512:(oc + 1) * 512],
                        B.scr_f1[:, oc * 512:(oc + 1) * 512],
                        pEO[:, i * 1024 + oc * 512: i * 1024 + (oc + 1) * 512],
                        ALU.add), "dve")
                p.wait("vector", "dve", tv)
                tok_ep[i] = tv
                p.wait("scalar", "dve", tv)
                ta = p.inc(nc.scalar.activation(
                    B.scr_bf[:, 0:1024], B.scr_f2[:, 0:1024], AF.Identity,
                    accum_out=c_sum), "act")
                p.wait("vector", "act", ta)
                tv = p.inc(nc.vector.tensor_scalar(c_m2, c_sum, 1.0 / D, None,
                                                   ALU.mult), "dve")
                tv = p.inc(nc.vector.tensor_scalar(c_nm2, c_m2, -1.0, None,
                                                   ALU.mult), "dve")
                p.wait("vector", "dve", tv)
                p.wait("scalar", "dve", tv)
                ta = p.inc(nc.scalar.activation(
                    B.scr_bf[:, 0:1024], B.scr_f2[:, 0:1024], AF.Square,
                    bias=c_nm2, accum_out=c_sum), "act")
                p.wait("vector", "act", ta)
                tv = p.inc(nc.vector.tensor_scalar(c_rs2, c_sum, 1.0 / D, None,
                                                   ALU.mult), "dve")
                p.wait("scalar", "dve", tv)
                ta = p.inc(nc.scalar.activation(c_rs2, c_rs2, AF.Ln,
                                                bias=B.epscol), "act")
                p.wait("scalar", "act", ta)
                ta = p.inc(nc.scalar.activation(c_rs2, c_rs2, AF.Exp,
                                                scale=-0.5), "act")
                p.wait("vector", "act", ta)
                tv = p.inc(nc.vector.tensor_scalar(
                    B.scr_f1[:, 0:1024], B.scr_f2[:, 0:1024], c_m2, c_rs2,
                    ALU.subtract, ALU.mult), "dve")
                p.wait("vector", "dve", tv)
                tv = p.inc(nc.vector.tensor_tensor(
                    B.scr_f2[:, 0:1024], B.scr_f1[:, 0:1024],
                    gb[:, 0:1024], ALU.mult), "dve")
                p.wait("vector", "dve", tv)
                tv = p.inc(nc.vector.tensor_tensor(
                    B.scr_f2[:, 0:1024], B.scr_f2[:, 0:1024],
                    gb[:, 1024:2048], ALU.add), "dve")
                p.wait("vector", "dve", tv)
                tv = p.inc(nc.vector.tensor_tensor(
                    B.X[:, ts(it, D)], B.X[:, ts(it, D)],
                    B.scr_f2[:, 0:1024], ALU.add), "dve")
                tok_ep[i] = tv
    p.barrier_into(("vector", "scalar", "tensor", "gpsimd", "sync"))


def emit_l3_ffn(p, B):
    nc = p.nc
    d = B.d
    NE = 8
    M3 = 256
    import contextlib
    with contextlib.ExitStack() as _st3:
        _e = _st3.enter_context
        Tb = _e(nc.sbuf_tensor(uname("f3_T"), [8, 2048], F32))
        G8 = _e(nc.sbuf_tensor(uname("f3_G8"), [8, 2048], F32))
        Ttok = _e(nc.sbuf_tensor(uname("f3_tok"), [128, 128], F32))
        gtok = _e(nc.sbuf_tensor(uname("f3_gtok"), [128, 128], F32))
        selb = _e(nc.sbuf_tensor(uname("f3_sel"), [128, 128], F32))
        hardb = _e(nc.sbuf_tensor(uname("f3_hard"), [128, 128], F32))
        Gb = _e(nc.sbuf_tensor(uname("f3_G"), [128, 128], F32))
        Gbf = _e(nc.sbuf_tensor(uname("f3_Gbf"), [128, 128], BF16))
        rankb = _e(nc.sbuf_tensor(uname("f3_rank"), [128, 128], F32))
        taub = _e(nc.sbuf_tensor(uname("f3_tau"), [128, 8], F32))
        PeT = _e(nc.sbuf_tensor(uname("f3_pet"), [128, 16 * 256], BF16))
        PeL = _e(nc.sbuf_tensor(uname("f3_pel"), [128, 256], BF16))
        ReT = _e(nc.sbuf_tensor(uname("f3_ret"), [128, 8 * 256], BF16))
        Ye = _e(nc.sbuf_tensor(uname("f3_y"), [128, 2048], BF16))
        webuf = _e(nc.sbuf_tensor(uname("f3_we"), [128, 4 * 1024], BF16))
        gwbuf = _e(nc.sbuf_tensor(uname("f3_gw"), [128, 64], BF16))
        stage = _e(nc.sbuf_tensor(uname("f3_stage"), [128, 1024], F32))
        pA = _e(nc.psum_tensor([128, 2560], F32))
        pB16 = _e(nc.psum_tensor([128, 1024], BF16))
        pY = _e(nc.psum_tensor([128, 1024], F32))
        mean, rstd = B.stats[:, 0:16], B.stats[:, 16:32]
        p.wait("vector", "act", B.tok_lnstats)
        for i in range(NT):
            txn = p.inc(nc.vector.tensor_scalar(
                B.arena[:, i * 1024:(i + 1) * 1024], B.X[:, ts(i, D)],
                mean[:, i:i + 1], rstd[:, i:i + 1],
                ALU.subtract, ALU.mult), "dve")
        B._l3_xn_tok = txn
        tok_gw = stream_cast(p, B, d["w_l3_gate"], 0, 8, 0, 8, gwbuf, 0, stage)
        p.wait("tensor", "gp", tok_gw)
        p.wait("tensor", "dve", B.tok_xnT)
        for cc in range(4):
            for dt in range(ND):
                tm = p.inc(nc.tensor.matmul(
                    pA[0:8, cc * 512:(cc + 1) * 512],
                    gwbuf[:, dt * 8:(dt + 1) * 8],
                    xnt(B, dt, rs(cc * 512, (cc + 1) * 512)),
                    start=(dt == 0), stop=(dt == ND - 1)), "pe")
        p.wait("vector", "pe", tm)
        for cc in range(4):
            tv = p.inc(nc.vector.tensor_copy(
                Tb[:, cc * 512:(cc + 1) * 512],
                pA[0:8, cc * 512:(cc + 1) * 512]), "dve")
        p.wait("vector", "dve", tv)
        tv = p.inc(nc.vector.tensor_scalar(Tb[:], Tb[:], 1e-6, None, ALU.max),
                   "dve")
        p.wait("scalar", "dve", tv)
        ta = p.inc(nc.scalar.activation(Tb[:], Tb[:], AF.Ln), "act")
        p.wait("vector", "act", ta)
        p.wait("tensor", "act", ta)

        c_rm = B.colbuf[0:8, 110:111]
        c_nrm = B.colbuf[0:8, 111:112]
        c_se = B.colbuf[0:8, 112:113]
        rm2 = B.colbuf[:, 113:129] if False else None
        for it in range(8):
            tv = p.inc(nc.vector.tensor_reduce(c_rm, Tb[:], AX.X, ALU.max),
                       "dve")
            p.wait("vector", "dve", tv)
            tv = p.inc(nc.vector.tensor_scalar(c_nrm, c_rm, -1.0, None,
                                               ALU.mult), "dve")
            p.wait("scalar", "dve", tv)
            ta = p.inc(nc.scalar.activation(B.scr_bf[0:8, 0:2048], Tb[:],
                                            AF.Exp, bias=c_nrm,
                                            accum_out=c_se), "act")
            p.wait("scalar", "act", ta)
            ta = p.inc(nc.scalar.activation(c_se, c_se, AF.Ln), "act")
            p.wait("vector", "act", ta)
            tv = p.inc(nc.vector.tensor_tensor(c_se, c_se, c_rm, ALU.add),
                       "dve")
            p.wait("vector", "dve", tv)
            tv = p.inc(nc.vector.tensor_scalar(Tb[:], Tb[:], c_se, None,
                                               ALU.subtract), "dve")
            p.wait("tensor", "dve", tv)
            for i in range(NT):
                tp = p.inc(nc.tensor.transpose(
                    pA[:, 2048 + i * 8: 2048 + (i + 1) * 8],
                    Tb[0:8, ts(i, 128)], B.identf[0:8, 0:8]), "pe")
            p.wait("vector", "pe", tp)
            tv = p.inc(nc.vector.tensor_copy(Ttok[:], pA[:, 2048:2176]), "dve")
            p.wait("vector", "dve", tv)
            t3 = Ttok[:].rearrange("p (g e) -> p g e", e=8)
            rm16 = B.scr_f1[:, 0:16]
            sm16 = B.scr_f1[:, 16:32]
            tv = p.inc(nc.vector.tensor_reduce(rm16, t3, AX.X, ALU.max), "dve")
            p.wait("vector", "dve", tv)
            bc = rm16.rearrange("p (g o) -> p g o", o=1).broadcast_to([128, 16, 8])
            tv = p.inc(nc.vector.tensor_tensor(
                B.scr_f2[:, 0:128].rearrange("p (g e) -> p g e", e=8), t3, bc,
                ALU.subtract), "dve")
            p.wait("scalar", "dve", tv)
            ta = p.inc(nc.scalar.activation(B.scr_f2[:, 128:256],
                                            B.scr_f2[:, 0:128], AF.Exp), "act")
            p.wait("vector", "act", ta)
            tv = p.inc(nc.vector.tensor_reduce(
                sm16, B.scr_f2[:, 128:256].rearrange("p (g e) -> p g e", e=8),
                AX.X, ALU.add), "dve")
            p.wait("scalar", "dve", tv)
            ta = p.inc(nc.scalar.activation(sm16, sm16, AF.Ln), "act")
            p.wait("vector", "act", ta)
            tv = p.inc(nc.vector.tensor_tensor(sm16, sm16, rm16, ALU.add),
                       "dve")
            p.wait("vector", "dve", tv)
            bc2 = sm16.rearrange("p (g o) -> p g o", o=1).broadcast_to([128, 16, 8])
            tv = p.inc(nc.vector.tensor_tensor(t3, t3, bc2, ALU.subtract),
                       "dve")
            p.wait("tensor", "dve", tv)
            p.wait("tensor", "dve", tv)
            for i in range(NT):
                tp = p.inc(nc.tensor.transpose(
                    pA[0:8, i * 128:(i + 1) * 128],
                    Ttok[:, i * 8:(i + 1) * 8], B.identf[:, 0:128]), "pe")
            p.wait("vector", "pe", tp)
            for cc in range(4):
                tv = p.inc(nc.vector.tensor_copy(
                    Tb[:, cc * 512:(cc + 1) * 512],
                    pA[0:8, cc * 512:(cc + 1) * 512]), "dve")
            p.wait("vector", "dve", tv)
            p.wait("scalar", "dve", tv)
            p.wait("tensor", "dve", tv)
        ta = p.inc(nc.scalar.activation(G8[:], Tb[:], AF.Exp), "act")
        p.wait("vector", "act", ta)
        p.wait("tensor", "act", ta)
        for i in range(NT):
            tp = p.inc(nc.tensor.transpose(
                pA[:, 2048 + i * 8: 2048 + (i + 1) * 8],
                G8[0:8, ts(i, 128)], B.identf[0:8, 0:8]), "pe")
        p.wait("vector", "pe", tp)
        tv = p.inc(nc.vector.tensor_copy(gtok[:], pA[:, 2048:2176]), "dve")
        p.wait("vector", "dve", tv)
        lo = B.colbuf[0:8, 114:115]
        hi = B.colbuf[0:8, 115:116]
        mid = B.colbuf[0:8, 116:117]
        cntc = B.colbuf[0:8, 117:118]
        prd = B.colbuf[0:8, 118:119]
        pm1 = B.colbuf[0:8, 119:120]
        t1c = B.colbuf[0:8, 120:121]
        tv = p.inc(nc.vector.memset(lo, 0.0), "dve")
        tv = p.inc(nc.vector.memset(hi, 1.125), "dve")
        p.wait("vector", "dve", tv)
        for bi in range(42):
            tv = p.inc(nc.vector.tensor_tensor(mid, lo, hi, ALU.add), "dve")
            p.wait("vector", "dve", tv)
            tv = p.inc(nc.vector.tensor_scalar(mid, mid, 0.5, None, ALU.mult),
                       "dve")
            p.wait("vector", "dve", tv)
            tv = p.inc(nc.vector.tensor_scalar(B.scr_bf[0:8, 0:2048], G8[:],
                                               mid, None, ALU.is_ge), "dve")
            p.wait("vector", "dve", tv)
            tv = p.inc(nc.vector.tensor_reduce(cntc, B.scr_bf[0:8, 0:2048],
                                               AX.X, ALU.add), "dve")
            p.wait("vector", "dve", tv)
            tv = p.inc(nc.vector.tensor_scalar(prd, cntc, float(M3), None,
                                               ALU.is_ge), "dve")
            p.wait("vector", "dve", tv)
            tv = p.inc(nc.vector.tensor_tensor(t1c, mid, lo, ALU.subtract),
                       "dve")
            p.wait("vector", "dve", tv)
            tv = p.inc(nc.vector.tensor_tensor(t1c, t1c, prd, ALU.mult), "dve")
            p.wait("vector", "dve", tv)
            tv = p.inc(nc.vector.tensor_tensor(lo, lo, t1c, ALU.add), "dve")
            p.wait("vector", "dve", tv)
            tv = p.inc(nc.vector.tensor_scalar(pm1, prd, -1.0, 1.0, ALU.mult,
                                               ALU.add), "dve")
            p.wait("vector", "dve", tv)
            tv = p.inc(nc.vector.tensor_tensor(t1c, mid, hi, ALU.subtract),
                       "dve")
            p.wait("vector", "dve", tv)
            tv = p.inc(nc.vector.tensor_tensor(t1c, t1c, pm1, ALU.mult), "dve")
            p.wait("vector", "dve", tv)
            tv = p.inc(nc.vector.tensor_tensor(hi, hi, t1c, ALU.add), "dve")
            p.wait("vector", "dve", tv)
        p.wait("tensor", "dve", tv)
        tp = p.inc(nc.tensor.transpose(pA[:, 2048:2056][0:1, :],
                                       lo, B.identf[0:8, 0:8]), "pe")
        p.wait("vector", "pe", tp)
        tv = p.inc(nc.vector.tensor_copy(B.scr_f2[0:1, 0:8],
                                         pA[0:1, 2048:2056]), "dve")
        p.wait("tensor", "dve", tv)
        tp = p.inc(nc.tensor.matmul(pA[:, 2048:2056], B.onesf[0:1, :],
                                    B.scr_f2[0:1, 0:8], start=True, stop=True),
                   "pe")
        p.wait("vector", "pe", tp)
        tv = p.inc(nc.vector.tensor_copy(taub[:], pA[:, 2048:2056]), "dve")
        p.wait("vector", "dve", tv)
        for i in range(NT):
            tv = p.inc(nc.vector.tensor_tensor(
                selb[:, i * 8:(i + 1) * 8], gtok[:, i * 8:(i + 1) * 8],
                taub[:], ALU.is_ge), "dve")
        tv = p.inc(nc.vector.tensor_scalar(hardb[:], gtok[:], 0.5, None,
                                           ALU.is_gt), "dve")
        p.wait("vector", "dve", tv)
        acc = B.scr_f1[:, 0:1]
        tcol = B.scr_f1[:, 1:2]
        for i in range(NT):
            tv = p.inc(nc.vector.memset(acc, 1.0), "dve")
            p.wait("vector", "dve", tv)
            for e in range(7, -1, -1):
                selc = selb[:, i * 8 + e: i * 8 + e + 1]
                tv = p.inc(nc.vector.tensor_tensor(
                    Gb[:, i * 8 + e: i * 8 + e + 1], selc, acc, ALU.mult),
                    "dve")
                p.wait("vector", "dve", tv)
                if e:
                    tv = p.inc(nc.vector.tensor_tensor(tcol, selc, acc,
                                                       ALU.mult), "dve")
                    p.wait("vector", "dve", tv)
                    tv = p.inc(nc.vector.tensor_tensor(acc, acc, tcol,
                                                       ALU.subtract), "dve")
                    p.wait("vector", "dve", tv)
        tv = p.inc(nc.vector.tensor_tensor(Gb[:], Gb[:], hardb[:], ALU.mult),
                   "dve")
        p.wait("vector", "dve", tv)
        tv = p.inc(nc.vector.tensor_copy(Gbf[:], Gb[:]), "dve")
        p.wait("tensor", "dve", tv)
        for i in range(NT):
            for j in range(i + 1):
                lhs = B.triu if j == i else B.ones
                tm = p.inc(nc.tensor.matmul(
                    pA[:, 2176:2184], lhs, Gbf[:, j * 8:(j + 1) * 8],
                    start=(j == 0), stop=(j == i)), "pe")
            p.wait("vector", "pe", tm)
            tv = p.inc(nc.vector.tensor_tensor(
                rankb[:, i * 8:(i + 1) * 8], pA[:, 2176:2184],
                Gb[:, i * 8:(i + 1) * 8], ALU.mult), "dve")
            p.wait("vector", "dve", tv)
            tv = p.inc(nc.vector.tensor_scalar(
                rankb[:, i * 8:(i + 1) * 8], rankb[:, i * 8:(i + 1) * 8],
                -1.0, None, ALU.add), "dve")
            p.wait("tensor", "dve", tv)
        p.wait("vector", "dve", tv)
        tok_sc = 0
        for e in range(NE):
            for i in range(NT):
                tv = p.inc(nc.vector.tensor_scalar(
                    PeT[:, i * 256:(i + 1) * 256], B.iota256[:],
                    rankb[:, i * 8 + e: i * 8 + e + 1], None, ALU.is_equal),
                    "dve")
            tok_pet = tv
            p.wait("tensor", "dve", tok_pet)
            for dt in range(ND):
                p.wait("tensor", "dve", B._l3_xn_tok)
                for i in range(NT):
                    tm = p.inc(nc.tensor.matmul(
                        pY[:, 0:256],
                        B.arena[:, i * 1024 + dt * 128: i * 1024 + (dt + 1) * 128],
                        PeT[:, i * 256:(i + 1) * 256],
                        start=(i == 0), stop=(i == NT - 1)), "pe")
                p.wait("vector", "pe", tm)
                tv = p.inc(nc.vector.tensor_copy(
                    ReT[:, dt * 256:(dt + 1) * 256], pY[:, 0:256]), "dve")
                p.wait("tensor", "dve", tv)
            for half in range(2):
                tok_we = stream_cast(p, B, d["w_l3_experts"][e], half * 512, 4,
                                     0, 1024, webuf, 0, stage)
                p.wait("tensor", "gp", tok_we)
                for rt in range(2):
                    prt = pY if rt == 0 else pA
                    for dth in range(4):
                        dt = half * 4 + dth
                        for oc in range(2):
                            tm = p.inc(nc.tensor.matmul(
                                prt[:, oc * 512:(oc + 1) * 512],
                                ReT[:, dt * 256 + rt * 128:
                                    dt * 256 + rt * 128 + 128],
                                webuf[:, dth * 1024 + oc * 512:
                                      dth * 1024 + (oc + 1) * 512],
                                start=(dt == 0), stop=(dt == ND - 1)), "pe")
            p.wait("vector", "pe", tm)
            for rt in range(2):
                prt = pY if rt == 0 else pA
                for oc in range(2):
                    tv = p.inc(nc.vector.tensor_copy(
                        Ye[:, rt * 1024 + oc * 512: rt * 1024 + (oc + 1) * 512],
                        prt[:, oc * 512:(oc + 1) * 512]), "dve")
            p.wait("tensor", "dve", tv)
            for i in range(NT):
                p.wait("tensor", "dve", tok_sc)
                for rt in range(2):
                    tp = p.inc(nc.tensor.transpose(
                        pB16[:, rt * 128:(rt + 1) * 128],
                        PeT[:, i * 256 + rt * 128: i * 256 + (rt + 1) * 128],
                        B.ident), "pe")
                p.wait("vector", "pe", tp)
                tv = p.inc(nc.vector.tensor_copy(PeL[:, 0:256].rearrange(
                    "p (a c) -> p a c", c=128), pB16[:, 0:256].rearrange(
                    "p (a c) -> p a c", c=128)), "dve")
                p.wait("tensor", "dve", tv)
                for oc in range(2):
                    for rt in range(2):
                        tm = p.inc(nc.tensor.matmul(
                            pY[:, oc * 512:(oc + 1) * 512],
                            PeL[:, rt * 128:(rt + 1) * 128],
                            Ye[:, rt * 1024 + oc * 512: rt * 1024 + (oc + 1) * 512],
                            start=(rt == 0), stop=(rt == 1)), "pe")
                p.wait("vector", "pe", tm)
                for oc in range(2):
                    xs = rs(i * D + oc * 512, i * D + (oc + 1) * 512)
                    tok_sc = p.inc(nc.vector.tensor_tensor(
                        B.X[:, xs], B.X[:, xs],
                        pY[:, oc * 512:(oc + 1) * 512], ALU.add), "dve")
                p.wait("tensor", "dve", tok_sc)
    p.barrier_into(("vector", "scalar", "tensor", "gpsimd", "sync"))


def emit_l3_attn(p, B):
    nc = p.nc
    d = B.d
    scale = float(1.0 / np.sqrt(128.0))
    with (
        nc.sbuf_tensor(uname("a3_qt"), [128, 2048], BF16) as QT,
        nc.sbuf_tensor(uname("a3_kt"), [128, 2048], BF16) as KT1,
        nc.sbuf_tensor(uname("a3_vtf"), [128, 2048], F32) as VTf,
        nc.sbuf_tensor(uname("a3_vtok"), [128, NT * 128], BF16) as vtok,
        nc.sbuf_tensor(uname("a3_attnT"), [128, 1024], BF16) as attnT,
        nc.sbuf_tensor(uname("a3_pet"), [128, 2048], BF16) as peT,
        nc.sbuf_tensor(uname("a3_pe1"), [128, 1], F32) as pe1c,
        nc.sbuf_tensor(uname("a3_wbuf"), [128, 2048], BF16) as wbuf,
        nc.sbuf_tensor(uname("a3_stage"), [128, 2048], F32) as stage,
        nc.psum_tensor([128, 2048], F32) as pS,
        nc.psum_tensor([128, 512], F32) as pO,
        nc.psum_tensor([128, 512], F32) as pSum,
    ):
        gB3 = p.dgrp()
        p.dma("sync", peT[:], d["c_l3_peT"], gB3)
        p.dma("sync", pe1c[:], d["c_l3_pe1"], gB3)
        p.dwait("vector", gB3)
        p.dwait("scalar", gB3)
        tok_w = stream_cast(p, B, d["w_l3_kT"], 0, 8, 0, 128, wbuf, 0, stage)
        tok_w = stream_cast(p, B, d["w_l3_vT"], 0, 8, 0, 128, wbuf, 1024, stage)
        p.wait("tensor", "gp", tok_w)
        p.wait("tensor", "dve", B.tok_xnT)
        tv = 0
        for w in range(2):
            for cc in range(4):
                for dt in range(ND):
                    tm = p.inc(nc.tensor.matmul(
                        pS[:, cc * 512:(cc + 1) * 512],
                        wbuf[:, w * 1024 + dt * 128: w * 1024 + (dt + 1) * 128],
                        xnt(B, dt, rs(cc * 512, (cc + 1) * 512)),
                        start=(dt == 0), stop=(dt == ND - 1)), "pe")
                if w == 0:
                    p.wait("scalar", "pe", tm)
                    tv = p.inc(nc.scalar.activation(
                        KT1[:, cc * 512:(cc + 1) * 512],
                        pS[:, cc * 512:(cc + 1) * 512], AF.Identity,
                        bias=pe1c[:]), "act")
                else:
                    p.wait("vector", "pe", tm)
                    tv = p.inc(nc.vector.tensor_copy(
                        VTf[:, cc * 512:(cc + 1) * 512],
                        pS[:, cc * 512:(cc + 1) * 512]), "dve")
            if w == 0:
                p.wait("tensor", "act", tv)
            else:
                p.wait("tensor", "dve", tv)
        tok_kv = tv
        tvt = emit_vtok(p, B, VTf, vtok, pS, 1, 128, 128, tok_kv)
        p.wait("tensor", "dve", tvt)
        for h in range(8):
            tok_wq = stream_cast(p, B, d["w_l3_qT"], 0, 8, h * 128, 128, wbuf,
                                 0, stage)
            p.wait("tensor", "gp", tok_wq)
            p.wait("tensor", "dve", B._tok_ctx_evac)
            for cc in range(4):
                for dt in range(ND):
                    tm = p.inc(nc.tensor.matmul(
                        pS[:, cc * 512:(cc + 1) * 512],
                        wbuf[:, dt * 128:(dt + 1) * 128],
                        xnt(B, dt, rs(cc * 512, (cc + 1) * 512)),
                        start=(dt == 0), stop=(dt == ND - 1)), "pe")
                p.wait("vector", "pe", tm)
                tv = p.inc(nc.vector.tensor_tensor(
                    QT[:, cc * 512:(cc + 1) * 512],
                    pS[:, cc * 512:(cc + 1) * 512],
                    peT[:, cc * 512:(cc + 1) * 512], ALU.add), "dve")
            tok_q = tv
            p.wait("tensor", "dve", tok_q)
            tok_qkv_pe = p.cnt["pe"]
            tnb = emit_negB_full(p, B, QT, KT1, pS, scale, 100, tok_qkv_pe,
                                 tok_q)
            p.wait("scalar", "dve", tnb)
            p.wait("tensor", "dve", tnb)
            tok_exp = {}
            for c in range(NC2):
                q0 = c * 512
                jmax = 4 * c + 3
                t_prev = B._tok_ctx_evac
                for j in range(jmax + 1):
                    par = j % 2
                    qlo = max(0, 128 * j - q0)
                    p.wait("tensor", "act", tok_exp.get((par, j - 2), 0))
                    if j == 0:
                        p.wait("tensor", "dve", t_prev)
                    tm = p.inc(nc.tensor.matmul(
                        pS[:, par * 512 + qlo: par * 512 + 512],
                        KT1[:, ts(j, 128)],
                        QT[:, rs(q0 + qlo, q0 + 512)],
                        start=True, stop=True), "pe")
                    tok_s = tm
                    p.wait("scalar", "pe", tok_s)
                    te = p.inc(nc.scalar.activation(
                        attnT[:, par * 512 + qlo: par * 512 + 512],
                        pS[:, par * 512 + qlo: par * 512 + 512],
                        AF.Exp, scale=scale,
                        bias=B.colbuf[:, 100:101]), "act")
                    tok_exp[(par, j)] = te
                    tmask = 0
                    if 128 * j >= q0:
                        p.wait("vector", "act", te)
                        sl = rs(par * 512 + qlo, par * 512 + qlo + 128)
                        tmask = p.inc(nc.vector.tensor_tensor(
                            attnT[:, sl], attnT[:, sl], B.triu, ALU.mult),
                            "dve")
                    p.wait("tensor", "act", te)
                    if tmask:
                        p.wait("tensor", "dve", tmask)
                    ta = p.inc(nc.tensor.matmul(
                        pO[0:128, qlo:512],
                        vtok[:, j * 128:(j + 1) * 128],
                        attnT[:, par * 512 + qlo: par * 512 + 512],
                        start=(j == 0), stop=(j == jmax)), "pe")
                    ta = p.inc(nc.tensor.matmul(
                        pSum[0:1, qlo:512],
                        B.ones[:, 0:1],
                        attnT[:, par * 512 + qlo: par * 512 + 512],
                        start=(j == 0), stop=(j == jmax)), "pe")
                p.wait("vector", "pe", ta)
                tv = p.inc(nc.vector.reciprocal(B.scr_f2[0:1, 0:512],
                                                pSum[0:1, 0:512]), "dve")
                p.wait("vector", "dve", tv)
                tv = p.inc(nc.vector.tensor_copy(B.scr_f1[:, 0:512],
                                                 pO[:, 0:512]), "dve")
                p.wait("tensor", "dve", tv)
                p.wait("tensor", "act", tok_exp[(jmax % 2, jmax)])
                tb = p.inc(nc.tensor.matmul(
                    pS[:, 0:512], B.onesf[0:1, 0:128], B.scr_f2[0:1, 0:512],
                    start=True, stop=True), "pe")
                p.wait("vector", "pe", tb)
                tv = p.inc(nc.vector.tensor_tensor(
                    B.scr_f1[:, 0:512], B.scr_f1[:, 0:512], pS[:, 0:512],
                    ALU.mult), "dve")
                p.wait("vector", "dve", tv)
                tv = p.inc(nc.vector.tensor_copy(
                    B.arena[:, h * 2048 + c * 512: h * 2048 + (c + 1) * 512],
                    B.scr_f1[:, 0:512]), "dve")
                B._tok_ctx_evac = p.cnt["dve"]
        tok_x = 0
        for quarter in range(4):
            tok_wp = stream_cast(p, B, d["w_l3_oT"], quarter * 256, 2, 0,
                                 1024, wbuf, 0, stage)
            p.wait("tensor", "gp", tok_wp)
            p.wait("tensor", "dve", B._tok_ctx_evac)
            for i in range(NT):
                for oc in range(2):
                    for hpq in range(2):
                        hp = quarter * 2 + hpq
                        tm = p.inc(nc.tensor.matmul(
                            pS[:, oc * 512: oc * 512 + 512],
                            B.arena[:, hp * 2048 + i * 128: hp * 2048 + i * 128 + 128],
                            wbuf[:, hpq * 1024 + oc * 512: hpq * 1024 + (oc + 1) * 512],
                            start=(hpq == 0), stop=(hpq == 1)), "pe")
                    p.wait("vector", "pe", tm)
                    tok_x = p.inc(nc.vector.tensor_tensor(
                        B.X[:, i * D + oc * 512: i * D + (oc + 1) * 512],
                        B.X[:, i * D + oc * 512: i * D + (oc + 1) * 512],
                        pS[:, oc * 512: oc * 512 + 512], ALU.add), "dve")
                    p.wait("tensor", "dve", tok_x)
    B._tok_ctx_evac = 0
    p.barrier_into(("vector", "scalar", "tensor", "gpsimd", "sync"))



def _run(common, per_core, layer_groups, x0=None):
    x = x0
    for layers in layer_groups:
        nc = build_program(common, layers)
        in_maps = []
        for b in range(8):
            m = {k: v for k, v in common.items()}
            m["x_in"] = per_core[b]["x_in"] if x is None else x[b]
            in_maps.append(m)
        res = run_bass_kernel_spmd(nc, in_maps, core_ids=list(range(8)))
        x = [np.ascontiguousarray(res.results[b]["x_out"]) for b in range(8)]
    return np.stack(x, axis=0)


def kernel(**inputs):
    common, per_core = host_prep(inputs)
    out = _run(common, per_core, [[0, 1, 2, 3]])
    return out.astype(np.float32)


# version 17
# speedup vs baseline: 1.2704x; 1.2704x over previous
import sys
import numpy as np
import ml_dtypes

sys.path.insert(0, "/opt/trn_rl_repo")
import concourse.bass as bass
import concourse.mybir as mybir
from concourse.bass_utils import run_bass_kernel_spmd

F32 = mybir.dt.float32
BF16 = mybir.dt.bfloat16
I32 = mybir.dt.int32
AF = mybir.ActivationFunctionType
ALU = mybir.AluOpType
AX = mybir.AxisListType
bf16 = ml_dtypes.bfloat16

T = 2048
D = 1024
NT = T // 128
ND = D // 128
NC2 = T // 512
LN_EPS = 1e-5
DEBUG_SKIP = set()
_NAME_CTR = [0]


def uname(s):
    _NAME_CTR[0] += 1
    return "%s_%d" % (s, _NAME_CTR[0])
NDMA_SEMS = 16
STRICT_DMA = True
FFN_BAR = 0


class Prog:
    def __init__(self, nc, sems):
        self.nc = nc
        self.sems = sems
        self.cnt = {k: 0 for k in sems}
        self.hw = {}

    def eng(self, name):
        return getattr(self.nc, name)

    ENG_OF = {"dve": "vector", "act": "scalar", "gp": "gpsimd"}

    def inc(self, instr, sem, n=1):
        instr.then_inc(self.sems[sem], n)
        self.cnt[sem] += n
        eng = self.ENG_OF.get(sem)
        if eng is not None:
            self.wait(eng, sem, self.cnt[sem])
        return self.cnt[sem]

    def wait(self, eng, sem, tgt):
        if not tgt or tgt <= 0:
            return
        key = (eng, sem)
        if self.hw.get(key, -1) >= tgt:
            return
        self.eng(eng).wait_ge(self.sems[sem], tgt)
        self.hw[key] = tgt

    def barrier_into(self, engines):
        for c in engines:
            for s in self.cnt:
                self.wait(c, s, self.cnt[s])

    def dgrp(self):
        g = self.rr % NDMA_SEMS
        self.rr += 1
        return g

    def dma(self, q, out, in_, g):
        instr = self.eng(q).dma_start(out=out, in_=in_)
        self.inc(instr, "d%d" % g, 16)
        return g

    def dwait(self, eng, g):
        self.wait(eng, "d%d" % g, self.cnt["d%d" % g])


def ts(i, n=128):
    return slice(i * n, (i + 1) * n)


def rs(a, b):
    return slice(a, b)



def host_prep(inputs):
    c = {}
    f32 = np.float32

    def asf(x):
        return np.ascontiguousarray(np.asarray(x), dtype=f32)

    c["c_ident"] = np.eye(128, dtype=np.float32).astype(bf16)
    c["c_identf"] = np.eye(128, dtype=np.float32)
    c["c_triu"] = np.triu(np.ones((128, 128), dtype=np.float32)).astype(bf16)
    c["c_negtri"] = (-240.0 * (1.0 - np.triu(np.ones((128, 128),
                     dtype=np.float32)))).astype(bf16)
    c["c_ones"] = np.ones((128, 128), dtype=np.float32).astype(bf16)
    c["c_onesf"] = np.ones((8, 128), dtype=np.float32)
    c["c_iota256"] = np.broadcast_to(np.arange(256, dtype=f32), (128, 256)).copy()
    dup = np.zeros((64, 128), dtype=np.float32)
    for j in range(128):
        dup[j % 64, j] = 1.0
    c["c_dup"] = dup.astype(bf16)

    c["w_l0_wq"] = asf(inputs["L0_attn_Wq"])
    c["w_l0_wk"] = asf(inputs["L0_attn_Wk"])
    c["w_l0_wv"] = asf(inputs["L0_attn_Wv"])
    for nm, key in (("w_l0_bq", "L0_attn_bq"), ("w_l0_bk", "L0_attn_bk"),
                    ("w_l0_bv", "L0_attn_bv")):
        c[nm] = np.ascontiguousarray(asf(inputs[key]).reshape(8, 128).T)
    c["b_l0_bo_row"] = asf(inputs["L0_attn_bo"]).reshape(1, 1024).astype(bf16)
    for nm, key in (("w_l0_aq", "L0_attn_alpha_q"), ("w_l0_ak", "L0_attn_alpha_k"),
                    ("w_l0_av", "L0_attn_alpha_v")):
        a = asf(inputs[key])
        ar = np.repeat(a.transpose(1, 2, 0), 64, axis=2)
        c[nm] = np.ascontiguousarray(
            ar.reshape(16, 2, 8, 128).transpose(3, 0, 1, 2).reshape(128, 256))
    e64 = np.eye(64, dtype=np.float32)
    c["c_p64"] = np.tile(np.concatenate([e64, e64], axis=1), (16, 1)).astype(bf16)
    col = asf(inputs["L0_attn_collapse"])
    c["w_l0_col"] = np.ascontiguousarray(
        np.repeat(col[:, :, None], 64, axis=2).transpose(2, 0, 1).reshape(64, 32))
    c["w_l0_woT"] = np.ascontiguousarray(asf(inputs["L0_attn_Wo"]).T)
    c["w_l0_gT"] = np.ascontiguousarray(np.asarray(inputs["L0_ffn_w_gate"]).T)
    c["w_l0_uT"] = np.ascontiguousarray(np.asarray(inputs["L0_ffn_w_up"]).T)
    c["w_l0_dT"] = np.ascontiguousarray(np.asarray(inputs["L0_ffn_w_down"]).T)

    c["w_l1_attnT"] = np.ascontiguousarray(asf(inputs["L1_attn_W_attn"]).T)
    c["w_l1_projT"] = np.ascontiguousarray(asf(inputs["L1_attn_W_proj"]).T)
    c["w_l1_win"] = asf(inputs["L1_ffn_Win"])
    c["w_l1_bin"] = asf(inputs["L1_ffn_bin_"]).reshape(64, 1)
    R = np.zeros((64, 1024), dtype=np.float32)
    for m in range(64):
        R[m, m * 16:(m + 1) * 16] = 1.0
    c["c_l1_R"] = R.astype(bf16)
    c["c_l1_w2mask"] = np.ascontiguousarray(R.T).astype(bf16)
    c["w_l1_w1col"] = np.ascontiguousarray(
        asf(inputs["L1_ffn_W1"]).reshape(8, 128).T)
    c["w_l1_b1col"] = np.ascontiguousarray(
        asf(inputs["L1_ffn_b1"]).reshape(8, 128).T)
    c["w_l1_w2col"] = np.ascontiguousarray(
        asf(inputs["L1_ffn_W2"]).reshape(8, 128).T)
    c["w_l1_b2"] = asf(inputs["L1_ffn_b2"]).reshape(64, 1)
    c["w_l1_wout"] = asf(inputs["L1_ffn_Wout"])
    c["b_l1_bout_row"] = asf(inputs["L1_ffn_bout"]).reshape(1, 1024).astype(bf16)

    c["w_l2_qT"] = np.ascontiguousarray(
        asf(inputs["L2_attn_WQ"]).transpose(2, 0, 1).reshape(1024, 1024))
    c["w_l2_kT"] = np.ascontiguousarray(
        asf(inputs["L2_attn_WK"]).transpose(2, 0, 1).reshape(1024, 1024))
    c["w_l2_vT"] = np.ascontiguousarray(
        asf(inputs["L2_attn_WV"]).transpose(2, 0, 1).reshape(1024, 1024))
    c["w_l2_oT"] = np.ascontiguousarray(
        asf(inputs["L2_attn_WO"]).transpose(0, 2, 1).reshape(1024, 1024))
    c["w_l2_w1"] = asf(inputs["L2_ffn_w1"][0])
    c["w_l2_w2"] = asf(inputs["L2_ffn_w2"][0])
    c["w_l2_b1col"] = np.ascontiguousarray(
        asf(inputs["L2_ffn_b1"][0]).reshape(32, 128).T)
    c["b_l2_b2_row"] = asf(inputs["L2_ffn_b2"][0]).reshape(1, 1024).astype(bf16)
    c["b_l2_gamma_row"] = asf(inputs["L2_ffn_gamma"]).reshape(1, 1024).astype(bf16)
    c["b_l2_beta_row"] = asf(inputs["L2_ffn_beta"]).reshape(1, 1024).astype(bf16)

    c["w_l3_qT"] = np.ascontiguousarray(asf(inputs["L3_attn_Wq"]).T)
    c["w_l3_kT"] = np.ascontiguousarray(asf(inputs["L3_attn_Wk"]).T)
    c["w_l3_vT"] = np.ascontiguousarray(asf(inputs["L3_attn_Wv"]).T)
    c["w_l3_oT"] = np.ascontiguousarray(asf(inputs["L3_attn_Wo"]).T)
    c["w_l3_gate"] = asf(inputs["L3_ffn_gate_w"][0])
    c["w_l3_experts"] = asf(inputs["L3_ffn_experts"])

    def sinus(length, dim):
        pos = np.arange(length, dtype=np.float64)[:, None]
        inv = np.exp(-np.log(10000.0) *
                     (np.arange(0, dim, 2, dtype=np.float64) / dim))
        ang = pos * inv[None, :]
        pe = np.zeros((length, dim), dtype=np.float64)
        pe[:, 0::2] = np.sin(ang)
        pe[:, 1::2] = np.cos(ang)
        return pe.astype(np.float32)
    c["c_l3_peT"] = np.ascontiguousarray(sinus(T, 128).T).astype(bf16)
    c["c_l3_pe1"] = sinus(1, 128).reshape(128, 1)

    x = np.ascontiguousarray(np.asarray(inputs["L0_attn_hidden_states"]),
                             dtype=f32)
    per_core = [{"x_in": x[b]} for b in range(8)]
    return c, per_core



class Bufs:
    pass


NEEDED_PREFIX = {0: ("w_l0",), 1: ("w_l1",), 2: ("w_l2",), 3: ("w_l3",)}


def declare_io(nc, common, layers):
    d = {}
    need = set()
    for k in common:
        if k.startswith(("c_", "b_")):
            need.add(k)
        for l in layers:
            for pre in NEEDED_PREFIX[l]:
                if k.startswith(pre):
                    need.add(k)
    for k in sorted(need):
        a = common[k]
        dt = {np.dtype(np.float32): F32, np.dtype(np.int32): I32,
              np.dtype(bf16): BF16}[a.dtype]
        d[k] = nc.dram_tensor(k, list(a.shape), dt, kind="ExternalInput").ap()
    d["x_in"] = nc.dram_tensor("x_in", [T, D], F32, kind="ExternalInput").ap()
    d["x_out"] = nc.dram_tensor("x_out", [T, D], F32, kind="ExternalOutput").ap()
    if 0 in layers:
        d["l0_wmix"] = nc.dram_tensor("l0_wmix", [48, 1024, 128], BF16,
                                      kind="Internal").ap()
    return d


def build_program(common, layers):
    import contextlib
    nc = bass.Bass("TRN2", target_bir_lowering=False)
    d = declare_io(nc, common, layers)
    _stk = contextlib.ExitStack()
    _dsems = [_stk.enter_context(nc.semaphore("d%d" % i))
              for i in range(NDMA_SEMS)]

    with (
        nc.sbuf_tensor(uname("X"), [128, NT * D], F32) as X,
        nc.sbuf_tensor(uname("xnT"), [128, ND * T], BF16) as xnT,
        nc.sbuf_tensor(uname("arena"), [128, 24576], BF16) as arena,
        nc.sbuf_tensor(uname("stats"), [128, 64], F32) as stats,
        nc.sbuf_tensor(uname("consts"), [128, 640], BF16) as consts,
        nc.sbuf_tensor(uname("identf"), [128, 128], F32) as identf,
        nc.sbuf_tensor(uname("onesf"), [8, 128], F32) as onesf,
        nc.sbuf_tensor(uname("iota256"), [128, 256], F32) as iota256,
        nc.sbuf_tensor(uname("colbuf"), [128, 128], F32) as colbuf,
        nc.sbuf_tensor(uname("scr_bf"), [128, 2048], BF16) as scr_bf,
        nc.sbuf_tensor(uname("scr_f1"), [128, 1024], F32) as scr_f1,
        nc.sbuf_tensor(uname("scr_f2"), [128, 1024], F32) as scr_f2,
        nc.semaphore(uname("dW")) as _s0,
        nc.semaphore(uname("dX")) as _s1,
        nc.semaphore(uname("pe")) as _s2,
        nc.semaphore(uname("act")) as _s3,
        nc.semaphore(uname("dve")) as _s4,
        nc.semaphore(uname("gp")) as _s5,
        nc.Block() as block,
    ):
        sems = {"dW": _s0, "dX": _s1, "pe": _s2, "act": _s3,
                "dve": _s4, "gp": _s5}
        for gi, h in enumerate(_dsems):
            sems["d%d" % gi] = h
        p = Prog(nc, sems)
        p.rr = 0
        B = Bufs()
        B.X, B.xnT, B.arena, B.stats = X, xnT, arena, stats
        B.consts, B.iota256, B.colbuf, B.onesf = consts, iota256, colbuf, onesf
        B.identf = identf
        B.scr_bf, B.scr_f1, B.scr_f2 = scr_bf, scr_f1, scr_f2
        B.d = d
        B.ident = consts[:, 0:128]
        B.triu = consts[:, 128:256]
        B.ones = consts[:, 256:384]
        B.dup = consts[0:64, 384:512]
        B.negtri = consts[:, 512:640]
        B._tok_ctx_evac = 0
        B._stage_toks = {}
        B._stage_grps = {}
        B._stream_grps = {}

        B.gC = p.dgrp()
        for nm, dst in (("c_ident", B.ident), ("c_triu", B.triu),
                        ("c_ones", B.ones), ("c_dup", B.dup),
                        ("c_identf", identf[:]), ("c_onesf", onesf[:]),
                        ("c_negtri", B.negtri),
                        ("c_iota256", iota256[:])):
            p.dma("sync", dst, d[nm], B.gC)
        B.epscol = colbuf[:, 127:128]
        te = p.inc(nc.vector.memset(B.epscol, LN_EPS), "dve")
        B.tok_eps = te
        B.gX = p.dgrp()
        xr = d["x_in"].rearrange("(n p) d -> n p d", p=128)
        for i in range(NT):
            p.dma("sync", X[:, ts(i, D)], xr[i], B.gX)

        for li in layers:
            p.barrier_into(("vector", "scalar", "tensor", "gpsimd", "sync"))
            emit_layernorm(p, B)
            if li == 0:
                if "l0ffn" not in DEBUG_SKIP:
                    emit_l0_ffn(p, B)
                if "l0attn" not in DEBUG_SKIP:
                    emit_l0_attn(p, B)
            elif li == 1:
                emit_l1_ffn(p, B)
                emit_l1_attn(p, B)
            elif li == 2:
                emit_l2_ffn(p, B)
                emit_l2_attn(p, B)
            else:
                emit_l3_ffn(p, B)
                emit_l3_attn(p, B)

        p.barrier_into(("sync",))
        gOut = p.dgrp()
        xo = d["x_out"].rearrange("(n p) d -> n p d", p=128)
        for i in range(NT):
            p.dma("sync", xo[i], X[:, ts(i, D)], gOut)
        p.dwait("sync", gOut)
    _stk.close()
    return nc



def emit_layernorm(p, B):
    nc = p.nc
    X, stats = B.X, B.stats
    p.dwait("scalar", B.gX)
    p.dwait("scalar", B.gC)
    p.dwait("tensor", B.gC)
    p.dwait("vector", B.gC)
    mean, rstd = stats[:, 0:16], stats[:, 16:32]
    sums, negmean = stats[:, 32:48], stats[:, 48:64]
    for i in range(NT):
        t = p.inc(nc.scalar.activation(B.scr_bf[:, 0:1024], X[:, ts(i, D)],
                                       AF.Identity, accum_out=sums[:, i:i + 1]),
                  "act")
        p.wait("scalar", "act", t)
    p.wait("vector", "act", t)
    t = p.inc(nc.vector.tensor_scalar(mean, sums, 1.0 / D, None, ALU.mult), "dve")
    p.wait("vector", "dve", t)
    t = p.inc(nc.vector.tensor_scalar(negmean, mean, -1.0, None, ALU.mult), "dve")
    p.wait("scalar", "dve", t)
    for i in range(NT):
        t = p.inc(nc.scalar.activation(B.scr_bf[:, 0:1024], X[:, ts(i, D)],
                                       AF.Square, bias=negmean[:, i:i + 1],
                                       accum_out=sums[:, i:i + 1]), "act")
        p.wait("scalar", "act", t)
    p.wait("vector", "act", t)
    t = p.inc(nc.vector.tensor_scalar(rstd, sums, 1.0 / D, None, ALU.mult), "dve")
    p.wait("scalar", "dve", t)
    p.wait("scalar", "dve", B.tok_eps)
    t = p.inc(nc.scalar.activation(rstd, rstd, AF.Ln, bias=B.epscol), "act")
    p.wait("scalar", "act", t)
    t = p.inc(nc.scalar.activation(rstd, rstd, AF.Exp, scale=-0.5), "act")
    B.tok_lnstats = t
    p.wait("vector", "act", t)

    with nc.psum_tensor([128, 1024], BF16) as pt:
        tv2 = 0
        for i in range(NT):
            tv = p.inc(nc.vector.tensor_scalar(
                B.scr_bf[:, 0:1024], X[:, ts(i, D)], mean[:, i:i + 1],
                rstd[:, i:i + 1], ALU.subtract, ALU.mult), "dve")
            p.wait("tensor", "dve", tv)
            for j in range(ND):
                tp = p.inc(nc.tensor.transpose(pt[:, ts(j, 128)],
                                               B.scr_bf[:, ts(j, 128)],
                                               B.ident), "pe")
            p.wait("vector", "pe", tp)
            dst = B.xnT[:].rearrange("p (j t) -> p j t", t=T)[:, :, ts(i, 128)]
            src = pt[:].rearrange("p (j t) -> p j t", t=128)
            tv2 = p.inc(nc.vector.tensor_copy(dst, src), "dve")
            p.wait("tensor", "dve", tv2)
    B.tok_xnT = tv2


def xnt(B, dtile, cols):
    return B.xnT[:, dtile * T + cols.start: dtile * T + cols.stop]



def stream_cast(p, B, dram, rows0, nrt, cols0, ncols, dst, dst_off, staging,
                war_pe=None):
    key = (id(dst), dst_off)
    grp = B._stream_grps.get(key)
    if grp is None:
        grp = p.dgrp()
        B._stream_grps[key] = grp
    war = p.cnt["pe"] if war_pe is None else war_pe
    p.wait("gpsimd", "pe", war)
    p.wait("sync", "pe", war)
    for j in range(nrt):
        src = dram[rows0 + j * 128: rows0 + (j + 1) * 128, cols0:cols0 + ncols]
        p.dma("gpsimd", dst[:, dst_off + j * ncols: dst_off + (j + 1) * ncols],
              src, grp)
    return grp



def emit_negB_pair(p, B, QT, KT, ps, scale, col_q, tok_ready_pe, tok_ready_dve):
    nc = p.nc
    tdv = 0
    for h in range(2):
        hs = ts(h, 64)
        maxs = []
        tm_prev = tok_ready_pe
        for w, MT in ((0, QT), (1, KT)):
            p.wait("scalar", "pe", tm_prev)
            p.wait("scalar", "dve", max(tok_ready_dve, tdv))
            ta = p.inc(nc.scalar.activation(B.scr_bf[0:64, 0:2048], MT[hs, :],
                                            AF.Square), "act")
            p.wait("tensor", "act", ta)
            p.wait("tensor", "dve", tdv)
            for cc in range(4):
                tm = p.inc(nc.tensor.matmul(
                    ps[:, cc * 512:(cc + 1) * 512],
                    B.ones[0:64, :],
                    B.scr_bf[0:64, cc * 512:(cc + 1) * 512],
                    start=True, stop=True), "pe")
            tm_prev = tm
            p.wait("vector", "pe", tm)
            for cc in range(4):
                tdv = p.inc(nc.vector.tensor_reduce(
                    B.colbuf[:, 122 + cc: 123 + cc],
                    ps[:, cc * 512:(cc + 1) * 512], AX.X, ALU.max), "dve")
            p.wait("vector", "dve", tdv)
            cdst = B.colbuf[:, 120 + w: 121 + w]
            tdv = p.inc(nc.vector.tensor_reduce(cdst, B.colbuf[:, 122:126],
                                                AX.X, ALU.max), "dve")
            p.wait("vector", "dve", tdv)
            if getattr(B, "dbg_negb", None):
                B.dbg_negb(p, B, h, w, ps)
        c0 = B.colbuf[:, 120:121]
        c1 = B.colbuf[:, 121:122]
        tdv = p.inc(nc.vector.tensor_tensor(c0, c0, c1, ALU.mult), "dve")
        p.wait("scalar", "dve", tdv)
        tac = p.inc(nc.scalar.activation(c0, c0, AF.Ln), "act")
        p.wait("scalar", "act", tac)
        tac = p.inc(nc.scalar.activation(c0, c0, AF.Exp, scale=0.5), "act")
        p.wait("vector", "act", tac)
        tdv = p.inc(nc.vector.tensor_scalar(
            B.colbuf[:, col_q + h: col_q + h + 1], c0, -scale, None, ALU.mult),
            "dve")
    return tdv


def emit_vtok(p, B, VTf, vtok, pOf, nheads, vdim, width, tok_src_dve):
    nc = p.nc
    tdv = 0
    for h in range(nheads):
        p.wait("tensor", "dve", max(tok_src_dve, tdv))
        for j in range(NT):
            tp = p.inc(nc.tensor.transpose(
                pOf[:, j * vdim:(j + 1) * vdim],
                VTf[h * vdim:(h + 1) * vdim, ts(j, 128)],
                B.identf[h * vdim: h * vdim + vdim,
                         h * vdim: h * vdim + vdim]), "pe")
        p.wait("vector", "pe", tp)
        if getattr(B, "dbg_vtok", None):
            B.dbg_vtok(p, B, h, pOf)
        base = h * NT * width
        nj_half = (512 // vdim)
        for bk in range((NT * vdim) // 512):
            dst = vtok[:, base + bk * nj_half * width:
                       base + (bk + 1) * nj_half * width].rearrange(
                "p (j c) -> p j c", c=width)[:, :, 0:vdim]
            srcb = pOf[:, bk * 512:(bk + 1) * 512].rearrange(
                "p (j c) -> p j c", c=vdim)
            tdv = p.inc(nc.vector.tensor_copy(dst, srcb), "dve")
        if width > vdim:
            ocol = vtok[:, base: base + NT * width].rearrange(
                "p (j c) -> p j c", c=width)[:, :, vdim:vdim + 1]
            tdv = p.inc(nc.vector.memset(ocol, 1.0), "dve")
    return tdv


def attn_chunks_pair(p, B, QT, KT, vtok, attnT, pS, pO, col_q, scale,
                     ctx_store, vwidth=66):
    nc = p.nc
    tok_exp = {}
    tok_av = {}
    for c in range(NC2):
        q0 = c * 512
        jmax = 4 * c + 3
        t_prev_ep = B._tok_ctx_evac

        def emit_av(j, par, qlo, tmask):
            p.wait("tensor", "act", tok_exp[(par, j)])
            if tmask:
                p.wait("tensor", "dve", tmask)
            ta = 0
            for h in range(2):
                for strip in range(2):
                    ta = p.inc(nc.tensor.matmul(
                        pO[0:65, (h * 2 + strip) * 512 + qlo:
                           (h * 2 + strip) * 512 + 512],
                        vtok[ts(strip, 64),
                             h * NT * vwidth + j * vwidth:
                             h * NT * vwidth + j * vwidth + 65],
                        attnT[ts(strip, 64), (par * 2 + h) * 512 + qlo:
                              (par * 2 + h) * 512 + 512],
                        start=(j == 0), stop=(j == jmax),
                        tile_position=(strip * 64, 0)), "pe")
            tok_av[(par, j)] = ta
            return ta

        pend = None
        for j in range(jmax + 1):
            par = j % 2
            qlo = max(0, 128 * j - q0)
            p.wait("tensor", "act", tok_exp.get((par, j - 2), 0))
            if j == 0:
                p.wait("tensor", "dve", t_prev_ep)
            diag = 128 * j >= q0
            for h in range(2):
                tm = p.inc(nc.tensor.matmul(
                    pS[:, (par * 2 + h) * 512 + qlo: (par * 2 + h) * 512 + 512],
                    KT[ts(h, 64), ts(j, 128)],
                    QT[ts(h, 64), rs(q0 + qlo, q0 + 512)],
                    start=True, stop=not diag,
                    tile_position=(h * 64, 0)), "pe")
                if diag:
                    tm = p.inc(nc.tensor.matmul(
                        pS[:, (par * 2 + h) * 512 + qlo:
                           (par * 2 + h) * 512 + qlo + 128],
                        B.ident, B.negtri,
                        start=False, stop=True,
                        tile_position=(0, 0)), "pe")
            tok_s = tm
            if pend is not None:
                emit_av(*pend)
            p.wait("scalar", "pe", tok_s)
            p.wait("scalar", "pe", tok_av.get((par, j - 2), 0))
            pin = pS[:, par * 1024: par * 1024 + 1024].rearrange(
                "p (h q) -> p h q", q=512)[:, :, qlo:512]
            pout = attnT[:, par * 1024: par * 1024 + 1024].rearrange(
                "p (h q) -> p h q", q=512)[:, :, qlo:512]
            te = p.inc(nc.scalar.activation(
                pout, pin, AF.Exp, scale=scale,
                bias=B.colbuf[:, col_q: col_q + 1]), "act")
            tok_exp[(par, j)] = te
            pend = (j, par, qlo, 0)
        ta = emit_av(*pend)
        p.wait("vector", "pe", ta)
        for h in range(2):
            tv = p.inc(nc.vector.tensor_copy(
                B.scr_f1[0:65, 0:512],
                pO[0:65, (h * 2) * 512:(h * 2) * 512 + 512]), "dve")
            p.wait("vector", "dve", tv)
            tv = p.inc(nc.vector.tensor_tensor(
                B.scr_f1[0:65, 0:512], B.scr_f1[0:65, 0:512],
                pO[0:65, (h * 2 + 1) * 512:(h * 2 + 1) * 512 + 512], ALU.add),
                "dve")
            p.wait("vector", "dve", tv)
            tv = p.inc(nc.vector.reciprocal(B.scr_f2[0:1, 0:512],
                                            B.scr_f1[64:65, 0:512]), "dve")
            p.wait("tensor", "dve", tv)
            p.wait("tensor", "act", tok_exp[(jmax % 2, jmax)])
            tb = p.inc(nc.tensor.matmul(
                pS[0:64, 0:512], B.onesf[0:1, 0:64], B.scr_f2[0:1, 0:512],
                start=True, stop=True), "pe")
            p.wait("vector", "pe", tb)
            tv = p.inc(nc.vector.tensor_tensor(
                B.scr_f1[0:64, 0:512], B.scr_f1[0:64, 0:512], pS[0:64, 0:512],
                ALU.mult), "dve")
            p.wait("vector", "dve", tv)
            ctx_store(h, c)
            p.wait("vector", "dve", p.cnt["dve"])
        B._tok_ctx_evac = p.cnt["dve"]



def emit_l1_attn(p, B):
    emit_std_attn(p, B,
                  qkv=[("w_l1_attnT", lambda hp: hp * 128),
                       ("w_l1_attnT", lambda hp: 1024 + hp * 128),
                       ("w_l1_attnT", lambda hp: 2048 + hp * 128)],
                  wo_key="w_l1_projT")


def emit_l2_attn(p, B):
    emit_std_attn(p, B,
                  qkv=[("w_l2_qT", lambda hp: hp * 128),
                       ("w_l2_kT", lambda hp: hp * 128),
                       ("w_l2_vT", lambda hp: hp * 128)],
                  wo_key="w_l2_oT")


def emit_std_attn(p, B, qkv, wo_key):
    nc = p.nc
    d = B.d
    scale = 0.125
    with (
        nc.sbuf_tensor(uname("sa_qt"), [128, 2048], BF16) as QT,
        nc.sbuf_tensor(uname("sa_kt"), [128, 2048], BF16) as KT,
        nc.sbuf_tensor(uname("sa_vtf"), [128, 2048], F32) as VTf,
        nc.sbuf_tensor(uname("sa_vtok"), [128, 2 * NT * 66], BF16) as vtok,
        nc.sbuf_tensor(uname("sa_attnT"), [128, 2048], BF16) as attnT,
        nc.sbuf_tensor(uname("sa_wbuf"), [128, 3 * 1024], BF16) as wbuf,
        nc.psum_tensor([128, 2048], F32) as pS,
        nc.psum_tensor([128, 2048], F32) as pO,
    ):
        tok_proj_prev = 0
        for hp in range(8):
            tok_w = 0
            for w in range(3):
                key, coloff = qkv[w]
                tok_w = stream_cast(p, B, d[key], 0, 8, coloff(hp), 128, wbuf,
                                    w * 1024, None, war_pe=tok_proj_prev)
            p.dwait("tensor", tok_w)
            p.wait("tensor", "dve", B._tok_ctx_evac)
            tv = 0
            for w in range(3):
                for cc in range(4):
                    for dt in range(ND):
                        tm = p.inc(nc.tensor.matmul(
                            pS[:, cc * 512:(cc + 1) * 512],
                            wbuf[:, w * 1024 + dt * 128: w * 1024 + (dt + 1) * 128],
                            xnt(B, dt, rs(cc * 512, (cc + 1) * 512)),
                            start=(dt == 0), stop=(dt == ND - 1)), "pe")
                    p.wait("vector", "pe", tm)
                    if w < 2:
                        dstT = QT if w == 0 else KT
                        tv = p.inc(nc.vector.tensor_copy(
                            dstT[:, cc * 512:(cc + 1) * 512],
                            pS[:, cc * 512:(cc + 1) * 512]), "dve")
                    else:
                        tv = p.inc(nc.vector.tensor_copy(
                            VTf[:, cc * 512:(cc + 1) * 512],
                            pS[:, cc * 512:(cc + 1) * 512]), "dve")
                p.wait("tensor", "dve", tv)
            tok_qkv_pe = p.cnt["pe"]
            tok_proj_prev = tok_qkv_pe
            tok_qkv = tv
            tvt = emit_vtok(p, B, VTf, vtok, pO, 2, 64, 66, tok_qkv)
            tnb = emit_negB_pair(p, B, QT, KT, pS, scale, 60, tok_qkv_pe,
                                 tok_qkv)
            p.wait("vector", "dve", tnb)
            tnb = p.inc(nc.vector.tensor_tensor(
                B.colbuf[:, 62:63], B.colbuf[:, 60:61], B.colbuf[:, 61:62],
                ALU.min), "dve")
            p.wait("vector", "dve", tnb)
            p.wait("tensor", "dve", max(tnb, tvt))
            p.wait("scalar", "dve", max(tnb, tvt))

            def store(h, c, hp=hp):
                dst = B.arena[ts(h, 64),
                              hp * 2048 + c * 512: hp * 2048 + (c + 1) * 512]
                p.inc(nc.vector.tensor_copy(dst, B.scr_f1[0:64, 0:512]), "dve")
            if getattr(B, "dbg_attn_pre", None):
                B.dbg_attn_pre(p, B, hp, QT, KT, vtok)
            attn_chunks_pair(p, B, QT, KT, vtok, attnT, pS, pO, 62, scale,
                             store)
            if getattr(B, "dbg_attn_post", None):
                B.dbg_attn_post(p, B, hp)

        tok_x = {}
        tok_wq_prev = p.cnt["pe"]
        for quarter in range(4):
            tok_wp = stream_cast(p, B, d[wo_key], quarter * 256, 2, 0,
                                 1024, wbuf, 0, None, war_pe=tok_wq_prev)
            p.dwait("tensor", tok_wp)
            p.wait("tensor", "dve", B._tok_ctx_evac)
            for i in range(NT):
                par = i % 2
                p.wait("tensor", "dve", tok_x.get((quarter, i - 2), 0))
                for oc in range(2):
                    for hpq in range(2):
                        hp = quarter * 2 + hpq
                        tm = p.inc(nc.tensor.matmul(
                            pS[:, par * 1024 + oc * 512:
                               par * 1024 + oc * 512 + 512],
                            B.arena[:, hp * 2048 + i * 128: hp * 2048 + i * 128 + 128],
                            wbuf[:, hpq * 1024 + oc * 512: hpq * 1024 + (oc + 1) * 512],
                            start=(hpq == 0), stop=(hpq == 1)), "pe")
                p.wait("vector", "pe", tm)
                for oc in range(2):
                    t_ = p.inc(nc.vector.tensor_tensor(
                        B.X[:, i * D + oc * 512: i * D + (oc + 1) * 512],
                        B.X[:, i * D + oc * 512: i * D + (oc + 1) * 512],
                        pS[:, par * 1024 + oc * 512: par * 1024 + (oc + 1) * 512],
                        ALU.add), "dve")
                tok_x[(quarter, i)] = t_
            tok_wq_prev = p.cnt["pe"]
            p.wait("tensor", "dve", t_)
    B._tok_ctx_evac = 0
    p.barrier_into(("vector", "scalar", "tensor", "gpsimd", "sync"))


def emit_l1_ffn(p, B):
    nc = p.nc
    d = B.d
    with (
        nc.sbuf_tensor(uname("f1_w"), [128, 1536], BF16) as wsm,
        nc.sbuf_tensor(uname("f1_wout"), [64, 1024], BF16) as wout,
        nc.sbuf_tensor(uname("f1_cols"), [128, 16], F32) as cols,
        nc.sbuf_tensor(uname("f1_rows"), [1, 1024], BF16) as brow,
        nc.sbuf_tensor(uname("f1_st"), [64, 2048], BF16) as sT,
        nc.sbuf_tensor(uname("f1_gt"), [64, 2048], BF16) as gT,
        nc.psum_tensor([128, 2048], F32) as ps,
        nc.psum_tensor([128, 512], F32) as ps2,
        nc.psum_tensor([128, 1024], F32) as ps3,
    ):
        tok_w = stream_cast(p, B, d["w_l1_win"], 0, 8, 0, 64, wsm, 0, None)
        gF = p.dgrp()
        p.dma("sync", wsm[0:64, 512:1536], d["c_l1_R"], gF)
        p.wait("sync", "act", p.cnt["act"])
        p.dma("sync", B.scr_bf[:, 0:512].rearrange("p (j c) -> p j c", c=64),
              d["c_l1_w2mask"].rearrange("(j p) c -> p j c", p=128), gF)
        p.dma("sync", cols[:, 0:8], d["w_l1_w1col"], gF)
        p.dma("sync", cols[:, 8:16], d["w_l1_b1col"], gF)
        p.dma("sync", B.colbuf[:, 96:104], d["w_l1_w2col"], gF)
        p.dma("sync", B.colbuf[0:64, 104:105], d["w_l1_bin"], gF)
        p.dma("sync", B.colbuf[0:64, 105:106], d["w_l1_b2"], gF)
        p.dma("sync", brow[:], d["b_l1_bout_row"], gF)
        gW2 = p.dgrp()
        p.dma("gpsimd", wout[:], d["w_l1_wout"], gW2)
        p.dwait("vector", gF)
        w2b = B.arena[:, 23 * 1024: 23 * 1024 + 512]
        for j in range(8):
            tbd = p.inc(nc.vector.tensor_scalar(
                w2b[:, j * 64:(j + 1) * 64],
                B.scr_bf[:, j * 64:(j + 1) * 64],
                B.colbuf[:, 96 + j: 97 + j], None, ALU.mult), "dve")

        p.dwait("tensor", tok_w)
        p.wait("tensor", "dve", B.tok_xnT)
        for cc in range(4):
            for dt in range(ND):
                tm = p.inc(nc.tensor.matmul(
                    ps[0:64, cc * 512:(cc + 1) * 512],
                    wsm[:, dt * 64:(dt + 1) * 64],
                    xnt(B, dt, rs(cc * 512, (cc + 1) * 512)),
                    start=(dt == 0), stop=(dt == ND - 1)), "pe")
        p.wait("scalar", "pe", tm)
        p.dwait("scalar", gF)
        tst = p.inc(nc.scalar.activation(sT[:], ps[0:64, 0:2048], AF.Identity,
                                         bias=B.colbuf[0:64, 104:105]), "act")
        p.wait("tensor", "act", tst)
        p.dwait("tensor", gF)
        p.wait("tensor", "dve", tbd)
        tg = 0
        th = 0
        for cc in range(4):
            csl = rs(cc * 512, (cc + 1) * 512)
            for j in range(8):
                p.wait("tensor", "dve", th)
                tmr = p.inc(nc.tensor.matmul(
                    ps2[:, 0:512],
                    wsm[0:64, 512 + j * 128: 512 + (j + 1) * 128],
                    sT[0:64, csl], start=True, stop=True), "pe")
                p.wait("vector", "pe", tmr)
                p.wait("vector", "pe", tg)
                th = p.inc(nc.vector.tensor_scalar(
                    B.scr_bf[:, 512:1024], ps2[:, 0:512], cols[:, j:j + 1],
                    cols[:, 8 + j: 8 + j + 1], ALU.mult, ALU.add), "dve")
                p.wait("vector", "dve", th)
                th = p.inc(nc.vector.tensor_scalar(
                    B.scr_bf[:, 512:1024], B.scr_bf[:, 512:1024], 0.0, None,
                    ALU.max), "dve")
                p.wait("tensor", "dve", th)
                tg = p.inc(nc.tensor.matmul(
                    ps[0:64, csl],
                    w2b[:, j * 64:(j + 1) * 64],
                    B.scr_bf[:, 512:1024],
                    start=(j == 0), stop=(j == 7)), "pe")
        p.wait("scalar", "pe", tg)
        tgt = p.inc(nc.scalar.activation(gT[:], ps[0:64, 0:2048], AF.Identity,
                                         bias=B.colbuf[0:64, 105:106]), "act")
        p.wait("tensor", "act", tgt)
        p.dwait("tensor", gW2)
        p.dwait("tensor", gF)
        tx = 0
        for i in range(NT):
            p.wait("tensor", "dve", tx)
            for oc in range(2):
                osl = rs(oc * 512, (oc + 1) * 512)
                tm = p.inc(nc.tensor.matmul(
                    ps3[:, osl], B.ones[0:1, :], brow[0:1, osl],
                    start=True, stop=False), "pe")
                tm = p.inc(nc.tensor.matmul(
                    ps3[:, osl], gT[0:64, ts(i, 128)], wout[0:64, osl],
                    start=False, stop=True), "pe")
            p.wait("vector", "pe", tm)
            for oc in range(2):
                tx = p.inc(nc.vector.tensor_tensor(
                    B.X[:, i * D + oc * 512: i * D + (oc + 1) * 512],
                    B.X[:, i * D + oc * 512: i * D + (oc + 1) * 512],
                    ps3[:, oc * 512:(oc + 1) * 512], ALU.add), "dve")
    p.barrier_into(("vector", "scalar", "tensor", "gpsimd", "sync"))


def emit_l0_ffn(p, B):
    nc = p.nc
    d = B.d
    S2 = float(1.0 / (127.0 * 127.0))
    with (
        nc.sbuf_tensor(uname("f0_wg"), [128, 4096], BF16) as wgbuf,
        nc.sbuf_tensor(uname("f0_wu"), [128, 4096], BF16) as wubuf,
        nc.sbuf_tensor(uname("f0_silu"), [128, 1024], BF16) as silu,
        nc.psum_tensor([128, 4096], F32) as pEO,
    ):
        tok_ep = [0, 0, 0, 0]
        tok_g2 = 0
        tok_hh = [0, 0]
        tok_g2h = [0, 0]
        tok_gmm = [0, 0, 0, 0]
        tok_umm = [0, 0, 0, 0]
        for tc in range(4):
            csl = rs(tc * 512, (tc + 1) * 512)
            tok_h = 0
            tok_gate_evac = 0
            for f in range(24):
                par = f % 2
                slot = f % 4
                tok_w = stream_cast(p, B, d["w_l0_gT"], 0, 8, f * 128, 128,
                                    wgbuf, slot * 1024, None,
                                    war_pe=tok_gmm[slot])
                p.dwait("tensor", tok_w)
                p.wait("tensor", "dve", B.tok_xnT)
                p.wait("tensor", "act", tok_g2h[par])
                p.wait("tensor", "dve", max(tok_ep))
                for dt in range(ND):
                    tm = p.inc(nc.tensor.matmul(
                        pEO[:, par * 512: par * 512 + 512],
                        wgbuf[:, slot * 1024 + dt * 128: slot * 1024 + (dt + 1) * 128],
                        xnt(B, dt, csl),
                        start=(dt == 0), stop=(dt == ND - 1)), "pe")
                tok_gmm[slot] = tm
                p.wait("scalar", "pe", tm)
                p.wait("scalar", "dve", tok_hh[par])
                tok_g1 = p.inc(nc.scalar.activation(
                    B.scr_f1[:, par * 512: par * 512 + 512],
                    pEO[:, par * 512: par * 512 + 512], AF.Tanh,
                    scale=float(0.5 / 127.0)), "act")
                tok_g2 = p.inc(nc.scalar.activation(
                    B.scr_f2[:, par * 512: par * 512 + 512],
                    pEO[:, par * 512: par * 512 + 512], AF.Identity,
                    scale=float(0.5 / 127.0)), "act")
                tok_g2h[par] = tok_g2
                p.wait("vector", "act", tok_g2)
                tok_gate_evac = p.inc(nc.vector.scalar_tensor_tensor(
                    silu[:, par * 512: par * 512 + 512],
                    B.scr_f1[:, par * 512: par * 512 + 512], 1.0,
                    B.scr_f2[:, par * 512: par * 512 + 512],
                    ALU.add, ALU.mult), "dve")
                tok_wu = stream_cast(p, B, d["w_l0_uT"], 0, 8, f * 128, 128,
                                     wubuf, slot * 1024, None,
                                     war_pe=tok_umm[slot])
                p.dwait("tensor", tok_wu)
                p.wait("tensor", "dve", tok_hh[par])
                for dt in range(ND):
                    tm = p.inc(nc.tensor.matmul(
                        pEO[:, 1024 + par * 512: 1024 + (par + 1) * 512],
                        wubuf[:, slot * 1024 + dt * 128: slot * 1024 + (dt + 1) * 128],
                        xnt(B, dt, csl),
                        start=(dt == 0), stop=(dt == ND - 1)), "pe")
                tok_umm[slot] = tm
                p.wait("vector", "pe", tm)
                p.wait("vector", "dve", tok_gate_evac)
                tok_h = p.inc(nc.vector.tensor_tensor(
                    B.arena[:, f * 512:(f + 1) * 512],
                    silu[:, par * 512: par * 512 + 512],
                    pEO[:, 1024 + par * 512: 1024 + (par + 1) * 512],
                    ALU.mult), "dve")
                tok_hh[par] = tok_h
            if getattr(B, "dbg_l0h", None):
                B.dbg_l0h(p, B, tc)
            if FFN_BAR >= 1:
                p.barrier_into(("vector", "scalar", "tensor", "gpsimd", "sync"))
            p.wait("tensor", "dve", tok_h)
            tok_dmm = [max(tok_gmm)] * 4
            for f in range(24):
                par = f % 2
                slot = f % 4
                tok_wd = stream_cast(p, B, d["w_l0_dT"], f * 128, 1, 0, 1024,
                                     wgbuf, slot * 1024, None,
                                     war_pe=tok_dmm[slot])
                p.dwait("tensor", tok_wd)
                for i in range(4):
                    for oc in range(2):
                        tm = p.inc(nc.tensor.matmul(
                            pEO[:, i * 1024 + oc * 512:
                                i * 1024 + (oc + 1) * 512],
                            B.arena[:, f * 512 + i * 128: f * 512 + (i + 1) * 128],
                            wgbuf[:, slot * 1024 + oc * 512:
                                  slot * 1024 + (oc + 1) * 512],
                            start=(f == 0), stop=(f == 23)), "pe")
                tok_dmm[slot] = tm
            tok_gmm = [tm, tm, tm, tm]
            if FFN_BAR >= 2:
                p.barrier_into(("vector", "scalar", "tensor", "gpsimd", "sync"))
            p.wait("vector", "pe", tm)
            for i in range(4):
                it = tc * 4 + i
                for oc in range(2):
                    xs = rs(it * D + oc * 512, it * D + (oc + 1) * 512)
                    tv = p.inc(nc.vector.scalar_tensor_tensor(
                        B.X[:, xs],
                        pEO[:, i * 1024 + oc * 512: i * 1024 + (oc + 1) * 512],
                        S2, B.X[:, xs], ALU.mult, ALU.add), "dve")
                tok_ep[i] = tv
            if FFN_BAR >= 3:
                p.barrier_into(("vector", "scalar", "tensor", "gpsimd", "sync"))
    p.barrier_into(("vector", "scalar", "tensor", "gpsimd", "sync"))


def emit_l0_attn(p, B):
    nc = p.nc
    d = B.d
    scale = 0.125
    wmix_d = d["l0_wmix"]
    with (
        nc.sbuf_tensor(uname("a0_wfull"), [128, 8192], BF16) as wfull,
        nc.sbuf_tensor(uname("a0_p64"), [128, 1024], BF16) as p64s,
        nc.sbuf_tensor(uname("a0_acol"), [128, 768], F32) as acol,
        nc.sbuf_tensor(uname("a0_bcol"), [128, 24], BF16) as bcol,
        nc.sbuf_tensor(uname("a0_abuf"), [128, 1024], BF16) as abuf,
        nc.sbuf_tensor(uname("a0_wmixT"), [128, 1024], BF16) as wmixT,
        nc.psum_tensor([128, 1024], F32) as pW,
        nc.psum_tensor([128, 512], F32) as pB2,
        nc.psum_tensor([128, 1024], BF16) as pT,
    ):
        gA = p.dgrp()
        p.dma("sync", p64s[:].rearrange("p (a c) -> p a c", c=128),
              d["c_p64"].rearrange("(a p) c -> p a c", p=128), gA)
        for w, (ak, bk) in enumerate((("w_l0_aq", "w_l0_bq"),
                                      ("w_l0_ak", "w_l0_bk"),
                                      ("w_l0_av", "w_l0_bv"))):
            p.dma("sync", acol[:, w * 256:(w + 1) * 256], d[ak], gA)
            p.dma("gpsimd", bcol[:, w * 8:(w + 1) * 8], d[bk], gA)
        p.dwait("vector", gA)
        p.dwait("tensor", gA)
        tok_mm = 0
        tok_st = 0
        gWS = p.dgrp()
        for w, wk in enumerate(("w_l0_wq", "w_l0_wk", "w_l0_wv")):
            tok_w = stream_cast(p, B, d[wk], 0, 8, 0, 1024, wfull, 0, None,
                                war_pe=tok_mm)
            for h in range(16):
                p.wait("vector", "pe", tok_mm)
                for ot in range(8):
                    for ph in range(2):
                        ta = p.inc(nc.vector.tensor_scalar(
                            abuf[:, ot * 128 + ph * 64: ot * 128 + ph * 64 + 64],
                            p64s[:, ot * 128 + ph * 64: ot * 128 + ph * 64 + 64],
                            acol[:, w * 256 + h * 16 + ph * 8 + ot:
                                 w * 256 + h * 16 + ph * 8 + ot + 1],
                            None, ALU.mult), "dve")
                p.wait("tensor", "dve", ta)
                p.dwait("tensor", tok_w)
                p.wait("tensor", "dve", tok_wm_ev if (w or h) else 0)
                for ot in range(8):
                    for oc in range(2):
                        tok_mm = p.inc(nc.tensor.matmul(
                            pW[:, oc * 512:(oc + 1) * 512],
                            abuf[:, ot * 128:(ot + 1) * 128],
                            wfull[:, ot * 1024 + oc * 512:
                                  ot * 1024 + (oc + 1) * 512],
                            start=(ot == 0), stop=(ot == 7)), "pe")
                    tok_mm = p.inc(nc.tensor.matmul(
                        pB2[:, 0:1], abuf[:, ot * 128:(ot + 1) * 128],
                        bcol[:, w * 8 + ot: w * 8 + ot + 1],
                        start=(ot == 0), stop=(ot == 7)), "pe")
                p.wait("vector", "pe", tok_mm)
                for oc in range(2):
                    tv = p.inc(nc.vector.tensor_copy(
                        B.scr_bf[:, oc * 512:(oc + 1) * 512],
                        pW[:, oc * 512:(oc + 1) * 512]), "dve")
                tv = p.inc(nc.vector.tensor_copy(
                    B.colbuf[:, w * 16 + h: w * 16 + h + 1], pB2[:, 0:1]),
                    "dve")
                tok_wm_ev = tv
                p.wait("tensor", "dve", tv)
                for it in range(8):
                    tp = p.inc(nc.tensor.transpose(
                        pT[:, it * 128:(it + 1) * 128],
                        B.scr_bf[:, it * 128:(it + 1) * 128], B.ident), "pe")
                p.wait("vector", "pe", tp)
                p.dwait("vector", gWS)
                tv = p.inc(nc.vector.tensor_copy(wmixT[:], pT[:]), "dve")
                p.wait("sync", "dve", tv)
                p.dma("sync",
                      wmix_d[w * 16 + h].rearrange("(a p) c -> p a c", p=128),
                      wmixT[:].rearrange("p (a c) -> p a c", c=128), gWS)
                tok_st = p.cnt["dve"]
        p.dwait("sync", gWS)
    p.barrier_into(("vector", "scalar", "tensor", "gpsimd", "sync"))

    with (
        nc.sbuf_tensor(uname("b0_wmix"), [128, 3 * 1024], BF16) as wmix,
        nc.sbuf_tensor(uname("b0_qtm"), [128, 2048], BF16) as QTm,
        nc.sbuf_tensor(uname("b0_qtd"), [128, 2048], BF16) as QTd,
        nc.sbuf_tensor(uname("b0_ktm"), [128, 2048], BF16) as KTm,
        nc.sbuf_tensor(uname("b0_vtf"), [128, 2048], F32) as VTmf,
        nc.sbuf_tensor(uname("b0_vtok"), [128, 2 * NT * 66], BF16) as vtok,
        nc.sbuf_tensor(uname("b0_attnT"), [128, 2048], BF16) as attnT,
        nc.sbuf_tensor(uname("b0_n0"), [64, 2048], BF16) as n0buf,
        nc.sbuf_tensor(uname("b0_ccol"), [64, 32], F32) as ccol,
        nc.sbuf_tensor(uname("b0_wo"), [128, 2048], BF16) as wobuf,
        nc.sbuf_tensor(uname("b0_worow"), [1, 1024], BF16) as worow,
        nc.psum_tensor([128, 2048], F32) as pS,
        nc.psum_tensor([128, 2048], F32) as pO,
    ):
        gB0 = p.dgrp()
        p.dma("sync", ccol[:], d["w_l0_col"], gB0)
        p.dma("sync", worow[:], d["b_l0_bo_row"], gB0)
        p.dwait("vector", gB0)
        p.dwait("tensor", gB0)
        for h in range(16):
            gH = p.dgrp()
            p.wait("sync", "pe", getattr(B, "_l0_projpe", 0))
            for w in range(3):
                p.dma("sync",
                      wmix[:, w * 1024:(w + 1) * 1024].rearrange(
                          "p (a c) -> p a c", c=128),
                      wmix_d[w * 16 + h].rearrange("(a p) c -> p a c", p=128),
                      gH)
            p.dwait("tensor", gH)
            p.wait("tensor", "dve", B._tok_ctx_evac)
            tv = 0
            for w in range(3):
                for cc in range(4):
                    for dt in range(ND):
                        tm = p.inc(nc.tensor.matmul(
                            pS[:, cc * 512:(cc + 1) * 512],
                            wmix[:, w * 1024 + dt * 128: w * 1024 + (dt + 1) * 128],
                            xnt(B, dt, rs(cc * 512, (cc + 1) * 512)),
                            start=(dt == 0), stop=(dt == ND - 1)), "pe")
                    p.wait("vector", "pe", tm)
                    if w < 2:
                        dstT = QTm if w == 0 else KTm
                        tv = p.inc(nc.vector.tensor_scalar(
                            dstT[:, cc * 512:(cc + 1) * 512],
                            pS[:, cc * 512:(cc + 1) * 512],
                            B.colbuf[:, w * 16 + h: w * 16 + h + 1],
                            None, ALU.add), "dve")
                    else:
                        tv = p.inc(nc.vector.tensor_scalar(
                            VTmf[:, cc * 512:(cc + 1) * 512],
                            pS[:, cc * 512:(cc + 1) * 512],
                            B.colbuf[:, w * 16 + h: w * 16 + h + 1],
                            None, ALU.add), "dve")
                p.wait("tensor", "dve", tv)
            B._l0_projpe = p.cnt["pe"]
            tok_qkv_pe = p.cnt["pe"]
            tok_qkv = tv
            tvt = emit_vtok(p, B, VTmf, vtok, pO, 2, 64, 66, tok_qkv)
            tnb = emit_negB_full(p, B, QTm, KTm, pS, scale, 100, tok_qkv_pe,
                                 tok_qkv)
            p.wait("tensor", "dve", max(tnb, tvt))
            p.wait("scalar", "dve", max(tnb, tvt))
            for p1 in range(2):
                gQ = p.dgrp()
                p.wait("sync", "dve", tok_qkv)
                p.wait("sync", "pe", p.cnt["pe"])
                p.dma("sync", QTd[0:64, :], QTm[p1 * 64: p1 * 64 + 64, :], gQ)
                p.dma("sync", QTd[64:128, :], QTm[p1 * 64: p1 * 64 + 64, :], gQ)
                p.dwait("tensor", gQ)
                attn_chunks_l0(p, B, QTd, KTm, vtok, attnT, pS, pO, 100, scale,
                               h, p1, n0buf, ccol)
        tok_x = 0
        for quarter in range(4):
            tok_wp = stream_cast(p, B, d["w_l0_woT"], quarter * 256, 2, 0,
                                 1024, wobuf, 0, VTmf)
            p.dwait("tensor", tok_wp)
            p.wait("tensor", "dve", B._tok_ctx_evac)
            for i in range(NT):
                for oc in range(2):
                    if quarter == 0:
                        tm = p.inc(nc.tensor.matmul(
                            pS[:, oc * 512: oc * 512 + 512], B.ones[0:1, :],
                            worow[0:1, oc * 512:(oc + 1) * 512],
                            start=True, stop=False), "pe")
                    for hpq in range(2):
                        hp = quarter * 2 + hpq
                        tm = p.inc(nc.tensor.matmul(
                            pS[:, oc * 512: oc * 512 + 512],
                            B.arena[:, hp * 2048 + i * 128: hp * 2048 + i * 128 + 128],
                            wobuf[:, hpq * 1024 + oc * 512: hpq * 1024 + (oc + 1) * 512],
                            start=(hpq == 0 and quarter != 0),
                            stop=(hpq == 1)), "pe")
                    p.wait("vector", "pe", tm)
                    tok_x = p.inc(nc.vector.tensor_tensor(
                        B.X[:, i * D + oc * 512: i * D + (oc + 1) * 512],
                        B.X[:, i * D + oc * 512: i * D + (oc + 1) * 512],
                        pS[:, oc * 512: oc * 512 + 512], ALU.add), "dve")
                    p.wait("tensor", "dve", tok_x)
    B._tok_ctx_evac = 0
    B._l0_projpe = 0
    p.barrier_into(("vector", "scalar", "tensor", "gpsimd", "sync"))


def emit_negB_full(p, B, QT, KT, ps, scale, col_q, tok_ready_pe, tok_ready_dve):
    nc = p.nc
    tdv = 0
    tm_prev = tok_ready_pe
    for w, MT in ((0, QT), (1, KT)):
        p.wait("scalar", "pe", tm_prev)
        p.wait("scalar", "dve", max(tok_ready_dve, tdv))
        ta = p.inc(nc.scalar.activation(B.scr_bf[:, 0:2048], MT[:],
                                        AF.Square), "act")
        p.wait("tensor", "act", ta)
        p.wait("tensor", "dve", tdv)
        for cc in range(4):
            tm = p.inc(nc.tensor.matmul(
                ps[:, cc * 512:(cc + 1) * 512], B.ones[:, :],
                B.scr_bf[:, cc * 512:(cc + 1) * 512],
                start=True, stop=True), "pe")
        tm_prev = tm
        p.wait("vector", "pe", tm)
        for cc in range(4):
            tdv = p.inc(nc.vector.tensor_reduce(
                B.colbuf[:, 122 + cc: 123 + cc],
                ps[:, cc * 512:(cc + 1) * 512], AX.X, ALU.max), "dve")
        p.wait("vector", "dve", tdv)
        tdv = p.inc(nc.vector.tensor_reduce(
            B.colbuf[:, 120 + w: 121 + w], B.colbuf[:, 122:126],
            AX.X, ALU.max), "dve")
        p.wait("vector", "dve", tdv)
    c0 = B.colbuf[:, 120:121]
    c1 = B.colbuf[:, 121:122]
    tdv = p.inc(nc.vector.tensor_tensor(c0, c0, c1, ALU.mult), "dve")
    p.wait("scalar", "dve", tdv)
    tac = p.inc(nc.scalar.activation(c0, c0, AF.Ln), "act")
    p.wait("scalar", "act", tac)
    tac = p.inc(nc.scalar.activation(c0, c0, AF.Exp, scale=0.5), "act")
    p.wait("vector", "act", tac)
    tdv = p.inc(nc.vector.tensor_scalar(
        B.colbuf[:, col_q: col_q + 1], c0, -scale, None, ALU.mult), "dve")
    return tdv


def attn_chunks_l0(p, B, QTd, KTm, vtok, attnT, pS, pO, col_b, scale,
                   h, p1, n0buf, ccol):
    nc = p.nc
    tok_exp = {}
    tok_av = {}
    for c in range(NC2):
        q0 = c * 512
        jmax = 4 * c + 3
        t_prev_ep = B._tok_ctx_evac

        def emit_av(j, par, qlo, tmask):
            p.wait("tensor", "act", tok_exp[(par, j)])
            if tmask:
                p.wait("tensor", "dve", tmask)
            ta = 0
            for p2 in range(2):
                for strip in range(2):
                    ta = p.inc(nc.tensor.matmul(
                        pO[0:65, strip * 512 + qlo: strip * 512 + 512],
                        vtok[ts(strip, 64),
                             p2 * NT * 66 + j * 66: p2 * NT * 66 + j * 66 + 65],
                        attnT[ts(strip, 64), (par * 2 + p2) * 512 + qlo:
                              (par * 2 + p2) * 512 + 512],
                        start=(j == 0 and p2 == 0),
                        stop=(j == jmax and p2 == 1),
                        tile_position=(strip * 64, 0)), "pe")
            tok_av[(par, j)] = ta
            return ta

        pend = None
        for j in range(jmax + 1):
            par = j % 2
            qlo = max(0, 128 * j - q0)
            p.wait("tensor", "act", tok_exp.get((par, j - 2), 0))
            if j == 0:
                p.wait("tensor", "dve", t_prev_ep)
            diag = 128 * j >= q0
            for p2 in range(2):
                tm = p.inc(nc.tensor.matmul(
                    pS[:, (par * 2 + p2) * 512 + qlo: (par * 2 + p2) * 512 + 512],
                    KTm[ts(p2, 64), ts(j, 128)],
                    QTd[ts(p2, 64), rs(q0 + qlo, q0 + 512)],
                    start=True, stop=not diag,
                    tile_position=(p2 * 64, 0)), "pe")
                if diag:
                    tm = p.inc(nc.tensor.matmul(
                        pS[:, (par * 2 + p2) * 512 + qlo:
                           (par * 2 + p2) * 512 + qlo + 128],
                        B.ident, B.negtri,
                        start=False, stop=True,
                        tile_position=(0, 0)), "pe")
            tok_s = tm
            if pend is not None:
                emit_av(*pend)
            p.wait("scalar", "pe", tok_s)
            p.wait("scalar", "pe", tok_av.get((par, j - 2), 0))
            pin = pS[:, par * 1024: par * 1024 + 1024].rearrange(
                "p (h q) -> p h q", q=512)[:, :, qlo:512]
            pout = attnT[:, par * 1024: par * 1024 + 1024].rearrange(
                "p (h q) -> p h q", q=512)[:, :, qlo:512]
            te = p.inc(nc.scalar.activation(
                pout, pin, AF.Exp, scale=scale,
                bias=B.colbuf[:, col_b: col_b + 1]), "act")
            tok_exp[(par, j)] = te
            pend = (j, par, qlo, 0)
        ta = emit_av(*pend)
        p.wait("vector", "pe", ta)
        tv = p.inc(nc.vector.tensor_copy(
            B.scr_f1[0:65, 0:512], pO[0:65, 0:512]), "dve")
        p.wait("vector", "dve", tv)
        tv = p.inc(nc.vector.tensor_tensor(
            B.scr_f1[0:65, 0:512], B.scr_f1[0:65, 0:512],
            pO[0:65, 512:1024], ALU.add), "dve")
        p.wait("vector", "dve", tv)
        tv = p.inc(nc.vector.reciprocal(B.scr_f2[0:1, 0:512],
                                        B.scr_f1[64:65, 0:512]), "dve")
        p.wait("tensor", "dve", tv)
        p.wait("tensor", "act", tok_exp[(jmax % 2, jmax)])
        tb = p.inc(nc.tensor.matmul(
            pS[0:64, 0:512], B.onesf[0:1, 0:64], B.scr_f2[0:1, 0:512],
            start=True, stop=True), "pe")
        p.wait("vector", "pe", tb)
        tv = p.inc(nc.vector.tensor_tensor(
            B.scr_f1[0:64, 0:512], B.scr_f1[0:64, 0:512], pS[0:64, 0:512],
            ALU.mult), "dve")
        p.wait("vector", "dve", tv)
        if p1 == 0:
            tv = p.inc(nc.vector.tensor_copy(
                n0buf[0:64, c * 512:(c + 1) * 512], B.scr_f1[0:64, 0:512]),
                "dve")
        else:
            tv = p.inc(nc.vector.tensor_scalar(
                B.scr_f2[0:64, 0:512], B.scr_f1[0:64, 0:512],
                ccol[0:64, h * 2 + 1: h * 2 + 2], None, ALU.mult), "dve")
            p.wait("vector", "dve", tv)
            dst = B.arena[(h % 2) * 64: (h % 2) * 64 + 64,
                          (h // 2) * 2048 + c * 512: (h // 2) * 2048 + (c + 1) * 512]
            tv = p.inc(nc.vector.scalar_tensor_tensor(
                dst, n0buf[0:64, c * 512:(c + 1) * 512],
                ccol[0:64, h * 2: h * 2 + 1], B.scr_f2[0:64, 0:512],
                ALU.mult, ALU.add), "dve")
        B._tok_ctx_evac = p.cnt["dve"]


def emit_l2_ffn(p, B):
    nc = p.nc
    d = B.d
    with (
        nc.sbuf_tensor(uname("f2_w1"), [128, 4096], BF16) as w1buf,
        nc.sbuf_tensor(uname("f2_w2"), [128, 4096], BF16) as w2buf,
        nc.sbuf_tensor(uname("f2_b1"), [128, 32], F32) as b1col,
        nc.sbuf_tensor(uname("f2_rows"), [1, 3 * 1024], BF16) as rows3,
        nc.sbuf_tensor(uname("f2_gb"), [128, 2048], BF16) as gb,
        nc.psum_tensor([128, 4096], F32) as pEO,
    ):
        gL2 = p.dgrp()
        p.dma("sync", b1col[:], d["w_l2_b1col"], gL2)
        p.dma("sync", rows3[0:1, 0:1024], d["b_l2_b2_row"], gL2)
        p.dma("sync", rows3[0:1, 1024:2048], d["b_l2_gamma_row"], gL2)
        p.dma("sync", rows3[0:1, 2048:3072], d["b_l2_beta_row"], gL2)
        p.dwait("tensor", gL2)
        for gi in range(2):
            for oc in range(2):
                tm = p.inc(nc.tensor.matmul(
                    pEO[:, oc * 512:(oc + 1) * 512], B.ones[0:1, :],
                    rows3[0:1, (1 + gi) * 1024 + oc * 512:
                          (1 + gi) * 1024 + (oc + 1) * 512],
                    start=True, stop=True), "pe")
            p.wait("vector", "pe", tm)
            for oc in range(2):
                tgb = p.inc(nc.vector.tensor_copy(
                    gb[:, gi * 1024 + oc * 512: gi * 1024 + (oc + 1) * 512],
                    pEO[:, oc * 512:(oc + 1) * 512]), "dve")
            p.wait("tensor", "dve", tgb)

        mean, rstd = B.stats[:, 0:16], B.stats[:, 16:32]
        c_sum = B.colbuf[:, 110:111]
        c_m2 = B.colbuf[:, 108:109]
        c_nm2 = B.colbuf[:, 111:112]
        c_rs2 = B.colbuf[:, 109:110]
        tok_ep = [0, 0, 0, 0]
        tok_w1mm = [0, 0, 0, 0]
        tok_w2mm = [0, 0, 0, 0]
        for tc in range(4):
            csl = rs(tc * 512, (tc + 1) * 512)
            tok_h0 = 0
            for f in range(32):
                tok_w1 = stream_cast(p, B, d["w_l2_w1"], 0, 8, f * 128, 128,
                                     w1buf, (f % 4) * 1024, None,
                                     war_pe=tok_w1mm[f % 4])
                p.dwait("tensor", tok_w1)
                p.wait("tensor", "dve", B.tok_xnT)
                p.wait("tensor", "act", tok_h0)
                p.wait("tensor", "dve", max(tok_ep))
                for dt in range(ND):
                    tm = p.inc(nc.tensor.matmul(
                        pEO[:, (f % 2) * 512:(f % 2) * 512 + 512],
                        w1buf[:, (f % 4) * 1024 + dt * 128:
                              (f % 4) * 1024 + (dt + 1) * 128],
                        xnt(B, dt, csl),
                        start=(dt == 0), stop=(dt == ND - 1)), "pe")
                tok_w1mm[f % 4] = tm
                p.wait("scalar", "pe", tm)
                p.dwait("scalar", gL2)
                tok_h0 = p.inc(nc.scalar.activation(
                    B.arena[:, f * 512:(f + 1) * 512],
                    pEO[:, (f % 2) * 512:(f % 2) * 512 + 512],
                    AF.Relu, bias=b1col[:, f:f + 1]), "act")
            p.wait("tensor", "act", tok_h0)
            for i in range(4):
                for oc in range(2):
                    tm = p.inc(nc.tensor.matmul(
                        pEO[:, i * 1024 + oc * 512: i * 1024 + (oc + 1) * 512],
                        B.ones[0:1, :],
                        rows3[0:1, oc * 512:(oc + 1) * 512],
                        start=True, stop=False), "pe")
            for f in range(32):
                tok_w2 = stream_cast(p, B, d["w_l2_w2"], f * 128, 1, 0, 1024,
                                     w2buf, (f % 4) * 1024, None,
                                     war_pe=tok_w2mm[f % 4])
                p.dwait("tensor", tok_w2)
                for i in range(4):
                    for oc in range(2):
                        tm = p.inc(nc.tensor.matmul(
                            pEO[:, i * 1024 + oc * 512:
                                i * 1024 + (oc + 1) * 512],
                            B.arena[:, f * 512 + i * 128: f * 512 + (i + 1) * 128],
                            w2buf[:, (f % 4) * 1024 + oc * 512:
                                  (f % 4) * 1024 + (oc + 1) * 512],
                            start=False, stop=(f == 31)), "pe")
                tok_w2mm[f % 4] = tm
            tok_eo = tm
            for i in range(4):
                it = tc * 4 + i
                p.wait("vector", "pe", tok_eo)
                tv = p.inc(nc.vector.tensor_scalar(
                    B.scr_f1[:, 0:1024], B.X[:, ts(it, D)],
                    mean[:, it:it + 1], rstd[:, it:it + 1],
                    ALU.subtract, ALU.mult), "dve")
                p.wait("vector", "dve", tv)
                for oc in range(2):
                    tv = p.inc(nc.vector.tensor_tensor(
                        B.scr_f2[:, oc * 512:(oc + 1) * 512],
                        B.scr_f1[:, oc * 512:(oc + 1) * 512],
                        pEO[:, i * 1024 + oc * 512: i * 1024 + (oc + 1) * 512],
                        ALU.add), "dve")
                p.wait("vector", "dve", tv)
                tok_ep[i] = tv
                p.wait("scalar", "dve", tv)
                ta = p.inc(nc.scalar.activation(
                    B.scr_bf[:, 0:1024], B.scr_f2[:, 0:1024], AF.Identity,
                    accum_out=c_sum), "act")
                p.wait("vector", "act", ta)
                tv = p.inc(nc.vector.tensor_scalar(c_m2, c_sum, 1.0 / D, None,
                                                   ALU.mult), "dve")
                tv = p.inc(nc.vector.tensor_scalar(c_nm2, c_m2, -1.0, None,
                                                   ALU.mult), "dve")
                p.wait("vector", "dve", tv)
                p.wait("scalar", "dve", tv)
                ta = p.inc(nc.scalar.activation(
                    B.scr_bf[:, 0:1024], B.scr_f2[:, 0:1024], AF.Square,
                    bias=c_nm2, accum_out=c_sum), "act")
                p.wait("vector", "act", ta)
                tv = p.inc(nc.vector.tensor_scalar(c_rs2, c_sum, 1.0 / D, None,
                                                   ALU.mult), "dve")
                p.wait("scalar", "dve", tv)
                ta = p.inc(nc.scalar.activation(c_rs2, c_rs2, AF.Ln,
                                                bias=B.epscol), "act")
                p.wait("scalar", "act", ta)
                ta = p.inc(nc.scalar.activation(c_rs2, c_rs2, AF.Exp,
                                                scale=-0.5), "act")
                p.wait("vector", "act", ta)
                tv = p.inc(nc.vector.tensor_scalar(
                    B.scr_f1[:, 0:1024], B.scr_f2[:, 0:1024], c_m2, c_rs2,
                    ALU.subtract, ALU.mult), "dve")
                p.wait("vector", "dve", tv)
                tv = p.inc(nc.vector.tensor_tensor(
                    B.scr_f2[:, 0:1024], B.scr_f1[:, 0:1024],
                    gb[:, 0:1024], ALU.mult), "dve")
                p.wait("vector", "dve", tv)
                tv = p.inc(nc.vector.tensor_tensor(
                    B.scr_f2[:, 0:1024], B.scr_f2[:, 0:1024],
                    gb[:, 1024:2048], ALU.add), "dve")
                p.wait("vector", "dve", tv)
                tv = p.inc(nc.vector.tensor_tensor(
                    B.X[:, ts(it, D)], B.X[:, ts(it, D)],
                    B.scr_f2[:, 0:1024], ALU.add), "dve")
                tok_ep[i] = tv
    p.barrier_into(("vector", "scalar", "tensor", "gpsimd", "sync"))


def emit_l3_ffn(p, B):
    nc = p.nc
    d = B.d
    NE = 8
    M3 = 256
    import contextlib
    with contextlib.ExitStack() as _st3:
        _e = _st3.enter_context
        Tb = _e(nc.sbuf_tensor(uname("f3_T"), [8, 2048], F32))
        G8 = _e(nc.sbuf_tensor(uname("f3_G8"), [8, 2048], F32))
        Ttok = _e(nc.sbuf_tensor(uname("f3_tok"), [128, 128], F32))
        gtok = _e(nc.sbuf_tensor(uname("f3_gtok"), [128, 128], F32))
        selb = _e(nc.sbuf_tensor(uname("f3_sel"), [128, 128], F32))
        hardb = _e(nc.sbuf_tensor(uname("f3_hard"), [128, 128], F32))
        Gb = _e(nc.sbuf_tensor(uname("f3_G"), [128, 128], F32))
        Gbf = _e(nc.sbuf_tensor(uname("f3_Gbf"), [128, 128], BF16))
        rankb = _e(nc.sbuf_tensor(uname("f3_rank"), [128, 128], F32))
        taub = _e(nc.sbuf_tensor(uname("f3_tau"), [128, 8], F32))
        PeT = _e(nc.sbuf_tensor(uname("f3_pet"), [128, 16 * 256], BF16))
        PeL = _e(nc.sbuf_tensor(uname("f3_pel"), [128, 256], BF16))
        ReT = _e(nc.sbuf_tensor(uname("f3_ret"), [128, 8 * 256], BF16))
        Ye = _e(nc.sbuf_tensor(uname("f3_y"), [128, 2048], BF16))
        webuf = _e(nc.sbuf_tensor(uname("f3_we"), [128, 4 * 1024], BF16))
        gwbuf = _e(nc.sbuf_tensor(uname("f3_gw"), [128, 64], BF16))
        pA = _e(nc.psum_tensor([128, 2560], F32))
        pB16 = _e(nc.psum_tensor([128, 1024], BF16))
        pY = _e(nc.psum_tensor([128, 1024], F32))
        mean, rstd = B.stats[:, 0:16], B.stats[:, 16:32]
        p.wait("vector", "act", B.tok_lnstats)
        for i in range(NT):
            txn = p.inc(nc.vector.tensor_scalar(
                B.arena[:, i * 1024:(i + 1) * 1024], B.X[:, ts(i, D)],
                mean[:, i:i + 1], rstd[:, i:i + 1],
                ALU.subtract, ALU.mult), "dve")
        B._l3_xn_tok = txn
        tok_gw = stream_cast(p, B, d["w_l3_gate"], 0, 8, 0, 8, gwbuf, 0, None)
        p.dwait("tensor", tok_gw)
        p.wait("tensor", "dve", B.tok_xnT)
        for cc in range(4):
            for dt in range(ND):
                tm = p.inc(nc.tensor.matmul(
                    pA[0:8, cc * 512:(cc + 1) * 512],
                    gwbuf[:, dt * 8:(dt + 1) * 8],
                    xnt(B, dt, rs(cc * 512, (cc + 1) * 512)),
                    start=(dt == 0), stop=(dt == ND - 1)), "pe")
        p.wait("vector", "pe", tm)
        for cc in range(4):
            tv = p.inc(nc.vector.tensor_copy(
                Tb[:, cc * 512:(cc + 1) * 512],
                pA[0:8, cc * 512:(cc + 1) * 512]), "dve")
        p.wait("vector", "dve", tv)
        tv = p.inc(nc.vector.tensor_scalar(Tb[:], Tb[:], 1e-6, None, ALU.max),
                   "dve")
        p.wait("scalar", "dve", tv)
        ta = p.inc(nc.scalar.activation(Tb[:], Tb[:], AF.Ln), "act")
        p.wait("vector", "act", ta)
        p.wait("tensor", "act", ta)

        c_rm = B.colbuf[0:8, 110:111]
        c_nrm = B.colbuf[0:8, 111:112]
        c_se = B.colbuf[0:8, 112:113]
        rm2 = B.colbuf[:, 113:129] if False else None
        for it in range(8):
            tv = p.inc(nc.vector.tensor_reduce(c_rm, Tb[:], AX.X, ALU.max),
                       "dve")
            p.wait("vector", "dve", tv)
            tv = p.inc(nc.vector.tensor_scalar(c_nrm, c_rm, -1.0, None,
                                               ALU.mult), "dve")
            p.wait("scalar", "dve", tv)
            ta = p.inc(nc.scalar.activation(B.scr_bf[0:8, 0:2048], Tb[:],
                                            AF.Exp, bias=c_nrm,
                                            accum_out=c_se), "act")
            p.wait("scalar", "act", ta)
            ta = p.inc(nc.scalar.activation(c_se, c_se, AF.Ln), "act")
            p.wait("vector", "act", ta)
            tv = p.inc(nc.vector.tensor_tensor(c_se, c_se, c_rm, ALU.add),
                       "dve")
            p.wait("vector", "dve", tv)
            tv = p.inc(nc.vector.tensor_scalar(Tb[:], Tb[:], c_se, None,
                                               ALU.subtract), "dve")
            p.wait("tensor", "dve", tv)
            for i in range(NT):
                tp = p.inc(nc.tensor.transpose(
                    pA[:, 2048 + i * 8: 2048 + (i + 1) * 8],
                    Tb[0:8, ts(i, 128)], B.identf[0:8, 0:8]), "pe")
            p.wait("vector", "pe", tp)
            tv = p.inc(nc.vector.tensor_copy(Ttok[:], pA[:, 2048:2176]), "dve")
            p.wait("vector", "dve", tv)
            t3 = Ttok[:].rearrange("p (g e) -> p g e", e=8)
            rm16 = B.scr_f1[:, 0:16]
            sm16 = B.scr_f1[:, 16:32]
            tv = p.inc(nc.vector.tensor_reduce(rm16, t3, AX.X, ALU.max), "dve")
            p.wait("vector", "dve", tv)
            bc = rm16.rearrange("p (g o) -> p g o", o=1).broadcast_to([128, 16, 8])
            tv = p.inc(nc.vector.tensor_tensor(
                B.scr_f2[:, 0:128].rearrange("p (g e) -> p g e", e=8), t3, bc,
                ALU.subtract), "dve")
            p.wait("scalar", "dve", tv)
            ta = p.inc(nc.scalar.activation(B.scr_f2[:, 128:256],
                                            B.scr_f2[:, 0:128], AF.Exp), "act")
            p.wait("vector", "act", ta)
            tv = p.inc(nc.vector.tensor_reduce(
                sm16, B.scr_f2[:, 128:256].rearrange("p (g e) -> p g e", e=8),
                AX.X, ALU.add), "dve")
            p.wait("scalar", "dve", tv)
            ta = p.inc(nc.scalar.activation(sm16, sm16, AF.Ln), "act")
            p.wait("vector", "act", ta)
            tv = p.inc(nc.vector.tensor_tensor(sm16, sm16, rm16, ALU.add),
                       "dve")
            p.wait("vector", "dve", tv)
            bc2 = sm16.rearrange("p (g o) -> p g o", o=1).broadcast_to([128, 16, 8])
            tv = p.inc(nc.vector.tensor_tensor(t3, t3, bc2, ALU.subtract),
                       "dve")
            p.wait("tensor", "dve", tv)
            p.wait("tensor", "dve", tv)
            for i in range(NT):
                tp = p.inc(nc.tensor.transpose(
                    pA[0:8, i * 128:(i + 1) * 128],
                    Ttok[:, i * 8:(i + 1) * 8], B.identf[:, 0:128]), "pe")
            p.wait("vector", "pe", tp)
            for cc in range(4):
                tv = p.inc(nc.vector.tensor_copy(
                    Tb[:, cc * 512:(cc + 1) * 512],
                    pA[0:8, cc * 512:(cc + 1) * 512]), "dve")
            p.wait("vector", "dve", tv)
            p.wait("scalar", "dve", tv)
            p.wait("tensor", "dve", tv)
        ta = p.inc(nc.scalar.activation(G8[:], Tb[:], AF.Exp), "act")
        p.wait("vector", "act", ta)
        p.wait("tensor", "act", ta)
        for i in range(NT):
            tp = p.inc(nc.tensor.transpose(
                pA[:, 2048 + i * 8: 2048 + (i + 1) * 8],
                G8[0:8, ts(i, 128)], B.identf[0:8, 0:8]), "pe")
        p.wait("vector", "pe", tp)
        tv = p.inc(nc.vector.tensor_copy(gtok[:], pA[:, 2048:2176]), "dve")
        p.wait("vector", "dve", tv)
        lo = B.colbuf[0:8, 114:115]
        hi = B.colbuf[0:8, 115:116]
        mid = B.colbuf[0:8, 116:117]
        cntc = B.colbuf[0:8, 117:118]
        prd = B.colbuf[0:8, 118:119]
        pm1 = B.colbuf[0:8, 119:120]
        t1c = B.colbuf[0:8, 120:121]
        tv = p.inc(nc.vector.memset(lo, 0.0), "dve")
        tv = p.inc(nc.vector.memset(hi, 1.125), "dve")
        p.wait("vector", "dve", tv)
        for bi in range(42):
            tv = p.inc(nc.vector.tensor_tensor(mid, lo, hi, ALU.add), "dve")
            p.wait("vector", "dve", tv)
            tv = p.inc(nc.vector.tensor_scalar(mid, mid, 0.5, None, ALU.mult),
                       "dve")
            p.wait("vector", "dve", tv)
            tv = p.inc(nc.vector.tensor_scalar(B.scr_bf[0:8, 0:2048], G8[:],
                                               mid, None, ALU.is_ge), "dve")
            p.wait("vector", "dve", tv)
            tv = p.inc(nc.vector.tensor_reduce(cntc, B.scr_bf[0:8, 0:2048],
                                               AX.X, ALU.add), "dve")
            p.wait("vector", "dve", tv)
            tv = p.inc(nc.vector.tensor_scalar(prd, cntc, float(M3), None,
                                               ALU.is_ge), "dve")
            p.wait("vector", "dve", tv)
            tv = p.inc(nc.vector.tensor_tensor(t1c, mid, lo, ALU.subtract),
                       "dve")
            p.wait("vector", "dve", tv)
            tv = p.inc(nc.vector.tensor_tensor(t1c, t1c, prd, ALU.mult), "dve")
            p.wait("vector", "dve", tv)
            tv = p.inc(nc.vector.tensor_tensor(lo, lo, t1c, ALU.add), "dve")
            p.wait("vector", "dve", tv)
            tv = p.inc(nc.vector.tensor_scalar(pm1, prd, -1.0, 1.0, ALU.mult,
                                               ALU.add), "dve")
            p.wait("vector", "dve", tv)
            tv = p.inc(nc.vector.tensor_tensor(t1c, mid, hi, ALU.subtract),
                       "dve")
            p.wait("vector", "dve", tv)
            tv = p.inc(nc.vector.tensor_tensor(t1c, t1c, pm1, ALU.mult), "dve")
            p.wait("vector", "dve", tv)
            tv = p.inc(nc.vector.tensor_tensor(hi, hi, t1c, ALU.add), "dve")
            p.wait("vector", "dve", tv)
        p.wait("tensor", "dve", tv)
        tp = p.inc(nc.tensor.transpose(pA[:, 2048:2056][0:1, :],
                                       lo, B.identf[0:8, 0:8]), "pe")
        p.wait("vector", "pe", tp)
        tv = p.inc(nc.vector.tensor_copy(B.scr_f2[0:1, 0:8],
                                         pA[0:1, 2048:2056]), "dve")
        p.wait("tensor", "dve", tv)
        tp = p.inc(nc.tensor.matmul(pA[:, 2048:2056], B.onesf[0:1, :],
                                    B.scr_f2[0:1, 0:8], start=True, stop=True),
                   "pe")
        p.wait("vector", "pe", tp)
        tv = p.inc(nc.vector.tensor_copy(taub[:], pA[:, 2048:2056]), "dve")
        p.wait("vector", "dve", tv)
        for i in range(NT):
            tv = p.inc(nc.vector.tensor_tensor(
                selb[:, i * 8:(i + 1) * 8], gtok[:, i * 8:(i + 1) * 8],
                taub[:], ALU.is_ge), "dve")
        tv = p.inc(nc.vector.tensor_scalar(hardb[:], gtok[:], 0.5, None,
                                           ALU.is_gt), "dve")
        p.wait("vector", "dve", tv)
        acc = B.scr_f1[:, 0:1]
        tcol = B.scr_f1[:, 1:2]
        for i in range(NT):
            tv = p.inc(nc.vector.memset(acc, 1.0), "dve")
            p.wait("vector", "dve", tv)
            for e in range(7, -1, -1):
                selc = selb[:, i * 8 + e: i * 8 + e + 1]
                tv = p.inc(nc.vector.tensor_tensor(
                    Gb[:, i * 8 + e: i * 8 + e + 1], selc, acc, ALU.mult),
                    "dve")
                p.wait("vector", "dve", tv)
                if e:
                    tv = p.inc(nc.vector.tensor_tensor(tcol, selc, acc,
                                                       ALU.mult), "dve")
                    p.wait("vector", "dve", tv)
                    tv = p.inc(nc.vector.tensor_tensor(acc, acc, tcol,
                                                       ALU.subtract), "dve")
                    p.wait("vector", "dve", tv)
        tv = p.inc(nc.vector.tensor_tensor(Gb[:], Gb[:], hardb[:], ALU.mult),
                   "dve")
        p.wait("vector", "dve", tv)
        tv = p.inc(nc.vector.tensor_copy(Gbf[:], Gb[:]), "dve")
        p.wait("tensor", "dve", tv)
        for i in range(NT):
            for j in range(i + 1):
                lhs = B.triu if j == i else B.ones
                tm = p.inc(nc.tensor.matmul(
                    pA[:, 2176:2184], lhs, Gbf[:, j * 8:(j + 1) * 8],
                    start=(j == 0), stop=(j == i)), "pe")
            p.wait("vector", "pe", tm)
            tv = p.inc(nc.vector.tensor_tensor(
                rankb[:, i * 8:(i + 1) * 8], pA[:, 2176:2184],
                Gb[:, i * 8:(i + 1) * 8], ALU.mult), "dve")
            p.wait("vector", "dve", tv)
            tv = p.inc(nc.vector.tensor_scalar(
                rankb[:, i * 8:(i + 1) * 8], rankb[:, i * 8:(i + 1) * 8],
                -1.0, None, ALU.add), "dve")
            p.wait("tensor", "dve", tv)
        p.wait("vector", "dve", tv)
        tok_sc = 0
        for e in range(NE):
            for i in range(NT):
                tv = p.inc(nc.vector.tensor_scalar(
                    PeT[:, i * 256:(i + 1) * 256], B.iota256[:],
                    rankb[:, i * 8 + e: i * 8 + e + 1], None, ALU.is_equal),
                    "dve")
            tok_pet = tv
            p.wait("tensor", "dve", tok_pet)
            for dt in range(ND):
                p.wait("tensor", "dve", B._l3_xn_tok)
                for i in range(NT):
                    tm = p.inc(nc.tensor.matmul(
                        pY[:, 0:256],
                        B.arena[:, i * 1024 + dt * 128: i * 1024 + (dt + 1) * 128],
                        PeT[:, i * 256:(i + 1) * 256],
                        start=(i == 0), stop=(i == NT - 1)), "pe")
                p.wait("vector", "pe", tm)
                tv = p.inc(nc.vector.tensor_copy(
                    ReT[:, dt * 256:(dt + 1) * 256], pY[:, 0:256]), "dve")
                p.wait("tensor", "dve", tv)
            for half in range(2):
                tok_we = stream_cast(p, B, d["w_l3_experts"][e], half * 512, 4,
                                     0, 1024, webuf, 0, None)
                p.dwait("tensor", tok_we)
                for rt in range(2):
                    prt = pY if rt == 0 else pA
                    for dth in range(4):
                        dt = half * 4 + dth
                        for oc in range(2):
                            tm = p.inc(nc.tensor.matmul(
                                prt[:, oc * 512:(oc + 1) * 512],
                                ReT[:, dt * 256 + rt * 128:
                                    dt * 256 + rt * 128 + 128],
                                webuf[:, dth * 1024 + oc * 512:
                                      dth * 1024 + (oc + 1) * 512],
                                start=(dt == 0), stop=(dt == ND - 1)), "pe")
            p.wait("vector", "pe", tm)
            for rt in range(2):
                prt = pY if rt == 0 else pA
                for oc in range(2):
                    tv = p.inc(nc.vector.tensor_copy(
                        Ye[:, rt * 1024 + oc * 512: rt * 1024 + (oc + 1) * 512],
                        prt[:, oc * 512:(oc + 1) * 512]), "dve")
            p.wait("tensor", "dve", tv)
            for i in range(NT):
                p.wait("tensor", "dve", tok_sc)
                for rt in range(2):
                    tp = p.inc(nc.tensor.transpose(
                        pB16[:, rt * 128:(rt + 1) * 128],
                        PeT[:, i * 256 + rt * 128: i * 256 + (rt + 1) * 128],
                        B.ident), "pe")
                p.wait("vector", "pe", tp)
                tv = p.inc(nc.vector.tensor_copy(PeL[:, 0:256].rearrange(
                    "p (a c) -> p a c", c=128), pB16[:, 0:256].rearrange(
                    "p (a c) -> p a c", c=128)), "dve")
                p.wait("tensor", "dve", tv)
                for oc in range(2):
                    for rt in range(2):
                        tm = p.inc(nc.tensor.matmul(
                            pY[:, oc * 512:(oc + 1) * 512],
                            PeL[:, rt * 128:(rt + 1) * 128],
                            Ye[:, rt * 1024 + oc * 512: rt * 1024 + (oc + 1) * 512],
                            start=(rt == 0), stop=(rt == 1)), "pe")
                p.wait("vector", "pe", tm)
                for oc in range(2):
                    xs = rs(i * D + oc * 512, i * D + (oc + 1) * 512)
                    tok_sc = p.inc(nc.vector.tensor_tensor(
                        B.X[:, xs], B.X[:, xs],
                        pY[:, oc * 512:(oc + 1) * 512], ALU.add), "dve")
                p.wait("tensor", "dve", tok_sc)
    p.barrier_into(("vector", "scalar", "tensor", "gpsimd", "sync"))


def emit_l3_attn(p, B):
    nc = p.nc
    d = B.d
    scale = float(1.0 / np.sqrt(128.0))
    with (
        nc.sbuf_tensor(uname("a3_qt"), [128, 2048], BF16) as QT,
        nc.sbuf_tensor(uname("a3_kt"), [128, 2048], BF16) as KT1,
        nc.sbuf_tensor(uname("a3_vtf"), [128, 2048], F32) as VTf,
        nc.sbuf_tensor(uname("a3_vtok"), [128, NT * 128], BF16) as vtok,
        nc.sbuf_tensor(uname("a3_attnT"), [128, 1024], BF16) as attnT,
        nc.sbuf_tensor(uname("a3_pet"), [128, 2048], BF16) as peT,
        nc.sbuf_tensor(uname("a3_pe1"), [128, 1], F32) as pe1c,
        nc.sbuf_tensor(uname("a3_wbuf"), [128, 2048], BF16) as wbuf,
        nc.psum_tensor([128, 2048], F32) as pS,
        nc.psum_tensor([128, 512], F32) as pO,
        nc.psum_tensor([128, 512], F32) as pSum,
    ):
        gB3 = p.dgrp()
        p.dma("sync", peT[:], d["c_l3_peT"], gB3)
        p.dma("sync", pe1c[:], d["c_l3_pe1"], gB3)
        p.dwait("vector", gB3)
        p.dwait("scalar", gB3)
        tok_w = stream_cast(p, B, d["w_l3_kT"], 0, 8, 0, 128, wbuf, 0, None)
        tok_w = stream_cast(p, B, d["w_l3_vT"], 0, 8, 0, 128, wbuf, 1024, None)
        p.dwait("tensor", tok_w)
        p.wait("tensor", "dve", B.tok_xnT)
        tv = 0
        for w in range(2):
            for cc in range(4):
                for dt in range(ND):
                    tm = p.inc(nc.tensor.matmul(
                        pS[:, cc * 512:(cc + 1) * 512],
                        wbuf[:, w * 1024 + dt * 128: w * 1024 + (dt + 1) * 128],
                        xnt(B, dt, rs(cc * 512, (cc + 1) * 512)),
                        start=(dt == 0), stop=(dt == ND - 1)), "pe")
                if w == 0:
                    p.wait("scalar", "pe", tm)
                    tv = p.inc(nc.scalar.activation(
                        KT1[:, cc * 512:(cc + 1) * 512],
                        pS[:, cc * 512:(cc + 1) * 512], AF.Identity,
                        bias=pe1c[:]), "act")
                else:
                    p.wait("vector", "pe", tm)
                    tv = p.inc(nc.vector.tensor_copy(
                        VTf[:, cc * 512:(cc + 1) * 512],
                        pS[:, cc * 512:(cc + 1) * 512]), "dve")
            if w == 0:
                p.wait("tensor", "act", tv)
            else:
                p.wait("tensor", "dve", tv)
        tok_kv = tv
        tvt = emit_vtok(p, B, VTf, vtok, pS, 1, 128, 128, tok_kv)
        p.wait("tensor", "dve", tvt)
        for h in range(8):
            tok_wq = stream_cast(p, B, d["w_l3_qT"], 0, 8, h * 128, 128, wbuf,
                                 0, None)
            p.dwait("tensor", tok_wq)
            p.wait("tensor", "dve", B._tok_ctx_evac)
            for cc in range(4):
                for dt in range(ND):
                    tm = p.inc(nc.tensor.matmul(
                        pS[:, cc * 512:(cc + 1) * 512],
                        wbuf[:, dt * 128:(dt + 1) * 128],
                        xnt(B, dt, rs(cc * 512, (cc + 1) * 512)),
                        start=(dt == 0), stop=(dt == ND - 1)), "pe")
                p.wait("vector", "pe", tm)
                tv = p.inc(nc.vector.tensor_tensor(
                    QT[:, cc * 512:(cc + 1) * 512],
                    pS[:, cc * 512:(cc + 1) * 512],
                    peT[:, cc * 512:(cc + 1) * 512], ALU.add), "dve")
            tok_q = tv
            p.wait("tensor", "dve", tok_q)
            tok_qkv_pe = p.cnt["pe"]
            tnb = emit_negB_full(p, B, QT, KT1, pS, scale, 100, tok_qkv_pe,
                                 tok_q)
            p.wait("scalar", "dve", tnb)
            p.wait("tensor", "dve", tnb)
            tok_exp = {}
            tok_av3 = {}
            for c in range(NC2):
                q0 = c * 512
                jmax = 4 * c + 3
                t_prev = B._tok_ctx_evac

                def emit_av3(j, par, qlo, tmask):
                    p.wait("tensor", "act", tok_exp[(par, j)])
                    if tmask:
                        p.wait("tensor", "dve", tmask)
                    ta = p.inc(nc.tensor.matmul(
                        pO[0:128, qlo:512],
                        vtok[:, j * 128:(j + 1) * 128],
                        attnT[:, par * 512 + qlo: par * 512 + 512],
                        start=(j == 0), stop=(j == jmax)), "pe")
                    ta = p.inc(nc.tensor.matmul(
                        pSum[0:1, qlo:512],
                        B.ones[:, 0:1],
                        attnT[:, par * 512 + qlo: par * 512 + 512],
                        start=(j == 0), stop=(j == jmax)), "pe")
                    tok_av3[(par, j)] = ta
                    return ta

                pend = None
                for j in range(jmax + 1):
                    par = j % 2
                    qlo = max(0, 128 * j - q0)
                    p.wait("tensor", "act", tok_exp.get((par, j - 2), 0))
                    if j == 0:
                        p.wait("tensor", "dve", t_prev)
                    diag = 128 * j >= q0
                    tm = p.inc(nc.tensor.matmul(
                        pS[:, par * 512 + qlo: par * 512 + 512],
                        KT1[:, ts(j, 128)],
                        QT[:, rs(q0 + qlo, q0 + 512)],
                        start=True, stop=not diag), "pe")
                    if diag:
                        tm = p.inc(nc.tensor.matmul(
                            pS[:, par * 512 + qlo: par * 512 + qlo + 128],
                            B.ident, B.negtri,
                            start=False, stop=True), "pe")
                    tok_s = tm
                    if pend is not None:
                        emit_av3(*pend)
                    p.wait("scalar", "pe", tok_s)
                    p.wait("scalar", "pe", tok_av3.get((par, j - 2), 0))
                    te = p.inc(nc.scalar.activation(
                        attnT[:, par * 512 + qlo: par * 512 + 512],
                        pS[:, par * 512 + qlo: par * 512 + 512],
                        AF.Exp, scale=scale,
                        bias=B.colbuf[:, 100:101]), "act")
                    tok_exp[(par, j)] = te
                    pend = (j, par, qlo, 0)
                ta = emit_av3(*pend)
                p.wait("vector", "pe", ta)
                tv = p.inc(nc.vector.reciprocal(B.scr_f2[0:1, 0:512],
                                                pSum[0:1, 0:512]), "dve")
                p.wait("vector", "dve", tv)
                tv = p.inc(nc.vector.tensor_copy(B.scr_f1[:, 0:512],
                                                 pO[:, 0:512]), "dve")
                p.wait("tensor", "dve", tv)
                p.wait("tensor", "act", tok_exp[(jmax % 2, jmax)])
                tb = p.inc(nc.tensor.matmul(
                    pS[:, 0:512], B.onesf[0:1, 0:128], B.scr_f2[0:1, 0:512],
                    start=True, stop=True), "pe")
                p.wait("vector", "pe", tb)
                tv = p.inc(nc.vector.tensor_tensor(
                    B.scr_f1[:, 0:512], B.scr_f1[:, 0:512], pS[:, 0:512],
                    ALU.mult), "dve")
                p.wait("vector", "dve", tv)
                tv = p.inc(nc.vector.tensor_copy(
                    B.arena[:, h * 2048 + c * 512: h * 2048 + (c + 1) * 512],
                    B.scr_f1[:, 0:512]), "dve")
                B._tok_ctx_evac = p.cnt["dve"]
        tok_x = 0
        for quarter in range(4):
            tok_wp = stream_cast(p, B, d["w_l3_oT"], quarter * 256, 2, 0,
                                 1024, wbuf, 0, None)
            p.dwait("tensor", tok_wp)
            p.wait("tensor", "dve", B._tok_ctx_evac)
            for i in range(NT):
                for oc in range(2):
                    for hpq in range(2):
                        hp = quarter * 2 + hpq
                        tm = p.inc(nc.tensor.matmul(
                            pS[:, oc * 512: oc * 512 + 512],
                            B.arena[:, hp * 2048 + i * 128: hp * 2048 + i * 128 + 128],
                            wbuf[:, hpq * 1024 + oc * 512: hpq * 1024 + (oc + 1) * 512],
                            start=(hpq == 0), stop=(hpq == 1)), "pe")
                    p.wait("vector", "pe", tm)
                    tok_x = p.inc(nc.vector.tensor_tensor(
                        B.X[:, i * D + oc * 512: i * D + (oc + 1) * 512],
                        B.X[:, i * D + oc * 512: i * D + (oc + 1) * 512],
                        pS[:, oc * 512: oc * 512 + 512], ALU.add), "dve")
                    p.wait("tensor", "dve", tok_x)
    B._tok_ctx_evac = 0
    p.barrier_into(("vector", "scalar", "tensor", "gpsimd", "sync"))



def _run(common, per_core, layer_groups, x0=None):
    x = x0
    for layers in layer_groups:
        nc = build_program(common, layers)
        in_maps = []
        for b in range(8):
            m = {k: v for k, v in common.items()}
            m["x_in"] = per_core[b]["x_in"] if x is None else x[b]
            in_maps.append(m)
        res = run_bass_kernel_spmd(nc, in_maps, core_ids=list(range(8)))
        x = [np.ascontiguousarray(res.results[b]["x_out"]) for b in range(8)]
    return np.stack(x, axis=0)


def kernel(**inputs):
    common, per_core = host_prep(inputs)
    out = _run(common, per_core, [[0, 1, 2, 3]])
    return out.astype(np.float32)


# version 18
# speedup vs baseline: 1.5017x; 1.1820x over previous
import sys
import numpy as np
import ml_dtypes

sys.path.insert(0, "/opt/trn_rl_repo")
import concourse.bass as bass
import concourse.mybir as mybir
from concourse.bass_utils import run_bass_kernel_spmd

F32 = mybir.dt.float32
BF16 = mybir.dt.bfloat16
I32 = mybir.dt.int32
AF = mybir.ActivationFunctionType
ALU = mybir.AluOpType
AX = mybir.AxisListType
bf16 = ml_dtypes.bfloat16

T = 2048
D = 1024
NT = T // 128
ND = D // 128
NC2 = T // 512
LN_EPS = 1e-5
DEBUG_SKIP = set()
_NAME_CTR = [0]


def uname(s):
    _NAME_CTR[0] += 1
    return "%s_%d" % (s, _NAME_CTR[0])
NDMA_SEMS = 48
STRICT_DMA = True
FFN_BAR = 0


class Prog:
    def __init__(self, nc, sems):
        self.nc = nc
        self.sems = sems
        self.cnt = {k: 0 for k in sems}
        self.hw = {}

    def eng(self, name):
        return getattr(self.nc, name)

    ENG_OF = {"dve": "vector", "act": "scalar", "gp": "gpsimd"}

    def inc(self, instr, sem, n=1):
        instr.then_inc(self.sems[sem], n)
        self.cnt[sem] += n
        eng = self.ENG_OF.get(sem)
        if eng is not None:
            self.wait(eng, sem, self.cnt[sem])
        return self.cnt[sem]

    def wait(self, eng, sem, tgt):
        if not tgt or tgt <= 0:
            return
        key = (eng, sem)
        if self.hw.get(key, -1) >= tgt:
            return
        self.eng(eng).wait_ge(self.sems[sem], tgt)
        self.hw[key] = tgt

    def barrier_into(self, engines):
        for c in engines:
            for s in self.cnt:
                self.wait(c, s, self.cnt[s])

    def dgrp(self, q="sync"):
        half = NDMA_SEMS // 2
        if q == "sync":
            g = self.rr % half
            self.rr += 1
        else:
            g = half + self.rrg % half
            self.rrg += 1
        return g

    def dma(self, q, out, in_, g):
        half = NDMA_SEMS // 2
        assert (g < half) == (q == "sync"), (q, g)
        instr = self.eng(q).dma_start(out=out, in_=in_)
        self.inc(instr, "d%d" % g, 16)
        return g

    def dwait(self, eng, g):
        self.wait(eng, "d%d" % g, self.cnt["d%d" % g])


def ts(i, n=128):
    return slice(i * n, (i + 1) * n)


def rs(a, b):
    return slice(a, b)



def host_prep(inputs):
    c = {}
    f32 = np.float32

    def asf(x):
        return np.ascontiguousarray(np.asarray(x), dtype=f32)

    c["c_ident"] = np.eye(128, dtype=np.float32).astype(bf16)
    c["c_identf"] = np.eye(128, dtype=np.float32)
    c["c_triu"] = np.triu(np.ones((128, 128), dtype=np.float32)).astype(bf16)
    c["c_negtri"] = (-240.0 * (1.0 - np.triu(np.ones((128, 128),
                     dtype=np.float32)))).astype(bf16)
    c["c_ones"] = np.ones((128, 128), dtype=np.float32).astype(bf16)
    c["c_onesf"] = np.ones((8, 128), dtype=np.float32)
    c["c_iota256"] = np.broadcast_to(np.arange(256, dtype=f32), (128, 256)).copy()
    dup = np.zeros((64, 128), dtype=np.float32)
    for j in range(128):
        dup[j % 64, j] = 1.0
    c["c_dup"] = dup.astype(bf16)

    c["w_l0_wq"] = asf(inputs["L0_attn_Wq"])
    c["w_l0_wk"] = asf(inputs["L0_attn_Wk"])
    c["w_l0_wv"] = asf(inputs["L0_attn_Wv"])
    for nm, key in (("w_l0_bq", "L0_attn_bq"), ("w_l0_bk", "L0_attn_bk"),
                    ("w_l0_bv", "L0_attn_bv")):
        c[nm] = np.ascontiguousarray(asf(inputs[key]).reshape(8, 128).T)
    c["b_l0_bo_row"] = asf(inputs["L0_attn_bo"]).reshape(1, 1024).astype(bf16)
    for nm, key in (("w_l0_aq", "L0_attn_alpha_q"), ("w_l0_ak", "L0_attn_alpha_k"),
                    ("w_l0_av", "L0_attn_alpha_v")):
        a = asf(inputs[key])
        ar = np.repeat(a.transpose(1, 2, 0), 64, axis=2)
        c[nm] = np.ascontiguousarray(
            ar.reshape(16, 2, 8, 128).transpose(3, 0, 1, 2).reshape(128, 256))
    e64 = np.eye(64, dtype=np.float32)
    c["c_p64"] = np.tile(np.concatenate([e64, e64], axis=1), (16, 1)).astype(bf16)
    col = asf(inputs["L0_attn_collapse"])
    c["w_l0_col"] = np.ascontiguousarray(
        np.repeat(col[:, :, None], 64, axis=2).transpose(2, 0, 1).reshape(64, 32))
    c["w_l0_woT"] = np.ascontiguousarray(asf(inputs["L0_attn_Wo"]).T)
    c["w_l0_gT"] = np.ascontiguousarray(np.asarray(inputs["L0_ffn_w_gate"]).T)
    c["w_l0_uT"] = np.ascontiguousarray(np.asarray(inputs["L0_ffn_w_up"]).T)
    c["w_l0_dT"] = np.ascontiguousarray(np.asarray(inputs["L0_ffn_w_down"]).T)

    c["w_l1_attnT"] = np.ascontiguousarray(asf(inputs["L1_attn_W_attn"]).T)
    c["w_l1_projT"] = np.ascontiguousarray(asf(inputs["L1_attn_W_proj"]).T)
    c["w_l1_win"] = asf(inputs["L1_ffn_Win"])
    c["w_l1_bin"] = asf(inputs["L1_ffn_bin_"]).reshape(64, 1)
    R = np.zeros((64, 1024), dtype=np.float32)
    for m in range(64):
        R[m, m * 16:(m + 1) * 16] = 1.0
    c["c_l1_R"] = R.astype(bf16)
    c["c_l1_w2mask"] = np.ascontiguousarray(R.T).astype(bf16)
    c["w_l1_w1col"] = np.ascontiguousarray(
        asf(inputs["L1_ffn_W1"]).reshape(8, 128).T)
    c["w_l1_b1col"] = np.ascontiguousarray(
        asf(inputs["L1_ffn_b1"]).reshape(8, 128).T)
    c["w_l1_w2col"] = np.ascontiguousarray(
        asf(inputs["L1_ffn_W2"]).reshape(8, 128).T)
    c["w_l1_b2"] = asf(inputs["L1_ffn_b2"]).reshape(64, 1)
    c["w_l1_wout"] = asf(inputs["L1_ffn_Wout"])
    c["b_l1_bout_row"] = asf(inputs["L1_ffn_bout"]).reshape(1, 1024).astype(bf16)

    c["w_l2_qT"] = np.ascontiguousarray(
        asf(inputs["L2_attn_WQ"]).transpose(2, 0, 1).reshape(1024, 1024))
    c["w_l2_kT"] = np.ascontiguousarray(
        asf(inputs["L2_attn_WK"]).transpose(2, 0, 1).reshape(1024, 1024))
    c["w_l2_vT"] = np.ascontiguousarray(
        asf(inputs["L2_attn_WV"]).transpose(2, 0, 1).reshape(1024, 1024))
    c["w_l2_oT"] = np.ascontiguousarray(
        asf(inputs["L2_attn_WO"]).transpose(0, 2, 1).reshape(1024, 1024))
    c["w_l2_w1"] = asf(inputs["L2_ffn_w1"][0])
    c["w_l2_w2"] = asf(inputs["L2_ffn_w2"][0])
    c["w_l2_b1col"] = np.ascontiguousarray(
        asf(inputs["L2_ffn_b1"][0]).reshape(32, 128).T)
    c["b_l2_b2_row"] = asf(inputs["L2_ffn_b2"][0]).reshape(1, 1024).astype(bf16)
    c["b_l2_gamma_row"] = asf(inputs["L2_ffn_gamma"]).reshape(1, 1024).astype(bf16)
    c["b_l2_beta_row"] = asf(inputs["L2_ffn_beta"]).reshape(1, 1024).astype(bf16)

    c["w_l3_qT"] = np.ascontiguousarray(asf(inputs["L3_attn_Wq"]).T)
    c["w_l3_kT"] = np.ascontiguousarray(asf(inputs["L3_attn_Wk"]).T)
    c["w_l3_vT"] = np.ascontiguousarray(asf(inputs["L3_attn_Wv"]).T)
    c["w_l3_oT"] = np.ascontiguousarray(asf(inputs["L3_attn_Wo"]).T)
    c["w_l3_gate"] = asf(inputs["L3_ffn_gate_w"][0])
    c["w_l3_experts"] = asf(inputs["L3_ffn_experts"])

    def sinus(length, dim):
        pos = np.arange(length, dtype=np.float64)[:, None]
        inv = np.exp(-np.log(10000.0) *
                     (np.arange(0, dim, 2, dtype=np.float64) / dim))
        ang = pos * inv[None, :]
        pe = np.zeros((length, dim), dtype=np.float64)
        pe[:, 0::2] = np.sin(ang)
        pe[:, 1::2] = np.cos(ang)
        return pe.astype(np.float32)
    c["c_l3_peT"] = np.ascontiguousarray(sinus(T, 128).T).astype(bf16)
    c["c_l3_pe1"] = sinus(1, 128).reshape(128, 1)

    x = np.ascontiguousarray(np.asarray(inputs["L0_attn_hidden_states"]),
                             dtype=f32)
    per_core = [{"x_in": x[b]} for b in range(8)]
    return c, per_core



class Bufs:
    pass


NEEDED_PREFIX = {0: ("w_l0",), 1: ("w_l1",), 2: ("w_l2",), 3: ("w_l3",)}


def declare_io(nc, common, layers):
    d = {}
    need = set()
    for k in common:
        if k.startswith(("c_", "b_")):
            need.add(k)
        for l in layers:
            for pre in NEEDED_PREFIX[l]:
                if k.startswith(pre):
                    need.add(k)
    for k in sorted(need):
        a = common[k]
        dt = {np.dtype(np.float32): F32, np.dtype(np.int32): I32,
              np.dtype(bf16): BF16}[a.dtype]
        d[k] = nc.dram_tensor(k, list(a.shape), dt, kind="ExternalInput").ap()
    d["x_in"] = nc.dram_tensor("x_in", [T, D], F32, kind="ExternalInput").ap()
    d["x_out"] = nc.dram_tensor("x_out", [T, D], F32, kind="ExternalOutput").ap()
    if 0 in layers:
        d["l0_wmix"] = nc.dram_tensor("l0_wmix", [48, 1024, 128], BF16,
                                      kind="Internal").ap()
    return d


def build_program(common, layers):
    import contextlib
    nc = bass.Bass("TRN2", target_bir_lowering=False)
    d = declare_io(nc, common, layers)
    _stk = contextlib.ExitStack()
    _dsems = [_stk.enter_context(nc.semaphore("d%d" % i))
              for i in range(NDMA_SEMS)]

    with (
        nc.sbuf_tensor(uname("X"), [128, NT * D], F32) as X,
        nc.sbuf_tensor(uname("xnT"), [128, ND * T], BF16) as xnT,
        nc.sbuf_tensor(uname("arena"), [128, 24576], BF16) as arena,
        nc.sbuf_tensor(uname("stats"), [128, 64], F32) as stats,
        nc.sbuf_tensor(uname("consts"), [128, 640], BF16) as consts,
        nc.sbuf_tensor(uname("identf"), [128, 128], F32) as identf,
        nc.sbuf_tensor(uname("onesf"), [8, 128], F32) as onesf,
        nc.sbuf_tensor(uname("iota256"), [128, 256], F32) as iota256,
        nc.sbuf_tensor(uname("colbuf"), [128, 128], F32) as colbuf,
        nc.sbuf_tensor(uname("scr_bf"), [128, 2048], BF16) as scr_bf,
        nc.sbuf_tensor(uname("scr_f1"), [128, 1024], F32) as scr_f1,
        nc.sbuf_tensor(uname("scr_f2"), [128, 1024], F32) as scr_f2,
        nc.semaphore(uname("dW")) as _s0,
        nc.semaphore(uname("dX")) as _s1,
        nc.semaphore(uname("pe")) as _s2,
        nc.semaphore(uname("act")) as _s3,
        nc.semaphore(uname("dve")) as _s4,
        nc.semaphore(uname("gp")) as _s5,
        nc.Block() as block,
    ):
        sems = {"dW": _s0, "dX": _s1, "pe": _s2, "act": _s3,
                "dve": _s4, "gp": _s5}
        for gi, h in enumerate(_dsems):
            sems["d%d" % gi] = h
        p = Prog(nc, sems)
        p.rr = 0
        p.rrg = 0
        B = Bufs()
        B.X, B.xnT, B.arena, B.stats = X, xnT, arena, stats
        B.consts, B.iota256, B.colbuf, B.onesf = consts, iota256, colbuf, onesf
        B.identf = identf
        B.scr_bf, B.scr_f1, B.scr_f2 = scr_bf, scr_f1, scr_f2
        B.d = d
        B.ident = consts[:, 0:128]
        B.triu = consts[:, 128:256]
        B.ones = consts[:, 256:384]
        B.dup = consts[0:64, 384:512]
        B.negtri = consts[:, 512:640]
        B._tok_ctx_evac = 0
        B._stage_toks = {}
        B._stage_grps = {}
        B._stream_grps = {}

        B.gC = p.dgrp()
        for nm, dst in (("c_ident", B.ident), ("c_triu", B.triu),
                        ("c_ones", B.ones), ("c_dup", B.dup),
                        ("c_identf", identf[:]), ("c_onesf", onesf[:]),
                        ("c_negtri", B.negtri),
                        ("c_iota256", iota256[:])):
            p.dma("sync", dst, d[nm], B.gC)
        B.epscol = colbuf[:, 127:128]
        te = p.inc(nc.vector.memset(B.epscol, LN_EPS), "dve")
        B.tok_eps = te
        B.gX = p.dgrp()
        xr = d["x_in"].rearrange("(n p) d -> n p d", p=128)
        for i in range(NT):
            p.dma("sync", X[:, ts(i, D)], xr[i], B.gX)

        for li in layers:
            p.barrier_into(("vector", "scalar", "tensor", "gpsimd", "sync"))
            emit_layernorm(p, B)
            if li == 0:
                if "l0ffn" not in DEBUG_SKIP:
                    emit_l0_ffn(p, B)
                if "l0attn" not in DEBUG_SKIP:
                    emit_l0_attn(p, B)
            elif li == 1:
                emit_l1_ffn(p, B)
                emit_l1_attn(p, B)
            elif li == 2:
                emit_l2_ffn(p, B)
                emit_l2_attn(p, B)
            else:
                emit_l3_ffn(p, B)
                emit_l3_attn(p, B)

        p.barrier_into(("sync",))
        gOut = p.dgrp()
        xo = d["x_out"].rearrange("(n p) d -> n p d", p=128)
        for i in range(NT):
            p.dma("sync", xo[i], X[:, ts(i, D)], gOut)
        p.dwait("sync", gOut)
    _stk.close()
    return nc



def emit_layernorm(p, B):
    nc = p.nc
    X, stats = B.X, B.stats
    p.dwait("scalar", B.gX)
    p.dwait("scalar", B.gC)
    p.dwait("tensor", B.gC)
    p.dwait("vector", B.gC)
    mean, rstd = stats[:, 0:16], stats[:, 16:32]
    sums, negmean = stats[:, 32:48], stats[:, 48:64]
    for i in range(NT):
        t = p.inc(nc.scalar.activation(B.scr_bf[:, 0:1024], X[:, ts(i, D)],
                                       AF.Identity, accum_out=sums[:, i:i + 1]),
                  "act")
        p.wait("scalar", "act", t)
    p.wait("vector", "act", t)
    t = p.inc(nc.vector.tensor_scalar(mean, sums, 1.0 / D, None, ALU.mult), "dve")
    p.wait("vector", "dve", t)
    t = p.inc(nc.vector.tensor_scalar(negmean, mean, -1.0, None, ALU.mult), "dve")
    p.wait("scalar", "dve", t)
    for i in range(NT):
        t = p.inc(nc.scalar.activation(B.scr_bf[:, 0:1024], X[:, ts(i, D)],
                                       AF.Square, bias=negmean[:, i:i + 1],
                                       accum_out=sums[:, i:i + 1]), "act")
        p.wait("scalar", "act", t)
    p.wait("vector", "act", t)
    t = p.inc(nc.vector.tensor_scalar(rstd, sums, 1.0 / D, None, ALU.mult), "dve")
    p.wait("scalar", "dve", t)
    p.wait("scalar", "dve", B.tok_eps)
    t = p.inc(nc.scalar.activation(rstd, rstd, AF.Ln, bias=B.epscol), "act")
    p.wait("scalar", "act", t)
    t = p.inc(nc.scalar.activation(rstd, rstd, AF.Exp, scale=-0.5), "act")
    B.tok_lnstats = t
    p.wait("vector", "act", t)

    with nc.psum_tensor([128, 1024], BF16) as pt:
        tv2 = 0
        for i in range(NT):
            tv = p.inc(nc.vector.tensor_scalar(
                B.scr_bf[:, 0:1024], X[:, ts(i, D)], mean[:, i:i + 1],
                rstd[:, i:i + 1], ALU.subtract, ALU.mult), "dve")
            p.wait("tensor", "dve", tv)
            for j in range(ND):
                tp = p.inc(nc.tensor.transpose(pt[:, ts(j, 128)],
                                               B.scr_bf[:, ts(j, 128)],
                                               B.ident), "pe")
            p.wait("vector", "pe", tp)
            dst = B.xnT[:].rearrange("p (j t) -> p j t", t=T)[:, :, ts(i, 128)]
            src = pt[:].rearrange("p (j t) -> p j t", t=128)
            tv2 = p.inc(nc.vector.tensor_copy(dst, src), "dve")
            p.wait("tensor", "dve", tv2)
    B.tok_xnT = tv2


def xnt(B, dtile, cols):
    return B.xnT[:, dtile * T + cols.start: dtile * T + cols.stop]



def stream_cast(p, B, dram, rows0, nrt, cols0, ncols, dst, dst_off, staging,
                war_pe=None):
    key = (id(dst), dst_off)
    grp = B._stream_grps.get(key)
    if grp is None:
        grp = p.dgrp("gpsimd")
        B._stream_grps[key] = grp
    war = p.cnt["pe"] if war_pe is None else war_pe
    p.wait("gpsimd", "pe", war)
    p.wait("sync", "pe", war)
    for j in range(nrt):
        src = dram[rows0 + j * 128: rows0 + (j + 1) * 128, cols0:cols0 + ncols]
        p.dma("gpsimd", dst[:, dst_off + j * ncols: dst_off + (j + 1) * ncols],
              src, grp)
    return grp



def emit_negB_pair(p, B, QT, KT, ps, scale, col_q, tok_ready_pe, tok_ready_dve):
    nc = p.nc
    tdv = 0
    for h in range(2):
        hs = ts(h, 64)
        maxs = []
        tm_prev = tok_ready_pe
        for w, MT in ((0, QT), (1, KT)):
            p.wait("scalar", "pe", tm_prev)
            p.wait("scalar", "dve", max(tok_ready_dve, tdv))
            ta = p.inc(nc.scalar.activation(B.scr_bf[0:64, 0:2048], MT[hs, :],
                                            AF.Square), "act")
            p.wait("tensor", "act", ta)
            p.wait("tensor", "dve", tdv)
            for cc in range(4):
                tm = p.inc(nc.tensor.matmul(
                    ps[:, cc * 512:(cc + 1) * 512],
                    B.ones[0:64, :],
                    B.scr_bf[0:64, cc * 512:(cc + 1) * 512],
                    start=True, stop=True), "pe")
            tm_prev = tm
            p.wait("vector", "pe", tm)
            for cc in range(4):
                tdv = p.inc(nc.vector.tensor_reduce(
                    B.colbuf[:, 122 + cc: 123 + cc],
                    ps[:, cc * 512:(cc + 1) * 512], AX.X, ALU.max), "dve")
            p.wait("vector", "dve", tdv)
            cdst = B.colbuf[:, 120 + w: 121 + w]
            tdv = p.inc(nc.vector.tensor_reduce(cdst, B.colbuf[:, 122:126],
                                                AX.X, ALU.max), "dve")
            p.wait("vector", "dve", tdv)
            if getattr(B, "dbg_negb", None):
                B.dbg_negb(p, B, h, w, ps)
        c0 = B.colbuf[:, 120:121]
        c1 = B.colbuf[:, 121:122]
        tdv = p.inc(nc.vector.tensor_tensor(c0, c0, c1, ALU.mult), "dve")
        p.wait("scalar", "dve", tdv)
        tac = p.inc(nc.scalar.activation(c0, c0, AF.Ln), "act")
        p.wait("scalar", "act", tac)
        tac = p.inc(nc.scalar.activation(c0, c0, AF.Exp, scale=0.5), "act")
        p.wait("vector", "act", tac)
        tdv = p.inc(nc.vector.tensor_scalar(
            B.colbuf[:, col_q + h: col_q + h + 1], c0, -scale, None, ALU.mult),
            "dve")
    return tdv


def emit_vtok(p, B, VTf, vtok, pOf, nheads, vdim, width, tok_src_dve):
    nc = p.nc
    tdv = 0
    for h in range(nheads):
        p.wait("tensor", "dve", max(tok_src_dve, tdv))
        for j in range(NT):
            tp = p.inc(nc.tensor.transpose(
                pOf[:, j * vdim:(j + 1) * vdim],
                VTf[h * vdim:(h + 1) * vdim, ts(j, 128)],
                B.identf[h * vdim: h * vdim + vdim,
                         h * vdim: h * vdim + vdim]), "pe")
        p.wait("vector", "pe", tp)
        if getattr(B, "dbg_vtok", None):
            B.dbg_vtok(p, B, h, pOf)
        base = h * NT * width
        nj_half = (512 // vdim)
        for bk in range((NT * vdim) // 512):
            dst = vtok[:, base + bk * nj_half * width:
                       base + (bk + 1) * nj_half * width].rearrange(
                "p (j c) -> p j c", c=width)[:, :, 0:vdim]
            srcb = pOf[:, bk * 512:(bk + 1) * 512].rearrange(
                "p (j c) -> p j c", c=vdim)
            tdv = p.inc(nc.vector.tensor_copy(dst, srcb), "dve")
        if width > vdim:
            ocol = vtok[:, base: base + NT * width].rearrange(
                "p (j c) -> p j c", c=width)[:, :, vdim:vdim + 1]
            tdv = p.inc(nc.vector.memset(ocol, 1.0), "dve")
    return tdv


def attn_chunks_pair(p, B, QT, KT, vtok, attnT, pS, pO, col_q, scale,
                     ctx_store, vwidth=66):
    nc = p.nc
    tok_exp = {}
    tok_av = {}
    for c in range(NC2):
        q0 = c * 512
        jmax = 4 * c + 3
        t_prev_ep = B._tok_ctx_evac

        def emit_av(j, par, qlo, tmask):
            p.wait("tensor", "act", tok_exp[(par, j)])
            if tmask:
                p.wait("tensor", "dve", tmask)
            ta = 0
            for h in range(2):
                for strip in range(2):
                    ta = p.inc(nc.tensor.matmul(
                        pO[0:65, (h * 2 + strip) * 512 + qlo:
                           (h * 2 + strip) * 512 + 512],
                        vtok[ts(strip, 64),
                             h * NT * vwidth + j * vwidth:
                             h * NT * vwidth + j * vwidth + 65],
                        attnT[ts(strip, 64), (par * 2 + h) * 512 + qlo:
                              (par * 2 + h) * 512 + 512],
                        start=(j == 0), stop=(j == jmax),
                        tile_position=(strip * 64, 0)), "pe")
            tok_av[(par, j)] = ta
            return ta

        pend = None
        for j in range(jmax + 1):
            par = j % 2
            qlo = max(0, 128 * j - q0)
            p.wait("tensor", "act", tok_exp.get((par, j - 2), 0))
            if j == 0:
                p.wait("tensor", "dve", t_prev_ep)
            diag = 128 * j >= q0
            for h in range(2):
                tm = p.inc(nc.tensor.matmul(
                    pS[:, (par * 2 + h) * 512 + qlo: (par * 2 + h) * 512 + 512],
                    KT[ts(h, 64), ts(j, 128)],
                    QT[ts(h, 64), rs(q0 + qlo, q0 + 512)],
                    start=True, stop=not diag,
                    tile_position=(h * 64, 0)), "pe")
                if diag:
                    tm = p.inc(nc.tensor.matmul(
                        pS[:, (par * 2 + h) * 512 + qlo:
                           (par * 2 + h) * 512 + qlo + 128],
                        B.ident, B.negtri,
                        start=False, stop=True,
                        tile_position=(0, 0)), "pe")
            tok_s = tm
            if pend is not None:
                emit_av(*pend)
            p.wait("scalar", "pe", tok_s)
            p.wait("scalar", "pe", tok_av.get((par, j - 2), 0))
            pin = pS[:, par * 1024: par * 1024 + 1024].rearrange(
                "p (h q) -> p h q", q=512)[:, :, qlo:512]
            pout = attnT[:, par * 1024: par * 1024 + 1024].rearrange(
                "p (h q) -> p h q", q=512)[:, :, qlo:512]
            te = p.inc(nc.scalar.activation(
                pout, pin, AF.Exp, scale=scale,
                bias=B.colbuf[:, col_q: col_q + 1]), "act")
            tok_exp[(par, j)] = te
            pend = (j, par, qlo, 0)
        ta = emit_av(*pend)
        p.wait("vector", "pe", ta)
        for h in range(2):
            tv = p.inc(nc.vector.tensor_copy(
                B.scr_f1[0:65, 0:512],
                pO[0:65, (h * 2) * 512:(h * 2) * 512 + 512]), "dve")
            p.wait("vector", "dve", tv)
            tv = p.inc(nc.vector.tensor_tensor(
                B.scr_f1[0:65, 0:512], B.scr_f1[0:65, 0:512],
                pO[0:65, (h * 2 + 1) * 512:(h * 2 + 1) * 512 + 512], ALU.add),
                "dve")
            p.wait("vector", "dve", tv)
            tv = p.inc(nc.vector.reciprocal(B.scr_f2[0:1, 0:512],
                                            B.scr_f1[64:65, 0:512]), "dve")
            p.wait("tensor", "dve", tv)
            p.wait("tensor", "act", tok_exp[(jmax % 2, jmax)])
            tb = p.inc(nc.tensor.matmul(
                pS[0:64, 0:512], B.onesf[0:1, 0:64], B.scr_f2[0:1, 0:512],
                start=True, stop=True), "pe")
            p.wait("vector", "pe", tb)
            tv = p.inc(nc.vector.tensor_tensor(
                B.scr_f1[0:64, 0:512], B.scr_f1[0:64, 0:512], pS[0:64, 0:512],
                ALU.mult), "dve")
            p.wait("vector", "dve", tv)
            ctx_store(h, c)
            p.wait("vector", "dve", p.cnt["dve"])
        B._tok_ctx_evac = p.cnt["dve"]



def emit_l1_attn(p, B):
    emit_std_attn(p, B,
                  qkv=[("w_l1_attnT", lambda hp: hp * 128),
                       ("w_l1_attnT", lambda hp: 1024 + hp * 128),
                       ("w_l1_attnT", lambda hp: 2048 + hp * 128)],
                  wo_key="w_l1_projT")


def emit_l2_attn(p, B):
    emit_std_attn(p, B,
                  qkv=[("w_l2_qT", lambda hp: hp * 128),
                       ("w_l2_kT", lambda hp: hp * 128),
                       ("w_l2_vT", lambda hp: hp * 128)],
                  wo_key="w_l2_oT")


def emit_std_attn(p, B, qkv, wo_key):
    nc = p.nc
    d = B.d
    scale = 0.125
    with (
        nc.sbuf_tensor(uname("sa_qt"), [128, 2048], BF16) as QT,
        nc.sbuf_tensor(uname("sa_kt"), [128, 2048], BF16) as KT,
        nc.sbuf_tensor(uname("sa_vtf"), [128, 2048], F32) as VTf,
        nc.sbuf_tensor(uname("sa_vtok"), [128, 2 * NT * 66], BF16) as vtok,
        nc.sbuf_tensor(uname("sa_attnT"), [128, 2048], BF16) as attnT,
        nc.sbuf_tensor(uname("sa_wbuf"), [128, 3 * 1024], BF16) as wbuf,
        nc.psum_tensor([128, 2048], F32) as pS,
        nc.psum_tensor([128, 2048], F32) as pO,
    ):
        tok_proj_prev = 0
        for hp in range(8):
            grps = []
            for w in range(3):
                key, coloff = qkv[w]
                grps.append(stream_cast(p, B, d[key], 0, 8, coloff(hp), 128,
                                        wbuf, w * 1024, None,
                                        war_pe=tok_proj_prev))
            for gw in grps:
                p.dwait("tensor", gw)
            p.wait("tensor", "dve", B._tok_ctx_evac)
            tv = 0
            for w in range(3):
                for cc in range(4):
                    for dt in range(ND):
                        tm = p.inc(nc.tensor.matmul(
                            pS[:, cc * 512:(cc + 1) * 512],
                            wbuf[:, w * 1024 + dt * 128: w * 1024 + (dt + 1) * 128],
                            xnt(B, dt, rs(cc * 512, (cc + 1) * 512)),
                            start=(dt == 0), stop=(dt == ND - 1)), "pe")
                    p.wait("vector", "pe", tm)
                    if w < 2:
                        dstT = QT if w == 0 else KT
                        tv = p.inc(nc.vector.tensor_copy(
                            dstT[:, cc * 512:(cc + 1) * 512],
                            pS[:, cc * 512:(cc + 1) * 512]), "dve")
                    else:
                        tv = p.inc(nc.vector.tensor_copy(
                            VTf[:, cc * 512:(cc + 1) * 512],
                            pS[:, cc * 512:(cc + 1) * 512]), "dve")
                p.wait("tensor", "dve", tv)
            tok_qkv_pe = p.cnt["pe"]
            tok_proj_prev = tok_qkv_pe
            tok_qkv = tv
            tvt = emit_vtok(p, B, VTf, vtok, pO, 2, 64, 66, tok_qkv)
            tnb = emit_negB_pair(p, B, QT, KT, pS, scale, 60, tok_qkv_pe,
                                 tok_qkv)
            p.wait("vector", "dve", tnb)
            tnb = p.inc(nc.vector.tensor_tensor(
                B.colbuf[:, 62:63], B.colbuf[:, 60:61], B.colbuf[:, 61:62],
                ALU.min), "dve")
            p.wait("vector", "dve", tnb)
            p.wait("tensor", "dve", max(tnb, tvt))
            p.wait("scalar", "dve", max(tnb, tvt))

            def store(h, c, hp=hp):
                dst = B.arena[ts(h, 64),
                              hp * 2048 + c * 512: hp * 2048 + (c + 1) * 512]
                p.inc(nc.vector.tensor_copy(dst, B.scr_f1[0:64, 0:512]), "dve")
            if getattr(B, "dbg_attn_pre", None):
                B.dbg_attn_pre(p, B, hp, QT, KT, vtok)
            attn_chunks_pair(p, B, QT, KT, vtok, attnT, pS, pO, 62, scale,
                             store)
            if getattr(B, "dbg_attn_post", None):
                B.dbg_attn_post(p, B, hp)

        tok_x = {}
        tok_wq_prev = p.cnt["pe"]
        for quarter in range(4):
            tok_wp = stream_cast(p, B, d[wo_key], quarter * 256, 2, 0,
                                 1024, wbuf, 0, None, war_pe=tok_wq_prev)
            p.dwait("tensor", tok_wp)
            p.wait("tensor", "dve", B._tok_ctx_evac)
            for i in range(NT):
                par = i % 2
                p.wait("tensor", "dve", tok_x.get((quarter, i - 2), 0))
                for oc in range(2):
                    for hpq in range(2):
                        hp = quarter * 2 + hpq
                        tm = p.inc(nc.tensor.matmul(
                            pS[:, par * 1024 + oc * 512:
                               par * 1024 + oc * 512 + 512],
                            B.arena[:, hp * 2048 + i * 128: hp * 2048 + i * 128 + 128],
                            wbuf[:, hpq * 1024 + oc * 512: hpq * 1024 + (oc + 1) * 512],
                            start=(hpq == 0), stop=(hpq == 1)), "pe")
                p.wait("vector", "pe", tm)
                for oc in range(2):
                    t_ = p.inc(nc.vector.tensor_tensor(
                        B.X[:, i * D + oc * 512: i * D + (oc + 1) * 512],
                        B.X[:, i * D + oc * 512: i * D + (oc + 1) * 512],
                        pS[:, par * 1024 + oc * 512: par * 1024 + (oc + 1) * 512],
                        ALU.add), "dve")
                tok_x[(quarter, i)] = t_
            tok_wq_prev = p.cnt["pe"]
            p.wait("tensor", "dve", t_)
    B._tok_ctx_evac = 0
    p.barrier_into(("vector", "scalar", "tensor", "gpsimd", "sync"))


def emit_l1_ffn(p, B):
    nc = p.nc
    d = B.d
    with (
        nc.sbuf_tensor(uname("f1_w"), [128, 1536], BF16) as wsm,
        nc.sbuf_tensor(uname("f1_wout"), [64, 1024], BF16) as wout,
        nc.sbuf_tensor(uname("f1_cols"), [128, 16], F32) as cols,
        nc.sbuf_tensor(uname("f1_rows"), [1, 1024], BF16) as brow,
        nc.sbuf_tensor(uname("f1_st"), [64, 2048], BF16) as sT,
        nc.sbuf_tensor(uname("f1_gt"), [64, 2048], BF16) as gT,
        nc.psum_tensor([128, 2048], F32) as ps,
        nc.psum_tensor([128, 512], F32) as ps2,
        nc.psum_tensor([128, 1024], F32) as ps3,
    ):
        tok_w = stream_cast(p, B, d["w_l1_win"], 0, 8, 0, 64, wsm, 0, None)
        gF = p.dgrp()
        p.dma("sync", wsm[0:64, 512:1536], d["c_l1_R"], gF)
        p.wait("sync", "act", p.cnt["act"])
        p.dma("sync", B.scr_bf[:, 0:512].rearrange("p (j c) -> p j c", c=64),
              d["c_l1_w2mask"].rearrange("(j p) c -> p j c", p=128), gF)
        p.dma("sync", cols[:, 0:8], d["w_l1_w1col"], gF)
        p.dma("sync", cols[:, 8:16], d["w_l1_b1col"], gF)
        p.dma("sync", B.colbuf[:, 96:104], d["w_l1_w2col"], gF)
        p.dma("sync", B.colbuf[0:64, 104:105], d["w_l1_bin"], gF)
        p.dma("sync", B.colbuf[0:64, 105:106], d["w_l1_b2"], gF)
        p.dma("sync", brow[:], d["b_l1_bout_row"], gF)
        gW2 = p.dgrp("gpsimd")
        p.dma("gpsimd", wout[:], d["w_l1_wout"], gW2)
        p.dwait("vector", gF)
        w2b = B.arena[:, 23 * 1024: 23 * 1024 + 512]
        for j in range(8):
            tbd = p.inc(nc.vector.tensor_scalar(
                w2b[:, j * 64:(j + 1) * 64],
                B.scr_bf[:, j * 64:(j + 1) * 64],
                B.colbuf[:, 96 + j: 97 + j], None, ALU.mult), "dve")

        p.dwait("tensor", tok_w)
        p.wait("tensor", "dve", B.tok_xnT)
        for cc in range(4):
            for dt in range(ND):
                tm = p.inc(nc.tensor.matmul(
                    ps[0:64, cc * 512:(cc + 1) * 512],
                    wsm[:, dt * 64:(dt + 1) * 64],
                    xnt(B, dt, rs(cc * 512, (cc + 1) * 512)),
                    start=(dt == 0), stop=(dt == ND - 1)), "pe")
        p.wait("scalar", "pe", tm)
        p.dwait("scalar", gF)
        tst = p.inc(nc.scalar.activation(sT[:], ps[0:64, 0:2048], AF.Identity,
                                         bias=B.colbuf[0:64, 104:105]), "act")
        p.wait("tensor", "act", tst)
        p.dwait("tensor", gF)
        p.wait("tensor", "dve", tbd)
        tg = 0
        th = 0
        for cc in range(4):
            csl = rs(cc * 512, (cc + 1) * 512)
            for j in range(8):
                p.wait("tensor", "dve", th)
                tmr = p.inc(nc.tensor.matmul(
                    ps2[:, 0:512],
                    wsm[0:64, 512 + j * 128: 512 + (j + 1) * 128],
                    sT[0:64, csl], start=True, stop=True), "pe")
                p.wait("vector", "pe", tmr)
                p.wait("vector", "pe", tg)
                th = p.inc(nc.vector.tensor_scalar(
                    B.scr_bf[:, 512:1024], ps2[:, 0:512], cols[:, j:j + 1],
                    cols[:, 8 + j: 8 + j + 1], ALU.mult, ALU.add), "dve")
                p.wait("vector", "dve", th)
                th = p.inc(nc.vector.tensor_scalar(
                    B.scr_bf[:, 512:1024], B.scr_bf[:, 512:1024], 0.0, None,
                    ALU.max), "dve")
                p.wait("tensor", "dve", th)
                tg = p.inc(nc.tensor.matmul(
                    ps[0:64, csl],
                    w2b[:, j * 64:(j + 1) * 64],
                    B.scr_bf[:, 512:1024],
                    start=(j == 0), stop=(j == 7)), "pe")
        p.wait("scalar", "pe", tg)
        tgt = p.inc(nc.scalar.activation(gT[:], ps[0:64, 0:2048], AF.Identity,
                                         bias=B.colbuf[0:64, 105:106]), "act")
        p.wait("tensor", "act", tgt)
        p.dwait("tensor", gW2)
        p.dwait("tensor", gF)
        tx = 0
        for i in range(NT):
            p.wait("tensor", "dve", tx)
            for oc in range(2):
                osl = rs(oc * 512, (oc + 1) * 512)
                tm = p.inc(nc.tensor.matmul(
                    ps3[:, osl], B.ones[0:1, :], brow[0:1, osl],
                    start=True, stop=False), "pe")
                tm = p.inc(nc.tensor.matmul(
                    ps3[:, osl], gT[0:64, ts(i, 128)], wout[0:64, osl],
                    start=False, stop=True), "pe")
            p.wait("vector", "pe", tm)
            for oc in range(2):
                tx = p.inc(nc.vector.tensor_tensor(
                    B.X[:, i * D + oc * 512: i * D + (oc + 1) * 512],
                    B.X[:, i * D + oc * 512: i * D + (oc + 1) * 512],
                    ps3[:, oc * 512:(oc + 1) * 512], ALU.add), "dve")
    p.barrier_into(("vector", "scalar", "tensor", "gpsimd", "sync"))


def emit_l0_ffn(p, B):
    nc = p.nc
    d = B.d
    S2 = float(1.0 / (127.0 * 127.0))
    with (
        nc.sbuf_tensor(uname("f0_wg"), [128, 4096], BF16) as wgbuf,
        nc.sbuf_tensor(uname("f0_wu"), [128, 4096], BF16) as wubuf,
        nc.sbuf_tensor(uname("f0_silu"), [128, 1024], BF16) as silu,
        nc.psum_tensor([128, 4096], F32) as pEO,
    ):
        tok_ep = [0, 0, 0, 0]
        tok_g2 = 0
        tok_hh = [0, 0]
        tok_g2h = [0, 0]
        tok_gmm = [0, 0, 0, 0]
        tok_umm = [0, 0, 0, 0]
        for tc in range(4):
            csl = rs(tc * 512, (tc + 1) * 512)
            tok_h = 0
            tok_gate_evac = 0
            for f in range(24):
                par = f % 2
                slot = f % 4
                tok_w = stream_cast(p, B, d["w_l0_gT"], 0, 8, f * 128, 128,
                                    wgbuf, slot * 1024, None,
                                    war_pe=tok_gmm[slot])
                p.dwait("tensor", tok_w)
                p.wait("tensor", "dve", B.tok_xnT)
                p.wait("tensor", "act", tok_g2h[par])
                p.wait("tensor", "dve", max(tok_ep))
                for dt in range(ND):
                    tm = p.inc(nc.tensor.matmul(
                        pEO[:, par * 512: par * 512 + 512],
                        wgbuf[:, slot * 1024 + dt * 128: slot * 1024 + (dt + 1) * 128],
                        xnt(B, dt, csl),
                        start=(dt == 0), stop=(dt == ND - 1)), "pe")
                tok_gmm[slot] = tm
                p.wait("scalar", "pe", tm)
                p.wait("scalar", "dve", tok_hh[par])
                tok_g1 = p.inc(nc.scalar.activation(
                    B.scr_f1[:, par * 512: par * 512 + 512],
                    pEO[:, par * 512: par * 512 + 512], AF.Tanh,
                    scale=float(0.5 / 127.0)), "act")
                tok_g2 = p.inc(nc.scalar.activation(
                    B.scr_f2[:, par * 512: par * 512 + 512],
                    pEO[:, par * 512: par * 512 + 512], AF.Identity,
                    scale=float(0.5 / 127.0)), "act")
                tok_g2h[par] = tok_g2
                p.wait("vector", "act", tok_g2)
                tok_gate_evac = p.inc(nc.vector.scalar_tensor_tensor(
                    silu[:, par * 512: par * 512 + 512],
                    B.scr_f1[:, par * 512: par * 512 + 512], 1.0,
                    B.scr_f2[:, par * 512: par * 512 + 512],
                    ALU.add, ALU.mult), "dve")
                tok_wu = stream_cast(p, B, d["w_l0_uT"], 0, 8, f * 128, 128,
                                     wubuf, slot * 1024, None,
                                     war_pe=tok_umm[slot])
                p.dwait("tensor", tok_wu)
                p.wait("tensor", "dve", tok_hh[par])
                for dt in range(ND):
                    tm = p.inc(nc.tensor.matmul(
                        pEO[:, 1024 + par * 512: 1024 + (par + 1) * 512],
                        wubuf[:, slot * 1024 + dt * 128: slot * 1024 + (dt + 1) * 128],
                        xnt(B, dt, csl),
                        start=(dt == 0), stop=(dt == ND - 1)), "pe")
                tok_umm[slot] = tm
                p.wait("vector", "pe", tm)
                p.wait("vector", "dve", tok_gate_evac)
                tok_h = p.inc(nc.vector.tensor_tensor(
                    B.arena[:, f * 512:(f + 1) * 512],
                    silu[:, par * 512: par * 512 + 512],
                    pEO[:, 1024 + par * 512: 1024 + (par + 1) * 512],
                    ALU.mult), "dve")
                tok_hh[par] = tok_h
            if getattr(B, "dbg_l0h", None):
                B.dbg_l0h(p, B, tc)
            if FFN_BAR >= 1:
                p.barrier_into(("vector", "scalar", "tensor", "gpsimd", "sync"))
            p.wait("tensor", "dve", tok_h)
            tok_dmm = [max(tok_gmm)] * 4
            for f in range(24):
                par = f % 2
                slot = f % 4
                tok_wd = stream_cast(p, B, d["w_l0_dT"], f * 128, 1, 0, 1024,
                                     wgbuf, slot * 1024, None,
                                     war_pe=tok_dmm[slot])
                p.dwait("tensor", tok_wd)
                for i in range(4):
                    for oc in range(2):
                        tm = p.inc(nc.tensor.matmul(
                            pEO[:, i * 1024 + oc * 512:
                                i * 1024 + (oc + 1) * 512],
                            B.arena[:, f * 512 + i * 128: f * 512 + (i + 1) * 128],
                            wgbuf[:, slot * 1024 + oc * 512:
                                  slot * 1024 + (oc + 1) * 512],
                            start=(f == 0), stop=(f == 23)), "pe")
                tok_dmm[slot] = tm
            tok_gmm = [tm, tm, tm, tm]
            if FFN_BAR >= 2:
                p.barrier_into(("vector", "scalar", "tensor", "gpsimd", "sync"))
            p.wait("vector", "pe", tm)
            for i in range(4):
                it = tc * 4 + i
                for oc in range(2):
                    xs = rs(it * D + oc * 512, it * D + (oc + 1) * 512)
                    tv = p.inc(nc.vector.scalar_tensor_tensor(
                        B.X[:, xs],
                        pEO[:, i * 1024 + oc * 512: i * 1024 + (oc + 1) * 512],
                        S2, B.X[:, xs], ALU.mult, ALU.add), "dve")
                tok_ep[i] = tv
            if FFN_BAR >= 3:
                p.barrier_into(("vector", "scalar", "tensor", "gpsimd", "sync"))
    p.barrier_into(("vector", "scalar", "tensor", "gpsimd", "sync"))


def emit_l0_attn(p, B):
    nc = p.nc
    d = B.d
    scale = 0.125
    wmix_d = d["l0_wmix"]
    with (
        nc.sbuf_tensor(uname("a0_wfull"), [128, 8192], BF16) as wfull,
        nc.sbuf_tensor(uname("a0_p64"), [128, 1024], BF16) as p64s,
        nc.sbuf_tensor(uname("a0_acol"), [128, 768], F32) as acol,
        nc.sbuf_tensor(uname("a0_bcol"), [128, 24], BF16) as bcol,
        nc.sbuf_tensor(uname("a0_abuf"), [128, 1024], BF16) as abuf,
        nc.sbuf_tensor(uname("a0_wmixT"), [128, 1024], BF16) as wmixT,
        nc.psum_tensor([128, 1024], F32) as pW,
        nc.psum_tensor([128, 512], F32) as pB2,
        nc.psum_tensor([128, 1024], BF16) as pT,
    ):
        gA = p.dgrp()
        gAb = p.dgrp("gpsimd")
        p.dma("sync", p64s[:].rearrange("p (a c) -> p a c", c=128),
              d["c_p64"].rearrange("(a p) c -> p a c", p=128), gA)
        for w, (ak, bk) in enumerate((("w_l0_aq", "w_l0_bq"),
                                      ("w_l0_ak", "w_l0_bk"),
                                      ("w_l0_av", "w_l0_bv"))):
            p.dma("sync", acol[:, w * 256:(w + 1) * 256], d[ak], gA)
            p.dma("gpsimd", bcol[:, w * 8:(w + 1) * 8], d[bk], gAb)
        p.dwait("vector", gA)
        p.dwait("tensor", gA)
        p.dwait("vector", gAb)
        p.dwait("tensor", gAb)
        tok_mm = 0
        tok_st = 0
        gWS = p.dgrp()
        for w, wk in enumerate(("w_l0_wq", "w_l0_wk", "w_l0_wv")):
            tok_w = stream_cast(p, B, d[wk], 0, 8, 0, 1024, wfull, 0, None,
                                war_pe=tok_mm)
            for h in range(16):
                p.wait("vector", "pe", tok_mm)
                for ot in range(8):
                    for ph in range(2):
                        ta = p.inc(nc.vector.tensor_scalar(
                            abuf[:, ot * 128 + ph * 64: ot * 128 + ph * 64 + 64],
                            p64s[:, ot * 128 + ph * 64: ot * 128 + ph * 64 + 64],
                            acol[:, w * 256 + h * 16 + ph * 8 + ot:
                                 w * 256 + h * 16 + ph * 8 + ot + 1],
                            None, ALU.mult), "dve")
                p.wait("tensor", "dve", ta)
                p.dwait("tensor", tok_w)
                p.wait("tensor", "dve", tok_wm_ev if (w or h) else 0)
                for ot in range(8):
                    for oc in range(2):
                        tok_mm = p.inc(nc.tensor.matmul(
                            pW[:, oc * 512:(oc + 1) * 512],
                            abuf[:, ot * 128:(ot + 1) * 128],
                            wfull[:, ot * 1024 + oc * 512:
                                  ot * 1024 + (oc + 1) * 512],
                            start=(ot == 0), stop=(ot == 7)), "pe")
                    tok_mm = p.inc(nc.tensor.matmul(
                        pB2[:, 0:1], abuf[:, ot * 128:(ot + 1) * 128],
                        bcol[:, w * 8 + ot: w * 8 + ot + 1],
                        start=(ot == 0), stop=(ot == 7)), "pe")
                p.wait("vector", "pe", tok_mm)
                for oc in range(2):
                    tv = p.inc(nc.vector.tensor_copy(
                        B.scr_bf[:, oc * 512:(oc + 1) * 512],
                        pW[:, oc * 512:(oc + 1) * 512]), "dve")
                tv = p.inc(nc.vector.tensor_copy(
                    B.colbuf[:, w * 16 + h: w * 16 + h + 1], pB2[:, 0:1]),
                    "dve")
                tok_wm_ev = tv
                p.wait("tensor", "dve", tv)
                for it in range(8):
                    tp = p.inc(nc.tensor.transpose(
                        pT[:, it * 128:(it + 1) * 128],
                        B.scr_bf[:, it * 128:(it + 1) * 128], B.ident), "pe")
                p.wait("vector", "pe", tp)
                p.dwait("vector", gWS)
                tv = p.inc(nc.vector.tensor_copy(wmixT[:], pT[:]), "dve")
                p.wait("sync", "dve", tv)
                p.dma("sync",
                      wmix_d[w * 16 + h].rearrange("(a p) c -> p a c", p=128),
                      wmixT[:].rearrange("p (a c) -> p a c", c=128), gWS)
                tok_st = p.cnt["dve"]
        p.dwait("sync", gWS)
    p.barrier_into(("vector", "scalar", "tensor", "gpsimd", "sync"))

    with (
        nc.sbuf_tensor(uname("b0_wmix"), [128, 3 * 1024], BF16) as wmix,
        nc.sbuf_tensor(uname("b0_qtm"), [128, 2048], BF16) as QTm,
        nc.sbuf_tensor(uname("b0_qtd"), [128, 2048], BF16) as QTd,
        nc.sbuf_tensor(uname("b0_ktm"), [128, 2048], BF16) as KTm,
        nc.sbuf_tensor(uname("b0_vtf"), [128, 2048], F32) as VTmf,
        nc.sbuf_tensor(uname("b0_vtok"), [128, 2 * NT * 66], BF16) as vtok,
        nc.sbuf_tensor(uname("b0_attnT"), [128, 2048], BF16) as attnT,
        nc.sbuf_tensor(uname("b0_n0"), [64, 2048], BF16) as n0buf,
        nc.sbuf_tensor(uname("b0_ccol"), [64, 32], F32) as ccol,
        nc.sbuf_tensor(uname("b0_wo"), [128, 2048], BF16) as wobuf,
        nc.sbuf_tensor(uname("b0_worow"), [1, 1024], BF16) as worow,
        nc.psum_tensor([128, 2048], F32) as pS,
        nc.psum_tensor([128, 2048], F32) as pO,
    ):
        gB0 = p.dgrp()
        p.dma("sync", ccol[:], d["w_l0_col"], gB0)
        p.dma("sync", worow[:], d["b_l0_bo_row"], gB0)
        p.dwait("vector", gB0)
        p.dwait("tensor", gB0)
        for h in range(16):
            gH = p.dgrp()
            p.wait("sync", "pe", getattr(B, "_l0_projpe", 0))
            for w in range(3):
                p.dma("sync",
                      wmix[:, w * 1024:(w + 1) * 1024].rearrange(
                          "p (a c) -> p a c", c=128),
                      wmix_d[w * 16 + h].rearrange("(a p) c -> p a c", p=128),
                      gH)
            p.dwait("tensor", gH)
            p.wait("tensor", "dve", B._tok_ctx_evac)
            tv = 0
            for w in range(3):
                for cc in range(4):
                    for dt in range(ND):
                        tm = p.inc(nc.tensor.matmul(
                            pS[:, cc * 512:(cc + 1) * 512],
                            wmix[:, w * 1024 + dt * 128: w * 1024 + (dt + 1) * 128],
                            xnt(B, dt, rs(cc * 512, (cc + 1) * 512)),
                            start=(dt == 0), stop=(dt == ND - 1)), "pe")
                    p.wait("vector", "pe", tm)
                    if w < 2:
                        dstT = QTm if w == 0 else KTm
                        tv = p.inc(nc.vector.tensor_scalar(
                            dstT[:, cc * 512:(cc + 1) * 512],
                            pS[:, cc * 512:(cc + 1) * 512],
                            B.colbuf[:, w * 16 + h: w * 16 + h + 1],
                            None, ALU.add), "dve")
                    else:
                        tv = p.inc(nc.vector.tensor_scalar(
                            VTmf[:, cc * 512:(cc + 1) * 512],
                            pS[:, cc * 512:(cc + 1) * 512],
                            B.colbuf[:, w * 16 + h: w * 16 + h + 1],
                            None, ALU.add), "dve")
                p.wait("tensor", "dve", tv)
            B._l0_projpe = p.cnt["pe"]
            tok_qkv_pe = p.cnt["pe"]
            tok_qkv = tv
            tvt = emit_vtok(p, B, VTmf, vtok, pO, 2, 64, 66, tok_qkv)
            tnb = emit_negB_full(p, B, QTm, KTm, pS, scale, 100, tok_qkv_pe,
                                 tok_qkv)
            p.wait("tensor", "dve", max(tnb, tvt))
            p.wait("scalar", "dve", max(tnb, tvt))
            for p1 in range(2):
                gQ = p.dgrp()
                p.wait("sync", "dve", tok_qkv)
                p.wait("sync", "pe", p.cnt["pe"])
                p.dma("sync", QTd[0:64, :], QTm[p1 * 64: p1 * 64 + 64, :], gQ)
                p.dma("sync", QTd[64:128, :], QTm[p1 * 64: p1 * 64 + 64, :], gQ)
                p.dwait("tensor", gQ)
                attn_chunks_l0(p, B, QTd, KTm, vtok, attnT, pS, pO, 100, scale,
                               h, p1, n0buf, ccol)
        tok_x = 0
        for quarter in range(4):
            tok_wp = stream_cast(p, B, d["w_l0_woT"], quarter * 256, 2, 0,
                                 1024, wobuf, 0, VTmf)
            p.dwait("tensor", tok_wp)
            p.wait("tensor", "dve", B._tok_ctx_evac)
            for i in range(NT):
                for oc in range(2):
                    if quarter == 0:
                        tm = p.inc(nc.tensor.matmul(
                            pS[:, oc * 512: oc * 512 + 512], B.ones[0:1, :],
                            worow[0:1, oc * 512:(oc + 1) * 512],
                            start=True, stop=False), "pe")
                    for hpq in range(2):
                        hp = quarter * 2 + hpq
                        tm = p.inc(nc.tensor.matmul(
                            pS[:, oc * 512: oc * 512 + 512],
                            B.arena[:, hp * 2048 + i * 128: hp * 2048 + i * 128 + 128],
                            wobuf[:, hpq * 1024 + oc * 512: hpq * 1024 + (oc + 1) * 512],
                            start=(hpq == 0 and quarter != 0),
                            stop=(hpq == 1)), "pe")
                    p.wait("vector", "pe", tm)
                    tok_x = p.inc(nc.vector.tensor_tensor(
                        B.X[:, i * D + oc * 512: i * D + (oc + 1) * 512],
                        B.X[:, i * D + oc * 512: i * D + (oc + 1) * 512],
                        pS[:, oc * 512: oc * 512 + 512], ALU.add), "dve")
                    p.wait("tensor", "dve", tok_x)
    B._tok_ctx_evac = 0
    B._l0_projpe = 0
    p.barrier_into(("vector", "scalar", "tensor", "gpsimd", "sync"))


def emit_negB_full(p, B, QT, KT, ps, scale, col_q, tok_ready_pe, tok_ready_dve):
    nc = p.nc
    tdv = 0
    tm_prev = tok_ready_pe
    for w, MT in ((0, QT), (1, KT)):
        p.wait("scalar", "pe", tm_prev)
        p.wait("scalar", "dve", max(tok_ready_dve, tdv))
        ta = p.inc(nc.scalar.activation(B.scr_bf[:, 0:2048], MT[:],
                                        AF.Square), "act")
        p.wait("tensor", "act", ta)
        p.wait("tensor", "dve", tdv)
        for cc in range(4):
            tm = p.inc(nc.tensor.matmul(
                ps[:, cc * 512:(cc + 1) * 512], B.ones[:, :],
                B.scr_bf[:, cc * 512:(cc + 1) * 512],
                start=True, stop=True), "pe")
        tm_prev = tm
        p.wait("vector", "pe", tm)
        for cc in range(4):
            tdv = p.inc(nc.vector.tensor_reduce(
                B.colbuf[:, 122 + cc: 123 + cc],
                ps[:, cc * 512:(cc + 1) * 512], AX.X, ALU.max), "dve")
        p.wait("vector", "dve", tdv)
        tdv = p.inc(nc.vector.tensor_reduce(
            B.colbuf[:, 120 + w: 121 + w], B.colbuf[:, 122:126],
            AX.X, ALU.max), "dve")
        p.wait("vector", "dve", tdv)
    c0 = B.colbuf[:, 120:121]
    c1 = B.colbuf[:, 121:122]
    tdv = p.inc(nc.vector.tensor_tensor(c0, c0, c1, ALU.mult), "dve")
    p.wait("scalar", "dve", tdv)
    tac = p.inc(nc.scalar.activation(c0, c0, AF.Ln), "act")
    p.wait("scalar", "act", tac)
    tac = p.inc(nc.scalar.activation(c0, c0, AF.Exp, scale=0.5), "act")
    p.wait("vector", "act", tac)
    tdv = p.inc(nc.vector.tensor_scalar(
        B.colbuf[:, col_q: col_q + 1], c0, -scale, None, ALU.mult), "dve")
    return tdv


def attn_chunks_l0(p, B, QTd, KTm, vtok, attnT, pS, pO, col_b, scale,
                   h, p1, n0buf, ccol):
    nc = p.nc
    tok_exp = {}
    tok_av = {}
    for c in range(NC2):
        q0 = c * 512
        jmax = 4 * c + 3
        t_prev_ep = B._tok_ctx_evac

        def emit_av(j, par, qlo, tmask):
            p.wait("tensor", "act", tok_exp[(par, j)])
            if tmask:
                p.wait("tensor", "dve", tmask)
            ta = 0
            for p2 in range(2):
                for strip in range(2):
                    ta = p.inc(nc.tensor.matmul(
                        pO[0:65, strip * 512 + qlo: strip * 512 + 512],
                        vtok[ts(strip, 64),
                             p2 * NT * 66 + j * 66: p2 * NT * 66 + j * 66 + 65],
                        attnT[ts(strip, 64), (par * 2 + p2) * 512 + qlo:
                              (par * 2 + p2) * 512 + 512],
                        start=(j == 0 and p2 == 0),
                        stop=(j == jmax and p2 == 1),
                        tile_position=(strip * 64, 0)), "pe")
            tok_av[(par, j)] = ta
            return ta

        pend = None
        for j in range(jmax + 1):
            par = j % 2
            qlo = max(0, 128 * j - q0)
            p.wait("tensor", "act", tok_exp.get((par, j - 2), 0))
            if j == 0:
                p.wait("tensor", "dve", t_prev_ep)
            diag = 128 * j >= q0
            for p2 in range(2):
                tm = p.inc(nc.tensor.matmul(
                    pS[:, (par * 2 + p2) * 512 + qlo: (par * 2 + p2) * 512 + 512],
                    KTm[ts(p2, 64), ts(j, 128)],
                    QTd[ts(p2, 64), rs(q0 + qlo, q0 + 512)],
                    start=True, stop=not diag,
                    tile_position=(p2 * 64, 0)), "pe")
                if diag:
                    tm = p.inc(nc.tensor.matmul(
                        pS[:, (par * 2 + p2) * 512 + qlo:
                           (par * 2 + p2) * 512 + qlo + 128],
                        B.ident, B.negtri,
                        start=False, stop=True,
                        tile_position=(0, 0)), "pe")
            tok_s = tm
            if pend is not None:
                emit_av(*pend)
            p.wait("scalar", "pe", tok_s)
            p.wait("scalar", "pe", tok_av.get((par, j - 2), 0))
            pin = pS[:, par * 1024: par * 1024 + 1024].rearrange(
                "p (h q) -> p h q", q=512)[:, :, qlo:512]
            pout = attnT[:, par * 1024: par * 1024 + 1024].rearrange(
                "p (h q) -> p h q", q=512)[:, :, qlo:512]
            te = p.inc(nc.scalar.activation(
                pout, pin, AF.Exp, scale=scale,
                bias=B.colbuf[:, col_b: col_b + 1]), "act")
            tok_exp[(par, j)] = te
            pend = (j, par, qlo, 0)
        ta = emit_av(*pend)
        p.wait("vector", "pe", ta)
        tv = p.inc(nc.vector.tensor_copy(
            B.scr_f1[0:65, 0:512], pO[0:65, 0:512]), "dve")
        p.wait("vector", "dve", tv)
        tv = p.inc(nc.vector.tensor_tensor(
            B.scr_f1[0:65, 0:512], B.scr_f1[0:65, 0:512],
            pO[0:65, 512:1024], ALU.add), "dve")
        p.wait("vector", "dve", tv)
        tv = p.inc(nc.vector.reciprocal(B.scr_f2[0:1, 0:512],
                                        B.scr_f1[64:65, 0:512]), "dve")
        p.wait("tensor", "dve", tv)
        p.wait("tensor", "act", tok_exp[(jmax % 2, jmax)])
        tb = p.inc(nc.tensor.matmul(
            pS[0:64, 0:512], B.onesf[0:1, 0:64], B.scr_f2[0:1, 0:512],
            start=True, stop=True), "pe")
        p.wait("vector", "pe", tb)
        tv = p.inc(nc.vector.tensor_tensor(
            B.scr_f1[0:64, 0:512], B.scr_f1[0:64, 0:512], pS[0:64, 0:512],
            ALU.mult), "dve")
        p.wait("vector", "dve", tv)
        if p1 == 0:
            tv = p.inc(nc.vector.tensor_copy(
                n0buf[0:64, c * 512:(c + 1) * 512], B.scr_f1[0:64, 0:512]),
                "dve")
        else:
            tv = p.inc(nc.vector.tensor_scalar(
                B.scr_f2[0:64, 0:512], B.scr_f1[0:64, 0:512],
                ccol[0:64, h * 2 + 1: h * 2 + 2], None, ALU.mult), "dve")
            p.wait("vector", "dve", tv)
            dst = B.arena[(h % 2) * 64: (h % 2) * 64 + 64,
                          (h // 2) * 2048 + c * 512: (h // 2) * 2048 + (c + 1) * 512]
            tv = p.inc(nc.vector.scalar_tensor_tensor(
                dst, n0buf[0:64, c * 512:(c + 1) * 512],
                ccol[0:64, h * 2: h * 2 + 1], B.scr_f2[0:64, 0:512],
                ALU.mult, ALU.add), "dve")
        B._tok_ctx_evac = p.cnt["dve"]


def emit_l2_ffn(p, B):
    nc = p.nc
    d = B.d
    with (
        nc.sbuf_tensor(uname("f2_w1"), [128, 4096], BF16) as w1buf,
        nc.sbuf_tensor(uname("f2_w2"), [128, 4096], BF16) as w2buf,
        nc.sbuf_tensor(uname("f2_b1"), [128, 32], F32) as b1col,
        nc.sbuf_tensor(uname("f2_rows"), [1, 3 * 1024], BF16) as rows3,
        nc.sbuf_tensor(uname("f2_gb"), [128, 2048], BF16) as gb,
        nc.psum_tensor([128, 4096], F32) as pEO,
    ):
        gL2 = p.dgrp()
        p.dma("sync", b1col[:], d["w_l2_b1col"], gL2)
        p.dma("sync", rows3[0:1, 0:1024], d["b_l2_b2_row"], gL2)
        p.dma("sync", rows3[0:1, 1024:2048], d["b_l2_gamma_row"], gL2)
        p.dma("sync", rows3[0:1, 2048:3072], d["b_l2_beta_row"], gL2)
        p.dwait("tensor", gL2)
        for gi in range(2):
            for oc in range(2):
                tm = p.inc(nc.tensor.matmul(
                    pEO[:, oc * 512:(oc + 1) * 512], B.ones[0:1, :],
                    rows3[0:1, (1 + gi) * 1024 + oc * 512:
                          (1 + gi) * 1024 + (oc + 1) * 512],
                    start=True, stop=True), "pe")
            p.wait("vector", "pe", tm)
            for oc in range(2):
                tgb = p.inc(nc.vector.tensor_copy(
                    gb[:, gi * 1024 + oc * 512: gi * 1024 + (oc + 1) * 512],
                    pEO[:, oc * 512:(oc + 1) * 512]), "dve")
            p.wait("tensor", "dve", tgb)

        mean, rstd = B.stats[:, 0:16], B.stats[:, 16:32]
        c_sum = B.colbuf[:, 110:111]
        c_m2 = B.colbuf[:, 108:109]
        c_nm2 = B.colbuf[:, 111:112]
        c_rs2 = B.colbuf[:, 109:110]
        tok_ep = [0, 0, 0, 0]
        tok_w1mm = [0, 0, 0, 0]
        tok_w2mm = [0, 0, 0, 0]
        for tc in range(4):
            csl = rs(tc * 512, (tc + 1) * 512)
            tok_h0 = 0
            for f in range(32):
                tok_w1 = stream_cast(p, B, d["w_l2_w1"], 0, 8, f * 128, 128,
                                     w1buf, (f % 4) * 1024, None,
                                     war_pe=tok_w1mm[f % 4])
                p.dwait("tensor", tok_w1)
                p.wait("tensor", "dve", B.tok_xnT)
                p.wait("tensor", "act", tok_h0)
                p.wait("tensor", "dve", max(tok_ep))
                for dt in range(ND):
                    tm = p.inc(nc.tensor.matmul(
                        pEO[:, (f % 2) * 512:(f % 2) * 512 + 512],
                        w1buf[:, (f % 4) * 1024 + dt * 128:
                              (f % 4) * 1024 + (dt + 1) * 128],
                        xnt(B, dt, csl),
                        start=(dt == 0), stop=(dt == ND - 1)), "pe")
                tok_w1mm[f % 4] = tm
                p.wait("scalar", "pe", tm)
                p.dwait("scalar", gL2)
                tok_h0 = p.inc(nc.scalar.activation(
                    B.arena[:, f * 512:(f + 1) * 512],
                    pEO[:, (f % 2) * 512:(f % 2) * 512 + 512],
                    AF.Relu, bias=b1col[:, f:f + 1]), "act")
            p.wait("tensor", "act", tok_h0)
            for i in range(4):
                for oc in range(2):
                    tm = p.inc(nc.tensor.matmul(
                        pEO[:, i * 1024 + oc * 512: i * 1024 + (oc + 1) * 512],
                        B.ones[0:1, :],
                        rows3[0:1, oc * 512:(oc + 1) * 512],
                        start=True, stop=False), "pe")
            for f in range(32):
                tok_w2 = stream_cast(p, B, d["w_l2_w2"], f * 128, 1, 0, 1024,
                                     w2buf, (f % 4) * 1024, None,
                                     war_pe=tok_w2mm[f % 4])
                p.dwait("tensor", tok_w2)
                for i in range(4):
                    for oc in range(2):
                        tm = p.inc(nc.tensor.matmul(
                            pEO[:, i * 1024 + oc * 512:
                                i * 1024 + (oc + 1) * 512],
                            B.arena[:, f * 512 + i * 128: f * 512 + (i + 1) * 128],
                            w2buf[:, (f % 4) * 1024 + oc * 512:
                                  (f % 4) * 1024 + (oc + 1) * 512],
                            start=False, stop=(f == 31)), "pe")
                tok_w2mm[f % 4] = tm
            tok_eo = tm
            for i in range(4):
                it = tc * 4 + i
                p.wait("vector", "pe", tok_eo)
                tv = p.inc(nc.vector.tensor_scalar(
                    B.scr_f1[:, 0:1024], B.X[:, ts(it, D)],
                    mean[:, it:it + 1], rstd[:, it:it + 1],
                    ALU.subtract, ALU.mult), "dve")
                p.wait("vector", "dve", tv)
                for oc in range(2):
                    tv = p.inc(nc.vector.tensor_tensor(
                        B.scr_f2[:, oc * 512:(oc + 1) * 512],
                        B.scr_f1[:, oc * 512:(oc + 1) * 512],
                        pEO[:, i * 1024 + oc * 512: i * 1024 + (oc + 1) * 512],
                        ALU.add), "dve")
                p.wait("vector", "dve", tv)
                tok_ep[i] = tv
                p.wait("scalar", "dve", tv)
                ta = p.inc(nc.scalar.activation(
                    B.scr_bf[:, 0:1024], B.scr_f2[:, 0:1024], AF.Identity,
                    accum_out=c_sum), "act")
                p.wait("vector", "act", ta)
                tv = p.inc(nc.vector.tensor_scalar(c_m2, c_sum, 1.0 / D, None,
                                                   ALU.mult), "dve")
                tv = p.inc(nc.vector.tensor_scalar(c_nm2, c_m2, -1.0, None,
                                                   ALU.mult), "dve")
                p.wait("vector", "dve", tv)
                p.wait("scalar", "dve", tv)
                ta = p.inc(nc.scalar.activation(
                    B.scr_bf[:, 0:1024], B.scr_f2[:, 0:1024], AF.Square,
                    bias=c_nm2, accum_out=c_sum), "act")
                p.wait("vector", "act", ta)
                tv = p.inc(nc.vector.tensor_scalar(c_rs2, c_sum, 1.0 / D, None,
                                                   ALU.mult), "dve")
                p.wait("scalar", "dve", tv)
                ta = p.inc(nc.scalar.activation(c_rs2, c_rs2, AF.Ln,
                                                bias=B.epscol), "act")
                p.wait("scalar", "act", ta)
                ta = p.inc(nc.scalar.activation(c_rs2, c_rs2, AF.Exp,
                                                scale=-0.5), "act")
                p.wait("vector", "act", ta)
                tv = p.inc(nc.vector.tensor_scalar(
                    B.scr_f1[:, 0:1024], B.scr_f2[:, 0:1024], c_m2, c_rs2,
                    ALU.subtract, ALU.mult), "dve")
                p.wait("vector", "dve", tv)
                tv = p.inc(nc.vector.tensor_tensor(
                    B.scr_f2[:, 0:1024], B.scr_f1[:, 0:1024],
                    gb[:, 0:1024], ALU.mult), "dve")
                p.wait("vector", "dve", tv)
                tv = p.inc(nc.vector.tensor_tensor(
                    B.scr_f2[:, 0:1024], B.scr_f2[:, 0:1024],
                    gb[:, 1024:2048], ALU.add), "dve")
                p.wait("vector", "dve", tv)
                tv = p.inc(nc.vector.tensor_tensor(
                    B.X[:, ts(it, D)], B.X[:, ts(it, D)],
                    B.scr_f2[:, 0:1024], ALU.add), "dve")
                tok_ep[i] = tv
    p.barrier_into(("vector", "scalar", "tensor", "gpsimd", "sync"))


def emit_l3_ffn(p, B):
    nc = p.nc
    d = B.d
    NE = 8
    M3 = 256
    import contextlib
    with contextlib.ExitStack() as _st3:
        _e = _st3.enter_context
        Tb = _e(nc.sbuf_tensor(uname("f3_T"), [8, 2048], F32))
        G8 = _e(nc.sbuf_tensor(uname("f3_G8"), [8, 2048], F32))
        Ttok = _e(nc.sbuf_tensor(uname("f3_tok"), [128, 128], F32))
        gtok = _e(nc.sbuf_tensor(uname("f3_gtok"), [128, 128], F32))
        selb = _e(nc.sbuf_tensor(uname("f3_sel"), [128, 128], F32))
        hardb = _e(nc.sbuf_tensor(uname("f3_hard"), [128, 128], F32))
        Gb = _e(nc.sbuf_tensor(uname("f3_G"), [128, 128], F32))
        Gbf = _e(nc.sbuf_tensor(uname("f3_Gbf"), [128, 128], BF16))
        rankb = _e(nc.sbuf_tensor(uname("f3_rank"), [128, 128], F32))
        taub = _e(nc.sbuf_tensor(uname("f3_tau"), [128, 8], F32))
        PeT = _e(nc.sbuf_tensor(uname("f3_pet"), [128, 16 * 256], BF16))
        PeL = _e(nc.sbuf_tensor(uname("f3_pel"), [128, 256], BF16))
        ReT = _e(nc.sbuf_tensor(uname("f3_ret"), [128, 8 * 256], BF16))
        Ye = _e(nc.sbuf_tensor(uname("f3_y"), [128, 2048], BF16))
        webuf = _e(nc.sbuf_tensor(uname("f3_we"), [128, 4 * 1024], BF16))
        gwbuf = _e(nc.sbuf_tensor(uname("f3_gw"), [128, 64], BF16))
        pA = _e(nc.psum_tensor([128, 2560], F32))
        pB16 = _e(nc.psum_tensor([128, 1024], BF16))
        pY = _e(nc.psum_tensor([128, 1024], F32))
        mean, rstd = B.stats[:, 0:16], B.stats[:, 16:32]
        p.wait("vector", "act", B.tok_lnstats)
        for i in range(NT):
            txn = p.inc(nc.vector.tensor_scalar(
                B.arena[:, i * 1024:(i + 1) * 1024], B.X[:, ts(i, D)],
                mean[:, i:i + 1], rstd[:, i:i + 1],
                ALU.subtract, ALU.mult), "dve")
        B._l3_xn_tok = txn
        tok_gw = stream_cast(p, B, d["w_l3_gate"], 0, 8, 0, 8, gwbuf, 0, None)
        p.dwait("tensor", tok_gw)
        p.wait("tensor", "dve", B.tok_xnT)
        for cc in range(4):
            for dt in range(ND):
                tm = p.inc(nc.tensor.matmul(
                    pA[0:8, cc * 512:(cc + 1) * 512],
                    gwbuf[:, dt * 8:(dt + 1) * 8],
                    xnt(B, dt, rs(cc * 512, (cc + 1) * 512)),
                    start=(dt == 0), stop=(dt == ND - 1)), "pe")
        p.wait("vector", "pe", tm)
        for cc in range(4):
            tv = p.inc(nc.vector.tensor_copy(
                Tb[:, cc * 512:(cc + 1) * 512],
                pA[0:8, cc * 512:(cc + 1) * 512]), "dve")
        p.wait("vector", "dve", tv)
        tv = p.inc(nc.vector.tensor_scalar(Tb[:], Tb[:], 1e-6, None, ALU.max),
                   "dve")
        p.wait("scalar", "dve", tv)
        ta = p.inc(nc.scalar.activation(Tb[:], Tb[:], AF.Ln), "act")
        p.wait("vector", "act", ta)
        p.wait("tensor", "act", ta)

        c_rm = B.colbuf[0:8, 110:111]
        c_nrm = B.colbuf[0:8, 111:112]
        c_se = B.colbuf[0:8, 112:113]
        rm2 = B.colbuf[:, 113:129] if False else None
        for it in range(8):
            tv = p.inc(nc.vector.tensor_reduce(c_rm, Tb[:], AX.X, ALU.max),
                       "dve")
            p.wait("vector", "dve", tv)
            tv = p.inc(nc.vector.tensor_scalar(c_nrm, c_rm, -1.0, None,
                                               ALU.mult), "dve")
            p.wait("scalar", "dve", tv)
            ta = p.inc(nc.scalar.activation(B.scr_bf[0:8, 0:2048], Tb[:],
                                            AF.Exp, bias=c_nrm,
                                            accum_out=c_se), "act")
            p.wait("scalar", "act", ta)
            ta = p.inc(nc.scalar.activation(c_se, c_se, AF.Ln), "act")
            p.wait("vector", "act", ta)
            tv = p.inc(nc.vector.tensor_tensor(c_se, c_se, c_rm, ALU.add),
                       "dve")
            p.wait("vector", "dve", tv)
            tv = p.inc(nc.vector.tensor_scalar(Tb[:], Tb[:], c_se, None,
                                               ALU.subtract), "dve")
            p.wait("tensor", "dve", tv)
            for i in range(NT):
                tp = p.inc(nc.tensor.transpose(
                    pA[:, 2048 + i * 8: 2048 + (i + 1) * 8],
                    Tb[0:8, ts(i, 128)], B.identf[0:8, 0:8]), "pe")
            p.wait("vector", "pe", tp)
            tv = p.inc(nc.vector.tensor_copy(Ttok[:], pA[:, 2048:2176]), "dve")
            p.wait("vector", "dve", tv)
            t3 = Ttok[:].rearrange("p (g e) -> p g e", e=8)
            rm16 = B.scr_f1[:, 0:16]
            sm16 = B.scr_f1[:, 16:32]
            tv = p.inc(nc.vector.tensor_reduce(rm16, t3, AX.X, ALU.max), "dve")
            p.wait("vector", "dve", tv)
            bc = rm16.rearrange("p (g o) -> p g o", o=1).broadcast_to([128, 16, 8])
            tv = p.inc(nc.vector.tensor_tensor(
                B.scr_f2[:, 0:128].rearrange("p (g e) -> p g e", e=8), t3, bc,
                ALU.subtract), "dve")
            p.wait("scalar", "dve", tv)
            ta = p.inc(nc.scalar.activation(B.scr_f2[:, 128:256],
                                            B.scr_f2[:, 0:128], AF.Exp), "act")
            p.wait("vector", "act", ta)
            tv = p.inc(nc.vector.tensor_reduce(
                sm16, B.scr_f2[:, 128:256].rearrange("p (g e) -> p g e", e=8),
                AX.X, ALU.add), "dve")
            p.wait("scalar", "dve", tv)
            ta = p.inc(nc.scalar.activation(sm16, sm16, AF.Ln), "act")
            p.wait("vector", "act", ta)
            tv = p.inc(nc.vector.tensor_tensor(sm16, sm16, rm16, ALU.add),
                       "dve")
            p.wait("vector", "dve", tv)
            bc2 = sm16.rearrange("p (g o) -> p g o", o=1).broadcast_to([128, 16, 8])
            tv = p.inc(nc.vector.tensor_tensor(t3, t3, bc2, ALU.subtract),
                       "dve")
            p.wait("tensor", "dve", tv)
            p.wait("tensor", "dve", tv)
            for i in range(NT):
                tp = p.inc(nc.tensor.transpose(
                    pA[0:8, i * 128:(i + 1) * 128],
                    Ttok[:, i * 8:(i + 1) * 8], B.identf[:, 0:128]), "pe")
            p.wait("vector", "pe", tp)
            for cc in range(4):
                tv = p.inc(nc.vector.tensor_copy(
                    Tb[:, cc * 512:(cc + 1) * 512],
                    pA[0:8, cc * 512:(cc + 1) * 512]), "dve")
            p.wait("vector", "dve", tv)
            p.wait("scalar", "dve", tv)
            p.wait("tensor", "dve", tv)
        ta = p.inc(nc.scalar.activation(G8[:], Tb[:], AF.Exp), "act")
        p.wait("vector", "act", ta)
        p.wait("tensor", "act", ta)
        for i in range(NT):
            tp = p.inc(nc.tensor.transpose(
                pA[:, 2048 + i * 8: 2048 + (i + 1) * 8],
                G8[0:8, ts(i, 128)], B.identf[0:8, 0:8]), "pe")
        p.wait("vector", "pe", tp)
        tv = p.inc(nc.vector.tensor_copy(gtok[:], pA[:, 2048:2176]), "dve")
        p.wait("vector", "dve", tv)
        lo = B.colbuf[0:8, 114:115]
        hi = B.colbuf[0:8, 115:116]
        mid = B.colbuf[0:8, 116:117]
        cntc = B.colbuf[0:8, 117:118]
        prd = B.colbuf[0:8, 118:119]
        pm1 = B.colbuf[0:8, 119:120]
        t1c = B.colbuf[0:8, 120:121]
        tv = p.inc(nc.vector.memset(lo, 0.0), "dve")
        tv = p.inc(nc.vector.memset(hi, 1.125), "dve")
        p.wait("vector", "dve", tv)
        for bi in range(42):
            tv = p.inc(nc.vector.tensor_tensor(mid, lo, hi, ALU.add), "dve")
            p.wait("vector", "dve", tv)
            tv = p.inc(nc.vector.tensor_scalar(mid, mid, 0.5, None, ALU.mult),
                       "dve")
            p.wait("vector", "dve", tv)
            tv = p.inc(nc.vector.tensor_scalar(B.scr_bf[0:8, 0:2048], G8[:],
                                               mid, None, ALU.is_ge), "dve")
            p.wait("vector", "dve", tv)
            tv = p.inc(nc.vector.tensor_reduce(cntc, B.scr_bf[0:8, 0:2048],
                                               AX.X, ALU.add), "dve")
            p.wait("vector", "dve", tv)
            tv = p.inc(nc.vector.tensor_scalar(prd, cntc, float(M3), None,
                                               ALU.is_ge), "dve")
            p.wait("vector", "dve", tv)
            tv = p.inc(nc.vector.tensor_tensor(t1c, mid, lo, ALU.subtract),
                       "dve")
            p.wait("vector", "dve", tv)
            tv = p.inc(nc.vector.tensor_tensor(t1c, t1c, prd, ALU.mult), "dve")
            p.wait("vector", "dve", tv)
            tv = p.inc(nc.vector.tensor_tensor(lo, lo, t1c, ALU.add), "dve")
            p.wait("vector", "dve", tv)
            tv = p.inc(nc.vector.tensor_scalar(pm1, prd, -1.0, 1.0, ALU.mult,
                                               ALU.add), "dve")
            p.wait("vector", "dve", tv)
            tv = p.inc(nc.vector.tensor_tensor(t1c, mid, hi, ALU.subtract),
                       "dve")
            p.wait("vector", "dve", tv)
            tv = p.inc(nc.vector.tensor_tensor(t1c, t1c, pm1, ALU.mult), "dve")
            p.wait("vector", "dve", tv)
            tv = p.inc(nc.vector.tensor_tensor(hi, hi, t1c, ALU.add), "dve")
            p.wait("vector", "dve", tv)
        p.wait("tensor", "dve", tv)
        tp = p.inc(nc.tensor.transpose(pA[:, 2048:2056][0:1, :],
                                       lo, B.identf[0:8, 0:8]), "pe")
        p.wait("vector", "pe", tp)
        tv = p.inc(nc.vector.tensor_copy(B.scr_f2[0:1, 0:8],
                                         pA[0:1, 2048:2056]), "dve")
        p.wait("tensor", "dve", tv)
        tp = p.inc(nc.tensor.matmul(pA[:, 2048:2056], B.onesf[0:1, :],
                                    B.scr_f2[0:1, 0:8], start=True, stop=True),
                   "pe")
        p.wait("vector", "pe", tp)
        tv = p.inc(nc.vector.tensor_copy(taub[:], pA[:, 2048:2056]), "dve")
        p.wait("vector", "dve", tv)
        for i in range(NT):
            tv = p.inc(nc.vector.tensor_tensor(
                selb[:, i * 8:(i + 1) * 8], gtok[:, i * 8:(i + 1) * 8],
                taub[:], ALU.is_ge), "dve")
        tv = p.inc(nc.vector.tensor_scalar(hardb[:], gtok[:], 0.5, None,
                                           ALU.is_gt), "dve")
        p.wait("vector", "dve", tv)
        acc = B.scr_f1[:, 0:1]
        tcol = B.scr_f1[:, 1:2]
        for i in range(NT):
            tv = p.inc(nc.vector.memset(acc, 1.0), "dve")
            p.wait("vector", "dve", tv)
            for e in range(7, -1, -1):
                selc = selb[:, i * 8 + e: i * 8 + e + 1]
                tv = p.inc(nc.vector.tensor_tensor(
                    Gb[:, i * 8 + e: i * 8 + e + 1], selc, acc, ALU.mult),
                    "dve")
                p.wait("vector", "dve", tv)
                if e:
                    tv = p.inc(nc.vector.tensor_tensor(tcol, selc, acc,
                                                       ALU.mult), "dve")
                    p.wait("vector", "dve", tv)
                    tv = p.inc(nc.vector.tensor_tensor(acc, acc, tcol,
                                                       ALU.subtract), "dve")
                    p.wait("vector", "dve", tv)
        tv = p.inc(nc.vector.tensor_tensor(Gb[:], Gb[:], hardb[:], ALU.mult),
                   "dve")
        p.wait("vector", "dve", tv)
        tv = p.inc(nc.vector.tensor_copy(Gbf[:], Gb[:]), "dve")
        p.wait("tensor", "dve", tv)
        for i in range(NT):
            for j in range(i + 1):
                lhs = B.triu if j == i else B.ones
                tm = p.inc(nc.tensor.matmul(
                    pA[:, 2176:2184], lhs, Gbf[:, j * 8:(j + 1) * 8],
                    start=(j == 0), stop=(j == i)), "pe")
            p.wait("vector", "pe", tm)
            tv = p.inc(nc.vector.tensor_tensor(
                rankb[:, i * 8:(i + 1) * 8], pA[:, 2176:2184],
                Gb[:, i * 8:(i + 1) * 8], ALU.mult), "dve")
            p.wait("vector", "dve", tv)
            tv = p.inc(nc.vector.tensor_scalar(
                rankb[:, i * 8:(i + 1) * 8], rankb[:, i * 8:(i + 1) * 8],
                -1.0, None, ALU.add), "dve")
            p.wait("tensor", "dve", tv)
        p.wait("vector", "dve", tv)
        tok_sc = 0
        for e in range(NE):
            for i in range(NT):
                tv = p.inc(nc.vector.tensor_scalar(
                    PeT[:, i * 256:(i + 1) * 256], B.iota256[:],
                    rankb[:, i * 8 + e: i * 8 + e + 1], None, ALU.is_equal),
                    "dve")
            tok_pet = tv
            p.wait("tensor", "dve", tok_pet)
            for dt in range(ND):
                p.wait("tensor", "dve", B._l3_xn_tok)
                for i in range(NT):
                    tm = p.inc(nc.tensor.matmul(
                        pY[:, 0:256],
                        B.arena[:, i * 1024 + dt * 128: i * 1024 + (dt + 1) * 128],
                        PeT[:, i * 256:(i + 1) * 256],
                        start=(i == 0), stop=(i == NT - 1)), "pe")
                p.wait("vector", "pe", tm)
                tv = p.inc(nc.vector.tensor_copy(
                    ReT[:, dt * 256:(dt + 1) * 256], pY[:, 0:256]), "dve")
                p.wait("tensor", "dve", tv)
            for half in range(2):
                tok_we = stream_cast(p, B, d["w_l3_experts"][e], half * 512, 4,
                                     0, 1024, webuf, 0, None)
                p.dwait("tensor", tok_we)
                for rt in range(2):
                    prt = pY if rt == 0 else pA
                    for dth in range(4):
                        dt = half * 4 + dth
                        for oc in range(2):
                            tm = p.inc(nc.tensor.matmul(
                                prt[:, oc * 512:(oc + 1) * 512],
                                ReT[:, dt * 256 + rt * 128:
                                    dt * 256 + rt * 128 + 128],
                                webuf[:, dth * 1024 + oc * 512:
                                      dth * 1024 + (oc + 1) * 512],
                                start=(dt == 0), stop=(dt == ND - 1)), "pe")
            p.wait("vector", "pe", tm)
            for rt in range(2):
                prt = pY if rt == 0 else pA
                for oc in range(2):
                    tv = p.inc(nc.vector.tensor_copy(
                        Ye[:, rt * 1024 + oc * 512: rt * 1024 + (oc + 1) * 512],
                        prt[:, oc * 512:(oc + 1) * 512]), "dve")
            p.wait("tensor", "dve", tv)
            for i in range(NT):
                p.wait("tensor", "dve", tok_sc)
                for rt in range(2):
                    tp = p.inc(nc.tensor.transpose(
                        pB16[:, rt * 128:(rt + 1) * 128],
                        PeT[:, i * 256 + rt * 128: i * 256 + (rt + 1) * 128],
                        B.ident), "pe")
                p.wait("vector", "pe", tp)
                tv = p.inc(nc.vector.tensor_copy(PeL[:, 0:256].rearrange(
                    "p (a c) -> p a c", c=128), pB16[:, 0:256].rearrange(
                    "p (a c) -> p a c", c=128)), "dve")
                p.wait("tensor", "dve", tv)
                for oc in range(2):
                    for rt in range(2):
                        tm = p.inc(nc.tensor.matmul(
                            pY[:, oc * 512:(oc + 1) * 512],
                            PeL[:, rt * 128:(rt + 1) * 128],
                            Ye[:, rt * 1024 + oc * 512: rt * 1024 + (oc + 1) * 512],
                            start=(rt == 0), stop=(rt == 1)), "pe")
                p.wait("vector", "pe", tm)
                for oc in range(2):
                    xs = rs(i * D + oc * 512, i * D + (oc + 1) * 512)
                    tok_sc = p.inc(nc.vector.tensor_tensor(
                        B.X[:, xs], B.X[:, xs],
                        pY[:, oc * 512:(oc + 1) * 512], ALU.add), "dve")
                p.wait("tensor", "dve", tok_sc)
    p.barrier_into(("vector", "scalar", "tensor", "gpsimd", "sync"))


def emit_l3_attn(p, B):
    nc = p.nc
    d = B.d
    scale = float(1.0 / np.sqrt(128.0))
    with (
        nc.sbuf_tensor(uname("a3_qt"), [128, 2048], BF16) as QT,
        nc.sbuf_tensor(uname("a3_kt"), [128, 2048], BF16) as KT1,
        nc.sbuf_tensor(uname("a3_vtf"), [128, 2048], F32) as VTf,
        nc.sbuf_tensor(uname("a3_vtok"), [128, NT * 128], BF16) as vtok,
        nc.sbuf_tensor(uname("a3_attnT"), [128, 1024], BF16) as attnT,
        nc.sbuf_tensor(uname("a3_pet"), [128, 2048], BF16) as peT,
        nc.sbuf_tensor(uname("a3_pe1"), [128, 1], F32) as pe1c,
        nc.sbuf_tensor(uname("a3_wbuf"), [128, 2048], BF16) as wbuf,
        nc.psum_tensor([128, 2048], F32) as pS,
        nc.psum_tensor([128, 512], F32) as pO,
        nc.psum_tensor([128, 512], F32) as pSum,
    ):
        gB3 = p.dgrp()
        p.dma("sync", peT[:], d["c_l3_peT"], gB3)
        p.dma("sync", pe1c[:], d["c_l3_pe1"], gB3)
        p.dwait("vector", gB3)
        p.dwait("scalar", gB3)
        gk = stream_cast(p, B, d["w_l3_kT"], 0, 8, 0, 128, wbuf, 0, None)
        gv = stream_cast(p, B, d["w_l3_vT"], 0, 8, 0, 128, wbuf, 1024, None)
        p.dwait("tensor", gk)
        p.dwait("tensor", gv)
        p.wait("tensor", "dve", B.tok_xnT)
        tv = 0
        for w in range(2):
            for cc in range(4):
                for dt in range(ND):
                    tm = p.inc(nc.tensor.matmul(
                        pS[:, cc * 512:(cc + 1) * 512],
                        wbuf[:, w * 1024 + dt * 128: w * 1024 + (dt + 1) * 128],
                        xnt(B, dt, rs(cc * 512, (cc + 1) * 512)),
                        start=(dt == 0), stop=(dt == ND - 1)), "pe")
                if w == 0:
                    p.wait("scalar", "pe", tm)
                    tv = p.inc(nc.scalar.activation(
                        KT1[:, cc * 512:(cc + 1) * 512],
                        pS[:, cc * 512:(cc + 1) * 512], AF.Identity,
                        bias=pe1c[:]), "act")
                else:
                    p.wait("vector", "pe", tm)
                    tv = p.inc(nc.vector.tensor_copy(
                        VTf[:, cc * 512:(cc + 1) * 512],
                        pS[:, cc * 512:(cc + 1) * 512]), "dve")
            if w == 0:
                p.wait("tensor", "act", tv)
            else:
                p.wait("tensor", "dve", tv)
        tok_kv = tv
        tvt = emit_vtok(p, B, VTf, vtok, pS, 1, 128, 128, tok_kv)
        p.wait("tensor", "dve", tvt)
        for h in range(8):
            tok_wq = stream_cast(p, B, d["w_l3_qT"], 0, 8, h * 128, 128, wbuf,
                                 0, None)
            p.dwait("tensor", tok_wq)
            p.wait("tensor", "dve", B._tok_ctx_evac)
            for cc in range(4):
                for dt in range(ND):
                    tm = p.inc(nc.tensor.matmul(
                        pS[:, cc * 512:(cc + 1) * 512],
                        wbuf[:, dt * 128:(dt + 1) * 128],
                        xnt(B, dt, rs(cc * 512, (cc + 1) * 512)),
                        start=(dt == 0), stop=(dt == ND - 1)), "pe")
                p.wait("vector", "pe", tm)
                tv = p.inc(nc.vector.tensor_tensor(
                    QT[:, cc * 512:(cc + 1) * 512],
                    pS[:, cc * 512:(cc + 1) * 512],
                    peT[:, cc * 512:(cc + 1) * 512], ALU.add), "dve")
            tok_q = tv
            p.wait("tensor", "dve", tok_q)
            tok_qkv_pe = p.cnt["pe"]
            tnb = emit_negB_full(p, B, QT, KT1, pS, scale, 100, tok_qkv_pe,
                                 tok_q)
            p.wait("scalar", "dve", tnb)
            p.wait("tensor", "dve", tnb)
            tok_exp = {}
            tok_av3 = {}
            for c in range(NC2):
                q0 = c * 512
                jmax = 4 * c + 3
                t_prev = B._tok_ctx_evac

                def emit_av3(j, par, qlo, tmask):
                    p.wait("tensor", "act", tok_exp[(par, j)])
                    if tmask:
                        p.wait("tensor", "dve", tmask)
                    ta = p.inc(nc.tensor.matmul(
                        pO[0:128, qlo:512],
                        vtok[:, j * 128:(j + 1) * 128],
                        attnT[:, par * 512 + qlo: par * 512 + 512],
                        start=(j == 0), stop=(j == jmax)), "pe")
                    ta = p.inc(nc.tensor.matmul(
                        pSum[0:1, qlo:512],
                        B.ones[:, 0:1],
                        attnT[:, par * 512 + qlo: par * 512 + 512],
                        start=(j == 0), stop=(j == jmax)), "pe")
                    tok_av3[(par, j)] = ta
                    return ta

                pend = None
                for j in range(jmax + 1):
                    par = j % 2
                    qlo = max(0, 128 * j - q0)
                    p.wait("tensor", "act", tok_exp.get((par, j - 2), 0))
                    if j == 0:
                        p.wait("tensor", "dve", t_prev)
                    diag = 128 * j >= q0
                    tm = p.inc(nc.tensor.matmul(
                        pS[:, par * 512 + qlo: par * 512 + 512],
                        KT1[:, ts(j, 128)],
                        QT[:, rs(q0 + qlo, q0 + 512)],
                        start=True, stop=not diag), "pe")
                    if diag:
                        tm = p.inc(nc.tensor.matmul(
                            pS[:, par * 512 + qlo: par * 512 + qlo + 128],
                            B.ident, B.negtri,
                            start=False, stop=True), "pe")
                    tok_s = tm
                    if pend is not None:
                        emit_av3(*pend)
                    p.wait("scalar", "pe", tok_s)
                    p.wait("scalar", "pe", tok_av3.get((par, j - 2), 0))
                    te = p.inc(nc.scalar.activation(
                        attnT[:, par * 512 + qlo: par * 512 + 512],
                        pS[:, par * 512 + qlo: par * 512 + 512],
                        AF.Exp, scale=scale,
                        bias=B.colbuf[:, 100:101]), "act")
                    tok_exp[(par, j)] = te
                    pend = (j, par, qlo, 0)
                ta = emit_av3(*pend)
                p.wait("vector", "pe", ta)
                tv = p.inc(nc.vector.reciprocal(B.scr_f2[0:1, 0:512],
                                                pSum[0:1, 0:512]), "dve")
                p.wait("vector", "dve", tv)
                tv = p.inc(nc.vector.tensor_copy(B.scr_f1[:, 0:512],
                                                 pO[:, 0:512]), "dve")
                p.wait("tensor", "dve", tv)
                p.wait("tensor", "act", tok_exp[(jmax % 2, jmax)])
                tb = p.inc(nc.tensor.matmul(
                    pS[:, 0:512], B.onesf[0:1, 0:128], B.scr_f2[0:1, 0:512],
                    start=True, stop=True), "pe")
                p.wait("vector", "pe", tb)
                tv = p.inc(nc.vector.tensor_tensor(
                    B.scr_f1[:, 0:512], B.scr_f1[:, 0:512], pS[:, 0:512],
                    ALU.mult), "dve")
                p.wait("vector", "dve", tv)
                tv = p.inc(nc.vector.tensor_copy(
                    B.arena[:, h * 2048 + c * 512: h * 2048 + (c + 1) * 512],
                    B.scr_f1[:, 0:512]), "dve")
                B._tok_ctx_evac = p.cnt["dve"]
        tok_x = 0
        for quarter in range(4):
            tok_wp = stream_cast(p, B, d["w_l3_oT"], quarter * 256, 2, 0,
                                 1024, wbuf, 0, None)
            p.dwait("tensor", tok_wp)
            p.wait("tensor", "dve", B._tok_ctx_evac)
            for i in range(NT):
                for oc in range(2):
                    for hpq in range(2):
                        hp = quarter * 2 + hpq
                        tm = p.inc(nc.tensor.matmul(
                            pS[:, oc * 512: oc * 512 + 512],
                            B.arena[:, hp * 2048 + i * 128: hp * 2048 + i * 128 + 128],
                            wbuf[:, hpq * 1024 + oc * 512: hpq * 1024 + (oc + 1) * 512],
                            start=(hpq == 0), stop=(hpq == 1)), "pe")
                    p.wait("vector", "pe", tm)
                    tok_x = p.inc(nc.vector.tensor_tensor(
                        B.X[:, i * D + oc * 512: i * D + (oc + 1) * 512],
                        B.X[:, i * D + oc * 512: i * D + (oc + 1) * 512],
                        pS[:, oc * 512: oc * 512 + 512], ALU.add), "dve")
                    p.wait("tensor", "dve", tok_x)
    B._tok_ctx_evac = 0
    p.barrier_into(("vector", "scalar", "tensor", "gpsimd", "sync"))



def _run(common, per_core, layer_groups, x0=None):
    x = x0
    for layers in layer_groups:
        nc = build_program(common, layers)
        in_maps = []
        for b in range(8):
            m = {k: v for k, v in common.items()}
            m["x_in"] = per_core[b]["x_in"] if x is None else x[b]
            in_maps.append(m)
        res = run_bass_kernel_spmd(nc, in_maps, core_ids=list(range(8)))
        x = [np.ascontiguousarray(res.results[b]["x_out"]) for b in range(8)]
    return np.stack(x, axis=0)


def kernel(**inputs):
    common, per_core = host_prep(inputs)
    out = _run(common, per_core, [[0, 1, 2, 3]])
    return out.astype(np.float32)


# version 21
# speedup vs baseline: 1.5160x; 1.0095x over previous
import sys
import numpy as np
import ml_dtypes

sys.path.insert(0, "/opt/trn_rl_repo")
import concourse.bass as bass
import concourse.mybir as mybir
from concourse.bass_utils import run_bass_kernel_spmd

F32 = mybir.dt.float32
BF16 = mybir.dt.bfloat16
I32 = mybir.dt.int32
AF = mybir.ActivationFunctionType
ALU = mybir.AluOpType
AX = mybir.AxisListType
bf16 = ml_dtypes.bfloat16

T = 2048
D = 1024
NT = T // 128
ND = D // 128
NC2 = T // 512
LN_EPS = 1e-5
DEBUG_SKIP = set()
_NAME_CTR = [0]


def uname(s):
    _NAME_CTR[0] += 1
    return "%s_%d" % (s, _NAME_CTR[0])
NDMA_SEMS = 48
STRICT_DMA = True
FFN_BAR = 0


class Prog:
    def __init__(self, nc, sems):
        self.nc = nc
        self.sems = sems
        self.cnt = {k: 0 for k in sems}
        self.hw = {}

    def eng(self, name):
        return getattr(self.nc, name)

    ENG_OF = {"dve": "vector", "act": "scalar", "gp": "gpsimd"}

    def inc(self, instr, sem, n=1):
        instr.then_inc(self.sems[sem], n)
        self.cnt[sem] += n
        eng = self.ENG_OF.get(sem)
        if eng is not None:
            self.wait(eng, sem, self.cnt[sem])
        return self.cnt[sem]

    def wait(self, eng, sem, tgt):
        if not tgt or tgt <= 0:
            return
        key = (eng, sem)
        if self.hw.get(key, -1) >= tgt:
            return
        self.eng(eng).wait_ge(self.sems[sem], tgt)
        self.hw[key] = tgt

    def barrier_into(self, engines):
        for c in engines:
            for s in self.cnt:
                self.wait(c, s, self.cnt[s])

    def dgrp(self, q="sync"):
        half = NDMA_SEMS // 2
        if q == "sync":
            g = self.rr % half
            self.rr += 1
        else:
            g = half + self.rrg % half
            self.rrg += 1
        return g

    def dma(self, q, out, in_, g):
        half = NDMA_SEMS // 2
        assert (g < half) == (q == "sync"), (q, g)
        instr = self.eng(q).dma_start(out=out, in_=in_)
        self.inc(instr, "d%d" % g, 16)
        return g

    def dwait(self, eng, g):
        self.wait(eng, "d%d" % g, self.cnt["d%d" % g])


def ts(i, n=128):
    return slice(i * n, (i + 1) * n)


def rs(a, b):
    return slice(a, b)



def host_prep(inputs):
    c = {}
    f32 = np.float32

    def asf(x):
        return np.ascontiguousarray(np.asarray(x), dtype=f32)

    c["c_ident"] = np.eye(128, dtype=np.float32).astype(bf16)
    c["c_identf"] = np.eye(128, dtype=np.float32)
    c["c_triu"] = np.triu(np.ones((128, 128), dtype=np.float32)).astype(bf16)
    c["c_negtri"] = (-240.0 * (1.0 - np.triu(np.ones((128, 128),
                     dtype=np.float32)))).astype(bf16)
    c["c_ones"] = np.ones((128, 128), dtype=np.float32).astype(bf16)
    c["c_onesf"] = np.ones((8, 128), dtype=np.float32)
    c["c_iota256"] = np.broadcast_to(np.arange(256, dtype=f32), (128, 256)).copy()
    dup = np.zeros((64, 128), dtype=np.float32)
    for j in range(128):
        dup[j % 64, j] = 1.0
    c["c_dup"] = dup.astype(bf16)

    c["w_l0_wq"] = asf(inputs["L0_attn_Wq"])
    c["w_l0_wk"] = asf(inputs["L0_attn_Wk"])
    c["w_l0_wv"] = asf(inputs["L0_attn_Wv"])
    for nm, key in (("w_l0_bq", "L0_attn_bq"), ("w_l0_bk", "L0_attn_bk"),
                    ("w_l0_bv", "L0_attn_bv")):
        c[nm] = np.ascontiguousarray(asf(inputs[key]).reshape(8, 128).T)
    c["b_l0_bo_row"] = asf(inputs["L0_attn_bo"]).reshape(1, 1024).astype(bf16)
    for nm, key in (("w_l0_aq", "L0_attn_alpha_q"), ("w_l0_ak", "L0_attn_alpha_k"),
                    ("w_l0_av", "L0_attn_alpha_v")):
        a = asf(inputs[key])
        ar = np.repeat(a.transpose(1, 2, 0), 64, axis=2)
        c[nm] = np.ascontiguousarray(
            ar.reshape(16, 2, 8, 128).transpose(3, 0, 1, 2).reshape(128, 256))
    e64 = np.eye(64, dtype=np.float32)
    c["c_p64"] = np.tile(np.concatenate([e64, e64], axis=1), (16, 1)).astype(bf16)
    col = asf(inputs["L0_attn_collapse"])
    c["w_l0_col"] = np.ascontiguousarray(
        np.repeat(col[:, :, None], 64, axis=2).transpose(2, 0, 1).reshape(64, 32))
    c["w_l0_woT"] = np.ascontiguousarray(asf(inputs["L0_attn_Wo"]).T)
    c["w_l0_gT"] = np.ascontiguousarray(np.asarray(inputs["L0_ffn_w_gate"]).T)
    c["w_l0_uT"] = np.ascontiguousarray(np.asarray(inputs["L0_ffn_w_up"]).T)
    c["w_l0_dT"] = np.ascontiguousarray(np.asarray(inputs["L0_ffn_w_down"]).T)

    c["w_l1_attnT"] = np.ascontiguousarray(asf(inputs["L1_attn_W_attn"]).T)
    c["w_l1_projT"] = np.ascontiguousarray(asf(inputs["L1_attn_W_proj"]).T)
    c["w_l1_win"] = asf(inputs["L1_ffn_Win"])
    c["w_l1_bin"] = asf(inputs["L1_ffn_bin_"]).reshape(64, 1)
    R = np.zeros((64, 1024), dtype=np.float32)
    for m in range(64):
        R[m, m * 16:(m + 1) * 16] = 1.0
    c["c_l1_R"] = R.astype(bf16)
    c["c_l1_w2mask"] = np.ascontiguousarray(R.T).astype(bf16)
    c["w_l1_w1col"] = np.ascontiguousarray(
        asf(inputs["L1_ffn_W1"]).reshape(8, 128).T)
    c["w_l1_b1col"] = np.ascontiguousarray(
        asf(inputs["L1_ffn_b1"]).reshape(8, 128).T)
    c["w_l1_w2col"] = np.ascontiguousarray(
        asf(inputs["L1_ffn_W2"]).reshape(8, 128).T)
    c["w_l1_b2"] = asf(inputs["L1_ffn_b2"]).reshape(64, 1)
    c["w_l1_wout"] = asf(inputs["L1_ffn_Wout"])
    c["b_l1_bout_row"] = asf(inputs["L1_ffn_bout"]).reshape(1, 1024).astype(bf16)

    c["w_l2_qT"] = np.ascontiguousarray(
        asf(inputs["L2_attn_WQ"]).transpose(2, 0, 1).reshape(1024, 1024))
    c["w_l2_kT"] = np.ascontiguousarray(
        asf(inputs["L2_attn_WK"]).transpose(2, 0, 1).reshape(1024, 1024))
    c["w_l2_vT"] = np.ascontiguousarray(
        asf(inputs["L2_attn_WV"]).transpose(2, 0, 1).reshape(1024, 1024))
    c["w_l2_oT"] = np.ascontiguousarray(
        asf(inputs["L2_attn_WO"]).transpose(0, 2, 1).reshape(1024, 1024))
    c["w_l2_w1"] = asf(inputs["L2_ffn_w1"][0])
    c["w_l2_w2"] = asf(inputs["L2_ffn_w2"][0])
    c["w_l2_b1col"] = np.ascontiguousarray(
        asf(inputs["L2_ffn_b1"][0]).reshape(32, 128).T)
    c["b_l2_b2_row"] = asf(inputs["L2_ffn_b2"][0]).reshape(1, 1024).astype(bf16)
    c["b_l2_gamma_row"] = asf(inputs["L2_ffn_gamma"]).reshape(1, 1024).astype(bf16)
    c["b_l2_beta_row"] = asf(inputs["L2_ffn_beta"]).reshape(1, 1024).astype(bf16)

    c["w_l3_qT"] = np.ascontiguousarray(asf(inputs["L3_attn_Wq"]).T)
    c["w_l3_kT"] = np.ascontiguousarray(asf(inputs["L3_attn_Wk"]).T)
    c["w_l3_vT"] = np.ascontiguousarray(asf(inputs["L3_attn_Wv"]).T)
    c["w_l3_oT"] = np.ascontiguousarray(asf(inputs["L3_attn_Wo"]).T)
    c["w_l3_gate"] = asf(inputs["L3_ffn_gate_w"][0])
    c["w_l3_experts"] = asf(inputs["L3_ffn_experts"])

    def sinus(length, dim):
        pos = np.arange(length, dtype=np.float64)[:, None]
        inv = np.exp(-np.log(10000.0) *
                     (np.arange(0, dim, 2, dtype=np.float64) / dim))
        ang = pos * inv[None, :]
        pe = np.zeros((length, dim), dtype=np.float64)
        pe[:, 0::2] = np.sin(ang)
        pe[:, 1::2] = np.cos(ang)
        return pe.astype(np.float32)
    c["c_l3_peT"] = np.ascontiguousarray(sinus(T, 128).T).astype(bf16)
    c["c_l3_pe1"] = sinus(1, 128).reshape(128, 1)

    x = np.ascontiguousarray(np.asarray(inputs["L0_attn_hidden_states"]),
                             dtype=f32)
    per_core = [{"x_in": x[b]} for b in range(8)]
    return c, per_core



class Bufs:
    pass


NEEDED_PREFIX = {0: ("w_l0",), 1: ("w_l1",), 2: ("w_l2",), 3: ("w_l3",)}


def declare_io(nc, common, layers):
    d = {}
    need = set()
    for k in common:
        if k.startswith(("c_", "b_")):
            need.add(k)
        for l in layers:
            for pre in NEEDED_PREFIX[l]:
                if k.startswith(pre):
                    need.add(k)
    for k in sorted(need):
        a = common[k]
        dt = {np.dtype(np.float32): F32, np.dtype(np.int32): I32,
              np.dtype(bf16): BF16}[a.dtype]
        d[k] = nc.dram_tensor(k, list(a.shape), dt, kind="ExternalInput").ap()
    d["x_in"] = nc.dram_tensor("x_in", [T, D], F32, kind="ExternalInput").ap()
    d["x_out"] = nc.dram_tensor("x_out", [T, D], F32, kind="ExternalOutput").ap()
    if 0 in layers:
        d["l0_wmix"] = nc.dram_tensor("l0_wmix", [48, 1024, 128], BF16,
                                      kind="Internal").ap()
    return d


def build_program(common, layers):
    import contextlib
    nc = bass.Bass("TRN2", target_bir_lowering=False)
    d = declare_io(nc, common, layers)
    _stk = contextlib.ExitStack()
    _dsems = [_stk.enter_context(nc.semaphore("d%d" % i))
              for i in range(NDMA_SEMS)]

    with (
        nc.sbuf_tensor(uname("X"), [128, NT * D], F32) as X,
        nc.sbuf_tensor(uname("xnT"), [128, ND * T], BF16) as xnT,
        nc.sbuf_tensor(uname("arena"), [128, 16896], BF16) as arena,
        nc.sbuf_tensor(uname("stats"), [128, 64], F32) as stats,
        nc.sbuf_tensor(uname("consts"), [128, 640], BF16) as consts,
        nc.sbuf_tensor(uname("identf"), [128, 128], F32) as identf,
        nc.sbuf_tensor(uname("onesf"), [8, 128], F32) as onesf,
        nc.sbuf_tensor(uname("iota256"), [128, 256], F32) as iota256,
        nc.sbuf_tensor(uname("colbuf"), [128, 128], F32) as colbuf,
        nc.sbuf_tensor(uname("scr_bf"), [128, 2048], BF16) as scr_bf,
        nc.sbuf_tensor(uname("scr_f1"), [128, 1024], F32) as scr_f1,
        nc.sbuf_tensor(uname("scr_f2"), [128, 1024], F32) as scr_f2,
        nc.semaphore(uname("dW")) as _s0,
        nc.semaphore(uname("dX")) as _s1,
        nc.semaphore(uname("pe")) as _s2,
        nc.semaphore(uname("act")) as _s3,
        nc.semaphore(uname("dve")) as _s4,
        nc.semaphore(uname("gp")) as _s5,
        nc.Block() as block,
    ):
        sems = {"dW": _s0, "dX": _s1, "pe": _s2, "act": _s3,
                "dve": _s4, "gp": _s5}
        for gi, h in enumerate(_dsems):
            sems["d%d" % gi] = h
        p = Prog(nc, sems)
        p.rr = 0
        p.rrg = 0
        B = Bufs()
        B.X, B.xnT, B.arena, B.stats = X, xnT, arena, stats
        B.consts, B.iota256, B.colbuf, B.onesf = consts, iota256, colbuf, onesf
        B.identf = identf
        B.scr_bf, B.scr_f1, B.scr_f2 = scr_bf, scr_f1, scr_f2
        B.d = d
        B.ident = consts[:, 0:128]
        B.triu = consts[:, 128:256]
        B.ones = consts[:, 256:384]
        B.dup = consts[0:64, 384:512]
        B.negtri = consts[:, 512:640]
        B._tok_ctx_evac = 0
        B._stage_toks = {}
        B._stage_grps = {}
        B._stream_grps = {}

        B.gC = p.dgrp()
        for nm, dst in (("c_ident", B.ident), ("c_triu", B.triu),
                        ("c_ones", B.ones), ("c_dup", B.dup),
                        ("c_identf", identf[:]), ("c_onesf", onesf[:]),
                        ("c_negtri", B.negtri),
                        ("c_iota256", iota256[:])):
            p.dma("sync", dst, d[nm], B.gC)
        B.epscol = colbuf[:, 127:128]
        te = p.inc(nc.vector.memset(B.epscol, LN_EPS), "dve")
        B.tok_eps = te
        B.gX = p.dgrp()
        xr = d["x_in"].rearrange("(n p) d -> n p d", p=128)
        for i in range(NT):
            p.dma("sync", X[:, ts(i, D)], xr[i], B.gX)

        for li in layers:
            p.barrier_into(("vector", "scalar", "tensor", "gpsimd", "sync"))
            emit_layernorm(p, B)
            if li == 0:
                if "l0ffn" not in DEBUG_SKIP:
                    emit_l0_ffn(p, B)
                if "l0attn" not in DEBUG_SKIP:
                    emit_l0_attn(p, B)
            elif li == 1:
                emit_l1_ffn(p, B)
                emit_l1_attn(p, B)
            elif li == 2:
                emit_l2_ffn(p, B)
                emit_l2_attn(p, B)
            else:
                emit_l3_ffn(p, B)
                emit_l3_attn(p, B)

        p.barrier_into(("sync",))
        gOut = p.dgrp()
        xo = d["x_out"].rearrange("(n p) d -> n p d", p=128)
        for i in range(NT):
            p.dma("sync", xo[i], X[:, ts(i, D)], gOut)
        p.dwait("sync", gOut)
    _stk.close()
    return nc



def emit_layernorm(p, B):
    nc = p.nc
    X, stats = B.X, B.stats
    p.dwait("scalar", B.gX)
    p.dwait("scalar", B.gC)
    p.dwait("tensor", B.gC)
    p.dwait("vector", B.gC)
    mean, rstd = stats[:, 0:16], stats[:, 16:32]
    sums, negmean = stats[:, 32:48], stats[:, 48:64]
    for i in range(NT):
        t = p.inc(nc.scalar.activation(B.scr_bf[:, 0:1024], X[:, ts(i, D)],
                                       AF.Identity, accum_out=sums[:, i:i + 1]),
                  "act")
        p.wait("scalar", "act", t)
    p.wait("vector", "act", t)
    t = p.inc(nc.vector.tensor_scalar(mean, sums, 1.0 / D, None, ALU.mult), "dve")
    p.wait("vector", "dve", t)
    t = p.inc(nc.vector.tensor_scalar(negmean, mean, -1.0, None, ALU.mult), "dve")
    p.wait("scalar", "dve", t)
    for i in range(NT):
        t = p.inc(nc.scalar.activation(B.scr_bf[:, 0:1024], X[:, ts(i, D)],
                                       AF.Square, bias=negmean[:, i:i + 1],
                                       accum_out=sums[:, i:i + 1]), "act")
        p.wait("scalar", "act", t)
    p.wait("vector", "act", t)
    t = p.inc(nc.vector.tensor_scalar(rstd, sums, 1.0 / D, None, ALU.mult), "dve")
    p.wait("scalar", "dve", t)
    p.wait("scalar", "dve", B.tok_eps)
    t = p.inc(nc.scalar.activation(rstd, rstd, AF.Ln, bias=B.epscol), "act")
    p.wait("scalar", "act", t)
    t = p.inc(nc.scalar.activation(rstd, rstd, AF.Exp, scale=-0.5), "act")
    B.tok_lnstats = t
    p.wait("vector", "act", t)

    with nc.psum_tensor([128, 1024], BF16) as pt:
        tv2 = 0
        for i in range(NT):
            tv = p.inc(nc.vector.tensor_scalar(
                B.scr_bf[:, 0:1024], X[:, ts(i, D)], mean[:, i:i + 1],
                rstd[:, i:i + 1], ALU.subtract, ALU.mult), "dve")
            p.wait("tensor", "dve", tv)
            for j in range(ND):
                tp = p.inc(nc.tensor.transpose(pt[:, ts(j, 128)],
                                               B.scr_bf[:, ts(j, 128)],
                                               B.ident), "pe")
            p.wait("vector", "pe", tp)
            dst = B.xnT[:].rearrange("p (j t) -> p j t", t=T)[:, :, ts(i, 128)]
            src = pt[:].rearrange("p (j t) -> p j t", t=128)
            tv2 = p.inc(nc.vector.tensor_copy(dst, src), "dve")
            p.wait("tensor", "dve", tv2)
    B.tok_xnT = tv2


def xnt(B, dtile, cols):
    return B.xnT[:, dtile * T + cols.start: dtile * T + cols.stop]



def stream_cast(p, B, dram, rows0, nrt, cols0, ncols, dst, dst_off, staging,
                war_pe=None):
    key = (id(dst), dst_off)
    grp = B._stream_grps.get(key)
    if grp is None:
        grp = p.dgrp("gpsimd")
        B._stream_grps[key] = grp
    war = p.cnt["pe"] if war_pe is None else war_pe
    p.wait("gpsimd", "pe", war)
    p.wait("sync", "pe", war)
    for j in range(nrt):
        src = dram[rows0 + j * 128: rows0 + (j + 1) * 128, cols0:cols0 + ncols]
        p.dma("gpsimd", dst[:, dst_off + j * ncols: dst_off + (j + 1) * ncols],
              src, grp)
    return grp



def emit_negB_pair(p, B, QT, KT, ps, scale, col_q, tok_ready_pe, tok_ready_dve):
    nc = p.nc
    tdv = 0
    for h in range(2):
        hs = ts(h, 64)
        maxs = []
        tm_prev = tok_ready_pe
        for w, MT in ((0, QT), (1, KT)):
            p.wait("scalar", "pe", tm_prev)
            p.wait("scalar", "dve", max(tok_ready_dve, tdv))
            ta = p.inc(nc.scalar.activation(B.scr_bf[0:64, 0:2048], MT[hs, :],
                                            AF.Square), "act")
            p.wait("tensor", "act", ta)
            p.wait("tensor", "dve", tdv)
            for cc in range(4):
                tm = p.inc(nc.tensor.matmul(
                    ps[:, cc * 512:(cc + 1) * 512],
                    B.ones[0:64, :],
                    B.scr_bf[0:64, cc * 512:(cc + 1) * 512],
                    start=True, stop=True), "pe")
            tm_prev = tm
            p.wait("vector", "pe", tm)
            for cc in range(4):
                tdv = p.inc(nc.vector.tensor_reduce(
                    B.colbuf[:, 122 + cc: 123 + cc],
                    ps[:, cc * 512:(cc + 1) * 512], AX.X, ALU.max), "dve")
            p.wait("vector", "dve", tdv)
            cdst = B.colbuf[:, 120 + w: 121 + w]
            tdv = p.inc(nc.vector.tensor_reduce(cdst, B.colbuf[:, 122:126],
                                                AX.X, ALU.max), "dve")
            p.wait("vector", "dve", tdv)
            if getattr(B, "dbg_negb", None):
                B.dbg_negb(p, B, h, w, ps)
        c0 = B.colbuf[:, 120:121]
        c1 = B.colbuf[:, 121:122]
        tdv = p.inc(nc.vector.tensor_tensor(c0, c0, c1, ALU.mult), "dve")
        p.wait("scalar", "dve", tdv)
        tac = p.inc(nc.scalar.activation(c0, c0, AF.Ln), "act")
        p.wait("scalar", "act", tac)
        tac = p.inc(nc.scalar.activation(c0, c0, AF.Exp, scale=0.5), "act")
        p.wait("vector", "act", tac)
        tdv = p.inc(nc.vector.tensor_scalar(
            B.colbuf[:, col_q + h: col_q + h + 1], c0, -scale, None, ALU.mult),
            "dve")
    return tdv


def emit_vtok(p, B, VTf, vtok, pOf, nheads, vdim, width, tok_src_dve):
    nc = p.nc
    tdv = 0
    for h in range(nheads):
        p.wait("tensor", "dve", max(tok_src_dve, tdv))
        for j in range(NT):
            tp = p.inc(nc.tensor.transpose(
                pOf[:, j * vdim:(j + 1) * vdim],
                VTf[h * vdim:(h + 1) * vdim, ts(j, 128)],
                B.identf[h * vdim: h * vdim + vdim,
                         h * vdim: h * vdim + vdim]), "pe")
        p.wait("vector", "pe", tp)
        if getattr(B, "dbg_vtok", None):
            B.dbg_vtok(p, B, h, pOf)
        base = h * NT * width
        nj_half = (512 // vdim)
        for bk in range((NT * vdim) // 512):
            dst = vtok[:, base + bk * nj_half * width:
                       base + (bk + 1) * nj_half * width].rearrange(
                "p (j c) -> p j c", c=width)[:, :, 0:vdim]
            srcb = pOf[:, bk * 512:(bk + 1) * 512].rearrange(
                "p (j c) -> p j c", c=vdim)
            tdv = p.inc(nc.vector.tensor_copy(dst, srcb), "dve")
        if width > vdim:
            ocol = vtok[:, base: base + NT * width].rearrange(
                "p (j c) -> p j c", c=width)[:, :, vdim:vdim + 1]
            tdv = p.inc(nc.vector.memset(ocol, 1.0), "dve")
    return tdv


def attn_chunks_pair(p, B, QT, KT, vtok, attnT, pS, pO, col_q, scale,
                     ctx_store, vwidth=66):
    nc = p.nc
    tok_exp = {}
    tok_av = {}
    for c in range(NC2):
        q0 = c * 512
        jmax = 4 * c + 3
        t_prev_ep = B._tok_ctx_evac

        def emit_av(j, par, qlo, tmask):
            p.wait("tensor", "act", tok_exp[(par, j)])
            if tmask:
                p.wait("tensor", "dve", tmask)
            ta = 0
            for h in range(2):
                for strip in range(2):
                    ta = p.inc(nc.tensor.matmul(
                        pO[0:65, (h * 2 + strip) * 512 + qlo:
                           (h * 2 + strip) * 512 + 512],
                        vtok[ts(strip, 64),
                             h * NT * vwidth + j * vwidth:
                             h * NT * vwidth + j * vwidth + 65],
                        attnT[ts(strip, 64), (par * 2 + h) * 512 + qlo:
                              (par * 2 + h) * 512 + 512],
                        start=(j == 0), stop=(j == jmax),
                        tile_position=(strip * 64, 0)), "pe")
            tok_av[(par, j)] = ta
            return ta

        pend = None
        for j in range(jmax + 1):
            par = j % 2
            qlo = max(0, 128 * j - q0)
            p.wait("tensor", "act", tok_exp.get((par, j - 2), 0))
            if j == 0:
                p.wait("tensor", "dve", t_prev_ep)
            diag = 128 * j >= q0
            for h in range(2):
                tm = p.inc(nc.tensor.matmul(
                    pS[:, (par * 2 + h) * 512 + qlo: (par * 2 + h) * 512 + 512],
                    KT[ts(h, 64), ts(j, 128)],
                    QT[ts(h, 64), rs(q0 + qlo, q0 + 512)],
                    start=True, stop=not diag,
                    tile_position=(h * 64, 0)), "pe")
                if diag:
                    tm = p.inc(nc.tensor.matmul(
                        pS[:, (par * 2 + h) * 512 + qlo:
                           (par * 2 + h) * 512 + qlo + 128],
                        B.ident, B.negtri,
                        start=False, stop=True,
                        tile_position=(0, 0)), "pe")
            tok_s = tm
            if pend is not None:
                emit_av(*pend)
            p.wait("scalar", "pe", tok_s)
            p.wait("scalar", "pe", tok_av.get((par, j - 2), 0))
            pin = pS[:, par * 1024: par * 1024 + 1024].rearrange(
                "p (h q) -> p h q", q=512)[:, :, qlo:512]
            pout = attnT[:, par * 1024: par * 1024 + 1024].rearrange(
                "p (h q) -> p h q", q=512)[:, :, qlo:512]
            te = p.inc(nc.scalar.activation(
                pout, pin, AF.Exp, scale=scale,
                bias=B.colbuf[:, col_q: col_q + 1]), "act")
            tok_exp[(par, j)] = te
            pend = (j, par, qlo, 0)
        ta = emit_av(*pend)
        p.wait("vector", "pe", ta)
        for h in range(2):
            tv = p.inc(nc.vector.tensor_copy(
                B.scr_f1[0:65, 0:512],
                pO[0:65, (h * 2) * 512:(h * 2) * 512 + 512]), "dve")
            p.wait("vector", "dve", tv)
            tv = p.inc(nc.vector.tensor_tensor(
                B.scr_f1[0:65, 0:512], B.scr_f1[0:65, 0:512],
                pO[0:65, (h * 2 + 1) * 512:(h * 2 + 1) * 512 + 512], ALU.add),
                "dve")
            p.wait("vector", "dve", tv)
            tv = p.inc(nc.vector.reciprocal(B.scr_f2[0:1, 0:512],
                                            B.scr_f1[64:65, 0:512]), "dve")
            p.wait("tensor", "dve", tv)
            p.wait("tensor", "act", tok_exp[(jmax % 2, jmax)])
            tb = p.inc(nc.tensor.matmul(
                pS[0:64, 0:512], B.onesf[0:1, 0:64], B.scr_f2[0:1, 0:512],
                start=True, stop=True), "pe")
            p.wait("vector", "pe", tb)
            tv = p.inc(nc.vector.tensor_tensor(
                B.scr_f1[0:64, 0:512], B.scr_f1[0:64, 0:512], pS[0:64, 0:512],
                ALU.mult), "dve")
            p.wait("vector", "dve", tv)
            ctx_store(h, c)
            p.wait("vector", "dve", p.cnt["dve"])
        B._tok_ctx_evac = p.cnt["dve"]



def emit_l1_attn(p, B):
    emit_std_attn(p, B,
                  qkv=[("w_l1_attnT", lambda hp: hp * 128),
                       ("w_l1_attnT", lambda hp: 1024 + hp * 128),
                       ("w_l1_attnT", lambda hp: 2048 + hp * 128)],
                  wo_key="w_l1_projT")


def emit_l2_attn(p, B):
    emit_std_attn(p, B,
                  qkv=[("w_l2_qT", lambda hp: hp * 128),
                       ("w_l2_kT", lambda hp: hp * 128),
                       ("w_l2_vT", lambda hp: hp * 128)],
                  wo_key="w_l2_oT")


def emit_std_attn(p, B, qkv, wo_key):
    nc = p.nc
    d = B.d
    scale = 0.125
    with (
        nc.sbuf_tensor(uname("sa_qt"), [128, 2048], BF16) as QT,
        nc.sbuf_tensor(uname("sa_kt"), [128, 2048], BF16) as KT,
        nc.sbuf_tensor(uname("sa_vtf"), [128, 2048], F32) as VTf,
        nc.sbuf_tensor(uname("sa_vtok"), [128, 2 * NT * 66], BF16) as vtok,
        nc.sbuf_tensor(uname("sa_attnT"), [128, 2048], BF16) as attnT,
        nc.sbuf_tensor(uname("sa_wbuf"), [128, 3 * 1024], BF16) as wbuf,
        nc.psum_tensor([128, 2048], F32) as pS,
        nc.psum_tensor([128, 2048], F32) as pO,
    ):
        tok_proj_prev = 0
        for hp in range(8):
            grps = []
            for w in range(3):
                key, coloff = qkv[w]
                grps.append(stream_cast(p, B, d[key], 0, 8, coloff(hp), 128,
                                        wbuf, w * 1024, None,
                                        war_pe=tok_proj_prev))
            for gw in grps:
                p.dwait("tensor", gw)
            p.wait("tensor", "dve", B._tok_ctx_evac)
            tv = 0
            for w in range(3):
                for cc in range(4):
                    for dt in range(ND):
                        tm = p.inc(nc.tensor.matmul(
                            pS[:, cc * 512:(cc + 1) * 512],
                            wbuf[:, w * 1024 + dt * 128: w * 1024 + (dt + 1) * 128],
                            xnt(B, dt, rs(cc * 512, (cc + 1) * 512)),
                            start=(dt == 0), stop=(dt == ND - 1)), "pe")
                    p.wait("vector", "pe", tm)
                    if w < 2:
                        dstT = QT if w == 0 else KT
                        tv = p.inc(nc.vector.tensor_copy(
                            dstT[:, cc * 512:(cc + 1) * 512],
                            pS[:, cc * 512:(cc + 1) * 512]), "dve")
                    else:
                        tv = p.inc(nc.vector.tensor_copy(
                            VTf[:, cc * 512:(cc + 1) * 512],
                            pS[:, cc * 512:(cc + 1) * 512]), "dve")
                p.wait("tensor", "dve", tv)
            tok_qkv_pe = p.cnt["pe"]
            tok_proj_prev = tok_qkv_pe
            tok_qkv = tv
            tvt = emit_vtok(p, B, VTf, vtok, pO, 2, 64, 66, tok_qkv)
            tnb = emit_negB_pair(p, B, QT, KT, pS, scale, 60, tok_qkv_pe,
                                 tok_qkv)
            p.wait("vector", "dve", tnb)
            tnb = p.inc(nc.vector.tensor_tensor(
                B.colbuf[:, 62:63], B.colbuf[:, 60:61], B.colbuf[:, 61:62],
                ALU.min), "dve")
            p.wait("vector", "dve", tnb)
            p.wait("tensor", "dve", max(tnb, tvt))
            p.wait("scalar", "dve", max(tnb, tvt))

            def store(h, c, hp=hp):
                dst = B.arena[ts(h, 64),
                              hp * 2048 + c * 512: hp * 2048 + (c + 1) * 512]
                p.inc(nc.vector.tensor_copy(dst, B.scr_f1[0:64, 0:512]), "dve")
            if getattr(B, "dbg_attn_pre", None):
                B.dbg_attn_pre(p, B, hp, QT, KT, vtok)
            attn_chunks_pair(p, B, QT, KT, vtok, attnT, pS, pO, 62, scale,
                             store)
            if getattr(B, "dbg_attn_post", None):
                B.dbg_attn_post(p, B, hp)

        tok_x = {}
        tok_wq_prev = p.cnt["pe"]
        for quarter in range(4):
            tok_wp = stream_cast(p, B, d[wo_key], quarter * 256, 2, 0,
                                 1024, wbuf, 0, None, war_pe=tok_wq_prev)
            p.dwait("tensor", tok_wp)
            p.wait("tensor", "dve", B._tok_ctx_evac)
            for i in range(NT):
                par = i % 2
                p.wait("tensor", "dve", tok_x.get((quarter, i - 2), 0))
                for oc in range(2):
                    for hpq in range(2):
                        hp = quarter * 2 + hpq
                        tm = p.inc(nc.tensor.matmul(
                            pS[:, par * 1024 + oc * 512:
                               par * 1024 + oc * 512 + 512],
                            B.arena[:, hp * 2048 + i * 128: hp * 2048 + i * 128 + 128],
                            wbuf[:, hpq * 1024 + oc * 512: hpq * 1024 + (oc + 1) * 512],
                            start=(hpq == 0), stop=(hpq == 1)), "pe")
                p.wait("vector", "pe", tm)
                for oc in range(2):
                    t_ = p.inc(nc.vector.tensor_tensor(
                        B.X[:, i * D + oc * 512: i * D + (oc + 1) * 512],
                        B.X[:, i * D + oc * 512: i * D + (oc + 1) * 512],
                        pS[:, par * 1024 + oc * 512: par * 1024 + (oc + 1) * 512],
                        ALU.add), "dve")
                tok_x[(quarter, i)] = t_
            tok_wq_prev = p.cnt["pe"]
            p.wait("tensor", "dve", t_)
    B._tok_ctx_evac = 0
    p.barrier_into(("vector", "scalar", "tensor", "gpsimd", "sync"))


def emit_l1_ffn(p, B):
    nc = p.nc
    d = B.d
    with (
        nc.sbuf_tensor(uname("f1_w"), [128, 1536], BF16) as wsm,
        nc.sbuf_tensor(uname("f1_wout"), [64, 1024], BF16) as wout,
        nc.sbuf_tensor(uname("f1_cols"), [128, 16], F32) as cols,
        nc.sbuf_tensor(uname("f1_rows"), [1, 1024], BF16) as brow,
        nc.sbuf_tensor(uname("f1_st"), [64, 2048], BF16) as sT,
        nc.sbuf_tensor(uname("f1_gt"), [64, 2048], BF16) as gT,
        nc.psum_tensor([128, 2048], F32) as ps,
        nc.psum_tensor([128, 512], F32) as ps2,
        nc.psum_tensor([128, 1024], F32) as ps3,
    ):
        tok_w = stream_cast(p, B, d["w_l1_win"], 0, 8, 0, 64, wsm, 0, None)
        gF = p.dgrp()
        p.dma("sync", wsm[0:64, 512:1536], d["c_l1_R"], gF)
        p.wait("sync", "act", p.cnt["act"])
        p.dma("sync", B.scr_bf[:, 0:512].rearrange("p (j c) -> p j c", c=64),
              d["c_l1_w2mask"].rearrange("(j p) c -> p j c", p=128), gF)
        p.dma("sync", cols[:, 0:8], d["w_l1_w1col"], gF)
        p.dma("sync", cols[:, 8:16], d["w_l1_b1col"], gF)
        p.dma("sync", B.colbuf[:, 96:104], d["w_l1_w2col"], gF)
        p.dma("sync", B.colbuf[0:64, 104:105], d["w_l1_bin"], gF)
        p.dma("sync", B.colbuf[0:64, 105:106], d["w_l1_b2"], gF)
        p.dma("sync", brow[:], d["b_l1_bout_row"], gF)
        gW2 = p.dgrp("gpsimd")
        p.dma("gpsimd", wout[:], d["w_l1_wout"], gW2)
        p.dwait("vector", gF)
        w2b = B.arena[:, 16384:16896]
        for j in range(8):
            tbd = p.inc(nc.vector.tensor_scalar(
                w2b[:, j * 64:(j + 1) * 64],
                B.scr_bf[:, j * 64:(j + 1) * 64],
                B.colbuf[:, 96 + j: 97 + j], None, ALU.mult), "dve")

        p.dwait("tensor", tok_w)
        p.wait("tensor", "dve", B.tok_xnT)
        for cc in range(4):
            for dt in range(ND):
                tm = p.inc(nc.tensor.matmul(
                    ps[0:64, cc * 512:(cc + 1) * 512],
                    wsm[:, dt * 64:(dt + 1) * 64],
                    xnt(B, dt, rs(cc * 512, (cc + 1) * 512)),
                    start=(dt == 0), stop=(dt == ND - 1)), "pe")
        p.wait("scalar", "pe", tm)
        p.dwait("scalar", gF)
        tst = p.inc(nc.scalar.activation(sT[:], ps[0:64, 0:2048], AF.Identity,
                                         bias=B.colbuf[0:64, 104:105]), "act")
        p.wait("tensor", "act", tst)
        p.dwait("tensor", gF)
        p.wait("tensor", "dve", tbd)
        tg = 0
        th = 0
        for cc in range(4):
            csl = rs(cc * 512, (cc + 1) * 512)
            for j in range(8):
                p.wait("tensor", "dve", th)
                tmr = p.inc(nc.tensor.matmul(
                    ps2[:, 0:512],
                    wsm[0:64, 512 + j * 128: 512 + (j + 1) * 128],
                    sT[0:64, csl], start=True, stop=True), "pe")
                p.wait("vector", "pe", tmr)
                p.wait("vector", "pe", tg)
                th = p.inc(nc.vector.tensor_scalar(
                    B.scr_bf[:, 512:1024], ps2[:, 0:512], cols[:, j:j + 1],
                    cols[:, 8 + j: 8 + j + 1], ALU.mult, ALU.add), "dve")
                p.wait("vector", "dve", th)
                th = p.inc(nc.vector.tensor_scalar(
                    B.scr_bf[:, 512:1024], B.scr_bf[:, 512:1024], 0.0, None,
                    ALU.max), "dve")
                p.wait("tensor", "dve", th)
                tg = p.inc(nc.tensor.matmul(
                    ps[0:64, csl],
                    w2b[:, j * 64:(j + 1) * 64],
                    B.scr_bf[:, 512:1024],
                    start=(j == 0), stop=(j == 7)), "pe")
        p.wait("scalar", "pe", tg)
        tgt = p.inc(nc.scalar.activation(gT[:], ps[0:64, 0:2048], AF.Identity,
                                         bias=B.colbuf[0:64, 105:106]), "act")
        p.wait("tensor", "act", tgt)
        p.dwait("tensor", gW2)
        p.dwait("tensor", gF)
        tx = 0
        for i in range(NT):
            p.wait("tensor", "dve", tx)
            for oc in range(2):
                osl = rs(oc * 512, (oc + 1) * 512)
                tm = p.inc(nc.tensor.matmul(
                    ps3[:, osl], B.ones[0:1, :], brow[0:1, osl],
                    start=True, stop=False), "pe")
                tm = p.inc(nc.tensor.matmul(
                    ps3[:, osl], gT[0:64, ts(i, 128)], wout[0:64, osl],
                    start=False, stop=True), "pe")
            p.wait("vector", "pe", tm)
            for oc in range(2):
                tx = p.inc(nc.vector.tensor_tensor(
                    B.X[:, i * D + oc * 512: i * D + (oc + 1) * 512],
                    B.X[:, i * D + oc * 512: i * D + (oc + 1) * 512],
                    ps3[:, oc * 512:(oc + 1) * 512], ALU.add), "dve")
    p.barrier_into(("vector", "scalar", "tensor", "gpsimd", "sync"))


def emit_l0_ffn(p, B):
    nc = p.nc
    d = B.d
    S2 = float(1.0 / (127.0 * 127.0))
    with (
        nc.sbuf_tensor(uname("f0_wg"), [128, 4096], BF16) as wgbuf,
        nc.sbuf_tensor(uname("f0_wu"), [128, 4096], BF16) as wubuf,
        nc.sbuf_tensor(uname("f0_silu"), [128, 1024], BF16) as silu,
        nc.psum_tensor([128, 4096], F32) as pEO,
    ):
        tok_ep = [0, 0, 0, 0]
        tok_g2 = 0
        tok_hh = [0, 0]
        tok_g2h = [0, 0]
        tok_gmm = [0, 0, 0, 0]
        tok_umm = [0, 0, 0, 0]
        for tc in range(4):
            csl = rs(tc * 512, (tc + 1) * 512)
            tok_h = 0
            tok_gate_evac = 0
            for f in range(24):
                par = f % 2
                slot = f % 4
                tok_w = stream_cast(p, B, d["w_l0_gT"], 0, 8, f * 128, 128,
                                    wgbuf, slot * 1024, None,
                                    war_pe=tok_gmm[slot])
                p.dwait("tensor", tok_w)
                p.wait("tensor", "dve", B.tok_xnT)
                p.wait("tensor", "act", tok_g2h[par])
                p.wait("tensor", "dve", max(tok_ep))
                for dt in range(ND):
                    tm = p.inc(nc.tensor.matmul(
                        pEO[:, par * 512: par * 512 + 512],
                        wgbuf[:, slot * 1024 + dt * 128: slot * 1024 + (dt + 1) * 128],
                        xnt(B, dt, csl),
                        start=(dt == 0), stop=(dt == ND - 1)), "pe")
                tok_gmm[slot] = tm
                p.wait("scalar", "pe", tm)
                p.wait("scalar", "dve", tok_hh[par])
                tok_g1 = p.inc(nc.scalar.activation(
                    B.scr_f1[:, par * 512: par * 512 + 512],
                    pEO[:, par * 512: par * 512 + 512], AF.Tanh,
                    scale=float(0.5 / 127.0)), "act")
                tok_g2 = p.inc(nc.scalar.activation(
                    B.scr_f2[:, par * 512: par * 512 + 512],
                    pEO[:, par * 512: par * 512 + 512], AF.Identity,
                    scale=float(0.5 / 127.0)), "act")
                tok_g2h[par] = tok_g2
                p.wait("vector", "act", tok_g2)
                tok_gate_evac = p.inc(nc.vector.scalar_tensor_tensor(
                    silu[:, par * 512: par * 512 + 512],
                    B.scr_f1[:, par * 512: par * 512 + 512], 1.0,
                    B.scr_f2[:, par * 512: par * 512 + 512],
                    ALU.add, ALU.mult), "dve")
                tok_wu = stream_cast(p, B, d["w_l0_uT"], 0, 8, f * 128, 128,
                                     wubuf, slot * 1024, None,
                                     war_pe=tok_umm[slot])
                p.dwait("tensor", tok_wu)
                p.wait("tensor", "dve", tok_hh[par])
                for dt in range(ND):
                    tm = p.inc(nc.tensor.matmul(
                        pEO[:, 1024 + par * 512: 1024 + (par + 1) * 512],
                        wubuf[:, slot * 1024 + dt * 128: slot * 1024 + (dt + 1) * 128],
                        xnt(B, dt, csl),
                        start=(dt == 0), stop=(dt == ND - 1)), "pe")
                tok_umm[slot] = tm
                p.wait("vector", "pe", tm)
                p.wait("vector", "dve", tok_gate_evac)
                tok_h = p.inc(nc.vector.tensor_tensor(
                    B.arena[:, f * 512:(f + 1) * 512],
                    silu[:, par * 512: par * 512 + 512],
                    pEO[:, 1024 + par * 512: 1024 + (par + 1) * 512],
                    ALU.mult), "dve")
                tok_hh[par] = tok_h
            if getattr(B, "dbg_l0h", None):
                B.dbg_l0h(p, B, tc)
            if FFN_BAR >= 1:
                p.barrier_into(("vector", "scalar", "tensor", "gpsimd", "sync"))
            p.wait("tensor", "dve", tok_h)
            tok_dmm = [max(tok_gmm)] * 4
            for f in range(24):
                par = f % 2
                slot = f % 4
                tok_wd = stream_cast(p, B, d["w_l0_dT"], f * 128, 1, 0, 1024,
                                     wgbuf, slot * 1024, None,
                                     war_pe=tok_dmm[slot])
                p.dwait("tensor", tok_wd)
                for i in range(4):
                    for oc in range(2):
                        tm = p.inc(nc.tensor.matmul(
                            pEO[:, i * 1024 + oc * 512:
                                i * 1024 + (oc + 1) * 512],
                            B.arena[:, f * 512 + i * 128: f * 512 + (i + 1) * 128],
                            wgbuf[:, slot * 1024 + oc * 512:
                                  slot * 1024 + (oc + 1) * 512],
                            start=(f == 0), stop=(f == 23)), "pe")
                tok_dmm[slot] = tm
            tok_gmm = [tm, tm, tm, tm]
            if FFN_BAR >= 2:
                p.barrier_into(("vector", "scalar", "tensor", "gpsimd", "sync"))
            p.wait("vector", "pe", tm)
            for i in range(4):
                it = tc * 4 + i
                for oc in range(2):
                    xs = rs(it * D + oc * 512, it * D + (oc + 1) * 512)
                    tv = p.inc(nc.vector.scalar_tensor_tensor(
                        B.X[:, xs],
                        pEO[:, i * 1024 + oc * 512: i * 1024 + (oc + 1) * 512],
                        S2, B.X[:, xs], ALU.mult, ALU.add), "dve")
                tok_ep[i] = tv
            if FFN_BAR >= 3:
                p.barrier_into(("vector", "scalar", "tensor", "gpsimd", "sync"))
    p.barrier_into(("vector", "scalar", "tensor", "gpsimd", "sync"))


def emit_l0_attn(p, B):
    nc = p.nc
    d = B.d
    scale = 0.125
    wmix_d = d["l0_wmix"]
    with (
        nc.sbuf_tensor(uname("a0_wfull"), [128, 8192], BF16) as wfull,
        nc.sbuf_tensor(uname("a0_p64"), [128, 1024], BF16) as p64s,
        nc.sbuf_tensor(uname("a0_acol"), [128, 768], F32) as acol,
        nc.sbuf_tensor(uname("a0_bcol"), [128, 24], BF16) as bcol,
        nc.sbuf_tensor(uname("a0_abuf"), [128, 1024], BF16) as abuf,
        nc.sbuf_tensor(uname("a0_wmixT"), [128, 1024], BF16) as wmixT,
        nc.psum_tensor([128, 1024], F32) as pW,
        nc.psum_tensor([128, 512], F32) as pB2,
        nc.psum_tensor([128, 1024], BF16) as pT,
    ):
        gA = p.dgrp()
        gAb = p.dgrp("gpsimd")
        p.dma("sync", p64s[:].rearrange("p (a c) -> p a c", c=128),
              d["c_p64"].rearrange("(a p) c -> p a c", p=128), gA)
        for w, (ak, bk) in enumerate((("w_l0_aq", "w_l0_bq"),
                                      ("w_l0_ak", "w_l0_bk"),
                                      ("w_l0_av", "w_l0_bv"))):
            p.dma("sync", acol[:, w * 256:(w + 1) * 256], d[ak], gA)
            p.dma("gpsimd", bcol[:, w * 8:(w + 1) * 8], d[bk], gAb)
        p.dwait("vector", gA)
        p.dwait("tensor", gA)
        p.dwait("vector", gAb)
        p.dwait("tensor", gAb)
        tok_mm = 0
        tok_st = 0
        gWS = p.dgrp()
        for w, wk in enumerate(("w_l0_wq", "w_l0_wk", "w_l0_wv")):
            tok_w = stream_cast(p, B, d[wk], 0, 8, 0, 1024, wfull, 0, None,
                                war_pe=tok_mm)
            for h in range(16):
                p.wait("vector", "pe", tok_mm)
                for ot in range(8):
                    for ph in range(2):
                        ta = p.inc(nc.vector.tensor_scalar(
                            abuf[:, ot * 128 + ph * 64: ot * 128 + ph * 64 + 64],
                            p64s[:, ot * 128 + ph * 64: ot * 128 + ph * 64 + 64],
                            acol[:, w * 256 + h * 16 + ph * 8 + ot:
                                 w * 256 + h * 16 + ph * 8 + ot + 1],
                            None, ALU.mult), "dve")
                p.wait("tensor", "dve", ta)
                p.dwait("tensor", tok_w)
                p.wait("tensor", "dve", tok_wm_ev if (w or h) else 0)
                for ot in range(8):
                    for oc in range(2):
                        tok_mm = p.inc(nc.tensor.matmul(
                            pW[:, oc * 512:(oc + 1) * 512],
                            abuf[:, ot * 128:(ot + 1) * 128],
                            wfull[:, ot * 1024 + oc * 512:
                                  ot * 1024 + (oc + 1) * 512],
                            start=(ot == 0), stop=(ot == 7)), "pe")
                    tok_mm = p.inc(nc.tensor.matmul(
                        pB2[:, 0:1], abuf[:, ot * 128:(ot + 1) * 128],
                        bcol[:, w * 8 + ot: w * 8 + ot + 1],
                        start=(ot == 0), stop=(ot == 7)), "pe")
                p.wait("vector", "pe", tok_mm)
                for oc in range(2):
                    tv = p.inc(nc.vector.tensor_copy(
                        B.scr_bf[:, oc * 512:(oc + 1) * 512],
                        pW[:, oc * 512:(oc + 1) * 512]), "dve")
                tv = p.inc(nc.vector.tensor_copy(
                    B.colbuf[:, w * 16 + h: w * 16 + h + 1], pB2[:, 0:1]),
                    "dve")
                tok_wm_ev = tv
                p.wait("tensor", "dve", tv)
                for it in range(8):
                    tp = p.inc(nc.tensor.transpose(
                        pT[:, it * 128:(it + 1) * 128],
                        B.scr_bf[:, it * 128:(it + 1) * 128], B.ident), "pe")
                p.wait("vector", "pe", tp)
                p.dwait("vector", gWS)
                tv = p.inc(nc.vector.tensor_copy(wmixT[:], pT[:]), "dve")
                p.wait("sync", "dve", tv)
                p.dma("sync",
                      wmix_d[w * 16 + h].rearrange("(a p) c -> p a c", p=128),
                      wmixT[:].rearrange("p (a c) -> p a c", c=128), gWS)
                tok_st = p.cnt["dve"]
        p.dwait("sync", gWS)
    p.barrier_into(("vector", "scalar", "tensor", "gpsimd", "sync"))

    with (
        nc.sbuf_tensor(uname("b0_wmix"), [128, 3 * 1024], BF16) as wmix,
        nc.sbuf_tensor(uname("b0_qtm"), [128, 2048], BF16) as QTm,
        nc.sbuf_tensor(uname("b0_qtd"), [128, 2048], BF16) as QTd,
        nc.sbuf_tensor(uname("b0_ktm"), [128, 2048], BF16) as KTm,
        nc.sbuf_tensor(uname("b0_vtf"), [128, 2048], F32) as VTmf,
        nc.sbuf_tensor(uname("b0_vtok"), [128, 2 * NT * 66], BF16) as vtok,
        nc.sbuf_tensor(uname("b0_attnT"), [128, 2048], BF16) as attnT,
        nc.sbuf_tensor(uname("b0_n0"), [64, 2048], BF16) as n0buf,
        nc.sbuf_tensor(uname("b0_un"), [65, 2048], F32) as unbuf,
        nc.sbuf_tensor(uname("b0_ccol"), [64, 32], F32) as ccol,
        nc.sbuf_tensor(uname("b0_wo"), [128, 2048], BF16) as wobuf,
        nc.sbuf_tensor(uname("b0_worow"), [1, 1024], BF16) as worow,
        nc.psum_tensor([128, 2048], F32) as pS,
        nc.psum_tensor([128, 2048], F32) as pO,
    ):
        gB0 = p.dgrp()
        p.dma("sync", ccol[:], d["w_l0_col"], gB0)
        p.dma("sync", worow[:], d["b_l0_bo_row"], gB0)
        p.dwait("vector", gB0)
        p.dwait("tensor", gB0)
        for h in range(16):
            gH = p.dgrp()
            p.wait("sync", "pe", getattr(B, "_l0_projpe", 0))
            for w in range(3):
                p.dma("sync",
                      wmix[:, w * 1024:(w + 1) * 1024].rearrange(
                          "p (a c) -> p a c", c=128),
                      wmix_d[w * 16 + h].rearrange("(a p) c -> p a c", p=128),
                      gH)
            p.dwait("tensor", gH)
            p.wait("tensor", "dve", B._tok_ctx_evac)
            tv = 0
            for w in range(3):
                for cc in range(4):
                    for dt in range(ND):
                        tm = p.inc(nc.tensor.matmul(
                            pS[:, cc * 512:(cc + 1) * 512],
                            wmix[:, w * 1024 + dt * 128: w * 1024 + (dt + 1) * 128],
                            xnt(B, dt, rs(cc * 512, (cc + 1) * 512)),
                            start=(dt == 0), stop=(dt == ND - 1)), "pe")
                    p.wait("vector", "pe", tm)
                    if w < 2:
                        dstT = QTm if w == 0 else KTm
                        tv = p.inc(nc.vector.tensor_scalar(
                            dstT[:, cc * 512:(cc + 1) * 512],
                            pS[:, cc * 512:(cc + 1) * 512],
                            B.colbuf[:, w * 16 + h: w * 16 + h + 1],
                            None, ALU.add), "dve")
                    else:
                        tv = p.inc(nc.vector.tensor_scalar(
                            VTmf[:, cc * 512:(cc + 1) * 512],
                            pS[:, cc * 512:(cc + 1) * 512],
                            B.colbuf[:, w * 16 + h: w * 16 + h + 1],
                            None, ALU.add), "dve")
                p.wait("tensor", "dve", tv)
            B._l0_projpe = p.cnt["pe"]
            tok_qkv_pe = p.cnt["pe"]
            tok_qkv = tv
            tvt = emit_vtok(p, B, VTmf, vtok, pO, 2, 64, 66, tok_qkv)
            tnb = emit_negB_full(p, B, QTm, KTm, pS, scale, 100, tok_qkv_pe,
                                 tok_qkv)
            p.wait("tensor", "dve", max(tnb, tvt))
            p.wait("scalar", "dve", max(tnb, tvt))
            for p1 in range(2):
                gQ = p.dgrp()
                p.wait("sync", "dve", tok_qkv)
                p.wait("sync", "pe", p.cnt["pe"])
                p.dma("sync", QTd[0:64, :], QTm[p1 * 64: p1 * 64 + 64, :], gQ)
                p.dma("sync", QTd[64:128, :], QTm[p1 * 64: p1 * 64 + 64, :], gQ)
                p.dwait("tensor", gQ)
                attn_chunks_l0(p, B, QTd, KTm, vtok, attnT, pS, pO, 100, scale,
                               h, p1, n0buf, ccol, unbuf)
        tok_x = 0
        for quarter in range(4):
            tok_wp = stream_cast(p, B, d["w_l0_woT"], quarter * 256, 2, 0,
                                 1024, wobuf, 0, VTmf)
            p.dwait("tensor", tok_wp)
            p.wait("tensor", "dve", B._tok_ctx_evac)
            for i in range(NT):
                for oc in range(2):
                    if quarter == 0:
                        tm = p.inc(nc.tensor.matmul(
                            pS[:, oc * 512: oc * 512 + 512], B.ones[0:1, :],
                            worow[0:1, oc * 512:(oc + 1) * 512],
                            start=True, stop=False), "pe")
                    for hpq in range(2):
                        hp = quarter * 2 + hpq
                        tm = p.inc(nc.tensor.matmul(
                            pS[:, oc * 512: oc * 512 + 512],
                            B.arena[:, hp * 2048 + i * 128: hp * 2048 + i * 128 + 128],
                            wobuf[:, hpq * 1024 + oc * 512: hpq * 1024 + (oc + 1) * 512],
                            start=(hpq == 0 and quarter != 0),
                            stop=(hpq == 1)), "pe")
                    p.wait("vector", "pe", tm)
                    tok_x = p.inc(nc.vector.tensor_tensor(
                        B.X[:, i * D + oc * 512: i * D + (oc + 1) * 512],
                        B.X[:, i * D + oc * 512: i * D + (oc + 1) * 512],
                        pS[:, oc * 512: oc * 512 + 512], ALU.add), "dve")
                    p.wait("tensor", "dve", tok_x)
    B._tok_ctx_evac = 0
    B._l0_projpe = 0
    p.barrier_into(("vector", "scalar", "tensor", "gpsimd", "sync"))


def emit_negB_full(p, B, QT, KT, ps, scale, col_q, tok_ready_pe, tok_ready_dve):
    nc = p.nc
    tdv = 0
    tm_prev = tok_ready_pe
    for w, MT in ((0, QT), (1, KT)):
        p.wait("scalar", "pe", tm_prev)
        p.wait("scalar", "dve", max(tok_ready_dve, tdv))
        ta = p.inc(nc.scalar.activation(B.scr_bf[:, 0:2048], MT[:],
                                        AF.Square), "act")
        p.wait("tensor", "act", ta)
        p.wait("tensor", "dve", tdv)
        for cc in range(4):
            tm = p.inc(nc.tensor.matmul(
                ps[:, cc * 512:(cc + 1) * 512], B.ones[:, :],
                B.scr_bf[:, cc * 512:(cc + 1) * 512],
                start=True, stop=True), "pe")
        tm_prev = tm
        p.wait("vector", "pe", tm)
        for cc in range(4):
            tdv = p.inc(nc.vector.tensor_reduce(
                B.colbuf[:, 122 + cc: 123 + cc],
                ps[:, cc * 512:(cc + 1) * 512], AX.X, ALU.max), "dve")
        p.wait("vector", "dve", tdv)
        tdv = p.inc(nc.vector.tensor_reduce(
            B.colbuf[:, 120 + w: 121 + w], B.colbuf[:, 122:126],
            AX.X, ALU.max), "dve")
        p.wait("vector", "dve", tdv)
    c0 = B.colbuf[:, 120:121]
    c1 = B.colbuf[:, 121:122]
    tdv = p.inc(nc.vector.tensor_tensor(c0, c0, c1, ALU.mult), "dve")
    p.wait("scalar", "dve", tdv)
    tac = p.inc(nc.scalar.activation(c0, c0, AF.Ln), "act")
    p.wait("scalar", "act", tac)
    tac = p.inc(nc.scalar.activation(c0, c0, AF.Exp, scale=0.5), "act")
    p.wait("vector", "act", tac)
    tdv = p.inc(nc.vector.tensor_scalar(
        B.colbuf[:, col_q: col_q + 1], c0, -scale, None, ALU.mult), "dve")
    return tdv


def attn_chunks_l0(p, B, QTd, KTm, vtok, attnT, pS, pO, col_b, scale,
                   h, p1, n0buf, ccol, unbuf):
    nc = p.nc
    tok_exp = {}
    tok_av = {}
    for c in range(NC2):
        q0 = c * 512
        jmax = 4 * c + 3
        t_prev_ep = B._tok_ctx_evac

        def emit_av(j, par, qlo, tmask):
            p.wait("tensor", "act", tok_exp[(par, j)])
            if tmask:
                p.wait("tensor", "dve", tmask)
            ta = 0
            for p2 in range(2):
                for strip in range(2):
                    ta = p.inc(nc.tensor.matmul(
                        pO[0:65, strip * 512 + qlo: strip * 512 + 512],
                        vtok[ts(strip, 64),
                             p2 * NT * 66 + j * 66: p2 * NT * 66 + j * 66 + 65],
                        attnT[ts(strip, 64), (par * 2 + p2) * 512 + qlo:
                              (par * 2 + p2) * 512 + 512],
                        start=(j == 0 and p2 == 0),
                        stop=(j == jmax and p2 == 1),
                        tile_position=(strip * 64, 0)), "pe")
            tok_av[(par, j)] = ta
            return ta

        pend = None
        for j in range(jmax + 1):
            par = j % 2
            qlo = max(0, 128 * j - q0)
            p.wait("tensor", "act", tok_exp.get((par, j - 2), 0))
            if j == 0:
                p.wait("tensor", "dve", t_prev_ep)
            diag = 128 * j >= q0
            for p2 in range(2):
                tm = p.inc(nc.tensor.matmul(
                    pS[:, (par * 2 + p2) * 512 + qlo: (par * 2 + p2) * 512 + 512],
                    KTm[ts(p2, 64), ts(j, 128)],
                    QTd[ts(p2, 64), rs(q0 + qlo, q0 + 512)],
                    start=True, stop=not diag,
                    tile_position=(p2 * 64, 0)), "pe")
                if diag:
                    tm = p.inc(nc.tensor.matmul(
                        pS[:, (par * 2 + p2) * 512 + qlo:
                           (par * 2 + p2) * 512 + qlo + 128],
                        B.ident, B.negtri,
                        start=False, stop=True,
                        tile_position=(0, 0)), "pe")
            tok_s = tm
            if pend is not None:
                emit_av(*pend)
            p.wait("scalar", "pe", tok_s)
            p.wait("scalar", "pe", tok_av.get((par, j - 2), 0))
            pin = pS[:, par * 1024: par * 1024 + 1024].rearrange(
                "p (h q) -> p h q", q=512)[:, :, qlo:512]
            pout = attnT[:, par * 1024: par * 1024 + 1024].rearrange(
                "p (h q) -> p h q", q=512)[:, :, qlo:512]
            te = p.inc(nc.scalar.activation(
                pout, pin, AF.Exp, scale=scale,
                bias=B.colbuf[:, col_b: col_b + 1]), "act")
            tok_exp[(par, j)] = te
            pend = (j, par, qlo, 0)
        ta = emit_av(*pend)
        p.wait("vector", "pe", ta)
        tv = p.inc(nc.vector.tensor_copy(
            unbuf[0:65, c * 512:(c + 1) * 512], pO[0:65, 0:512]), "dve")
        p.wait("vector", "dve", tv)
        tv = p.inc(nc.vector.tensor_tensor(
            unbuf[0:65, c * 512:(c + 1) * 512],
            unbuf[0:65, c * 512:(c + 1) * 512],
            pO[0:65, 512:1024], ALU.add), "dve")
        p.wait("vector", "dve", tv)
        B._tok_ctx_evac = p.cnt["dve"]

    tv = p.inc(nc.vector.reciprocal(B.scr_f2[0:1, 0:1024],
                                    unbuf[64:65, 0:1024]), "dve")
    p.wait("vector", "dve", tv)
    tv = p.inc(nc.vector.reciprocal(B.scr_f1[0:1, 0:1024],
                                    unbuf[64:65, 1024:2048]), "dve")
    p.wait("tensor", "dve", tv)
    p.wait("tensor", "act", p.cnt["act"])
    tb = 0
    for cc in range(4):
        row = B.scr_f2 if cc < 2 else B.scr_f1
        tb = p.inc(nc.tensor.matmul(
            pS[0:64, cc * 512:(cc + 1) * 512], B.onesf[0:1, 0:64],
            row[0:1, (cc % 2) * 512:(cc % 2) * 512 + 512],
            start=True, stop=True), "pe")
    p.wait("vector", "pe", tb)
    for cc in range(4):
        sl = rs(cc * 512, (cc + 1) * 512)
        tv = p.inc(nc.vector.tensor_tensor(
            unbuf[0:64, sl], unbuf[0:64, sl], pS[0:64, sl], ALU.mult), "dve")
        p.wait("vector", "dve", tv)
        if p1 == 0:
            tv = p.inc(nc.vector.tensor_copy(
                n0buf[0:64, sl], unbuf[0:64, sl]), "dve")
        else:
            tv = p.inc(nc.vector.tensor_scalar(
                unbuf[0:64, sl], unbuf[0:64, sl],
                ccol[0:64, h * 2 + 1: h * 2 + 2], None, ALU.mult), "dve")
            p.wait("vector", "dve", tv)
            dst = B.arena[(h % 2) * 64: (h % 2) * 64 + 64,
                          (h // 2) * 2048 + cc * 512:
                          (h // 2) * 2048 + (cc + 1) * 512]
            tv = p.inc(nc.vector.scalar_tensor_tensor(
                dst, n0buf[0:64, sl],
                ccol[0:64, h * 2: h * 2 + 1], unbuf[0:64, sl],
                ALU.mult, ALU.add), "dve")
    B._tok_ctx_evac = p.cnt["dve"]


def emit_l2_ffn(p, B):
    nc = p.nc
    d = B.d
    with (
        nc.sbuf_tensor(uname("f2_w1"), [128, 4096], BF16) as w1buf,
        nc.sbuf_tensor(uname("f2_w2"), [128, 4096], BF16) as w2buf,
        nc.sbuf_tensor(uname("f2_b1"), [128, 32], F32) as b1col,
        nc.sbuf_tensor(uname("f2_rows"), [1, 3 * 1024], BF16) as rows3,
        nc.sbuf_tensor(uname("f2_gb"), [128, 2048], BF16) as gb,
        nc.psum_tensor([128, 4096], F32) as pEO,
    ):
        gL2 = p.dgrp()
        p.dma("sync", b1col[:], d["w_l2_b1col"], gL2)
        p.dma("sync", rows3[0:1, 0:1024], d["b_l2_b2_row"], gL2)
        p.dma("sync", rows3[0:1, 1024:2048], d["b_l2_gamma_row"], gL2)
        p.dma("sync", rows3[0:1, 2048:3072], d["b_l2_beta_row"], gL2)
        p.dwait("tensor", gL2)
        for gi in range(2):
            for oc in range(2):
                tm = p.inc(nc.tensor.matmul(
                    pEO[:, oc * 512:(oc + 1) * 512], B.ones[0:1, :],
                    rows3[0:1, (1 + gi) * 1024 + oc * 512:
                          (1 + gi) * 1024 + (oc + 1) * 512],
                    start=True, stop=True), "pe")
            p.wait("vector", "pe", tm)
            for oc in range(2):
                tgb = p.inc(nc.vector.tensor_copy(
                    gb[:, gi * 1024 + oc * 512: gi * 1024 + (oc + 1) * 512],
                    pEO[:, oc * 512:(oc + 1) * 512]), "dve")
            p.wait("tensor", "dve", tgb)

        mean, rstd = B.stats[:, 0:16], B.stats[:, 16:32]
        c_sum = B.colbuf[:, 110:111]
        c_m2 = B.colbuf[:, 108:109]
        c_nm2 = B.colbuf[:, 111:112]
        c_rs2 = B.colbuf[:, 109:110]
        tok_ep = [0, 0, 0, 0]
        tok_w1mm = [0, 0, 0, 0]
        tok_w2mm = [0, 0, 0, 0]
        for tc in range(4):
            csl = rs(tc * 512, (tc + 1) * 512)
            tok_h0 = 0
            for f in range(32):
                tok_w1 = stream_cast(p, B, d["w_l2_w1"], 0, 8, f * 128, 128,
                                     w1buf, (f % 4) * 1024, None,
                                     war_pe=tok_w1mm[f % 4])
                p.dwait("tensor", tok_w1)
                p.wait("tensor", "dve", B.tok_xnT)
                p.wait("tensor", "act", tok_h0)
                p.wait("tensor", "dve", max(tok_ep))
                for dt in range(ND):
                    tm = p.inc(nc.tensor.matmul(
                        pEO[:, (f % 2) * 512:(f % 2) * 512 + 512],
                        w1buf[:, (f % 4) * 1024 + dt * 128:
                              (f % 4) * 1024 + (dt + 1) * 128],
                        xnt(B, dt, csl),
                        start=(dt == 0), stop=(dt == ND - 1)), "pe")
                tok_w1mm[f % 4] = tm
                p.wait("scalar", "pe", tm)
                p.dwait("scalar", gL2)
                tok_h0 = p.inc(nc.scalar.activation(
                    B.arena[:, f * 512:(f + 1) * 512],
                    pEO[:, (f % 2) * 512:(f % 2) * 512 + 512],
                    AF.Relu, bias=b1col[:, f:f + 1]), "act")
            p.wait("tensor", "act", tok_h0)
            for i in range(4):
                for oc in range(2):
                    tm = p.inc(nc.tensor.matmul(
                        pEO[:, i * 1024 + oc * 512: i * 1024 + (oc + 1) * 512],
                        B.ones[0:1, :],
                        rows3[0:1, oc * 512:(oc + 1) * 512],
                        start=True, stop=False), "pe")
            for f in range(32):
                tok_w2 = stream_cast(p, B, d["w_l2_w2"], f * 128, 1, 0, 1024,
                                     w2buf, (f % 4) * 1024, None,
                                     war_pe=tok_w2mm[f % 4])
                p.dwait("tensor", tok_w2)
                for i in range(4):
                    for oc in range(2):
                        tm = p.inc(nc.tensor.matmul(
                            pEO[:, i * 1024 + oc * 512:
                                i * 1024 + (oc + 1) * 512],
                            B.arena[:, f * 512 + i * 128: f * 512 + (i + 1) * 128],
                            w2buf[:, (f % 4) * 1024 + oc * 512:
                                  (f % 4) * 1024 + (oc + 1) * 512],
                            start=False, stop=(f == 31)), "pe")
                tok_w2mm[f % 4] = tm
            tok_eo = tm
            for i in range(4):
                it = tc * 4 + i
                p.wait("vector", "pe", tok_eo)
                tv = p.inc(nc.vector.tensor_scalar(
                    B.scr_f1[:, 0:1024], B.X[:, ts(it, D)],
                    mean[:, it:it + 1], rstd[:, it:it + 1],
                    ALU.subtract, ALU.mult), "dve")
                p.wait("vector", "dve", tv)
                for oc in range(2):
                    tv = p.inc(nc.vector.tensor_tensor(
                        B.scr_f2[:, oc * 512:(oc + 1) * 512],
                        B.scr_f1[:, oc * 512:(oc + 1) * 512],
                        pEO[:, i * 1024 + oc * 512: i * 1024 + (oc + 1) * 512],
                        ALU.add), "dve")
                p.wait("vector", "dve", tv)
                tok_ep[i] = tv
                p.wait("scalar", "dve", tv)
                ta = p.inc(nc.scalar.activation(
                    B.scr_bf[:, 0:1024], B.scr_f2[:, 0:1024], AF.Identity,
                    accum_out=c_sum), "act")
                p.wait("vector", "act", ta)
                tv = p.inc(nc.vector.tensor_scalar(c_m2, c_sum, 1.0 / D, None,
                                                   ALU.mult), "dve")
                tv = p.inc(nc.vector.tensor_scalar(c_nm2, c_m2, -1.0, None,
                                                   ALU.mult), "dve")
                p.wait("vector", "dve", tv)
                p.wait("scalar", "dve", tv)
                ta = p.inc(nc.scalar.activation(
                    B.scr_bf[:, 0:1024], B.scr_f2[:, 0:1024], AF.Square,
                    bias=c_nm2, accum_out=c_sum), "act")
                p.wait("vector", "act", ta)
                tv = p.inc(nc.vector.tensor_scalar(c_rs2, c_sum, 1.0 / D, None,
                                                   ALU.mult), "dve")
                p.wait("scalar", "dve", tv)
                ta = p.inc(nc.scalar.activation(c_rs2, c_rs2, AF.Ln,
                                                bias=B.epscol), "act")
                p.wait("scalar", "act", ta)
                ta = p.inc(nc.scalar.activation(c_rs2, c_rs2, AF.Exp,
                                                scale=-0.5), "act")
                p.wait("vector", "act", ta)
                tv = p.inc(nc.vector.tensor_scalar(
                    B.scr_f1[:, 0:1024], B.scr_f2[:, 0:1024], c_m2, c_rs2,
                    ALU.subtract, ALU.mult), "dve")
                p.wait("vector", "dve", tv)
                tv = p.inc(nc.vector.tensor_tensor(
                    B.scr_f2[:, 0:1024], B.scr_f1[:, 0:1024],
                    gb[:, 0:1024], ALU.mult), "dve")
                p.wait("vector", "dve", tv)
                tv = p.inc(nc.vector.tensor_tensor(
                    B.scr_f2[:, 0:1024], B.scr_f2[:, 0:1024],
                    gb[:, 1024:2048], ALU.add), "dve")
                p.wait("vector", "dve", tv)
                tv = p.inc(nc.vector.tensor_tensor(
                    B.X[:, ts(it, D)], B.X[:, ts(it, D)],
                    B.scr_f2[:, 0:1024], ALU.add), "dve")
                tok_ep[i] = tv
    p.barrier_into(("vector", "scalar", "tensor", "gpsimd", "sync"))


def emit_l3_ffn(p, B):
    nc = p.nc
    d = B.d
    NE = 8
    M3 = 256
    import contextlib
    with contextlib.ExitStack() as _st3:
        _e = _st3.enter_context
        Tb = _e(nc.sbuf_tensor(uname("f3_T"), [8, 2048], F32))
        G8 = _e(nc.sbuf_tensor(uname("f3_G8"), [8, 2048], F32))
        Ttok = _e(nc.sbuf_tensor(uname("f3_tok"), [128, 128], F32))
        gtok = _e(nc.sbuf_tensor(uname("f3_gtok"), [128, 128], F32))
        selb = _e(nc.sbuf_tensor(uname("f3_sel"), [128, 128], F32))
        hardb = _e(nc.sbuf_tensor(uname("f3_hard"), [128, 128], F32))
        Gb = _e(nc.sbuf_tensor(uname("f3_G"), [128, 128], F32))
        Gbf = _e(nc.sbuf_tensor(uname("f3_Gbf"), [128, 128], BF16))
        rankb = _e(nc.sbuf_tensor(uname("f3_rank"), [128, 128], F32))
        taub = _e(nc.sbuf_tensor(uname("f3_tau"), [128, 8], F32))
        PeT = _e(nc.sbuf_tensor(uname("f3_pet"), [128, 16 * 256], BF16))
        PeL = _e(nc.sbuf_tensor(uname("f3_pel"), [128, 256], BF16))
        ReT = _e(nc.sbuf_tensor(uname("f3_ret"), [128, 8 * 256], BF16))
        Ye = _e(nc.sbuf_tensor(uname("f3_y"), [128, 2048], BF16))
        webuf = _e(nc.sbuf_tensor(uname("f3_we"), [128, 4 * 1024], BF16))
        gwbuf = _e(nc.sbuf_tensor(uname("f3_gw"), [128, 64], BF16))
        pA = _e(nc.psum_tensor([128, 2560], F32))
        pB16 = _e(nc.psum_tensor([128, 1024], BF16))
        pY = _e(nc.psum_tensor([128, 1024], F32))
        mean, rstd = B.stats[:, 0:16], B.stats[:, 16:32]
        p.wait("vector", "act", B.tok_lnstats)
        for i in range(NT):
            txn = p.inc(nc.vector.tensor_scalar(
                B.arena[:, i * 1024:(i + 1) * 1024], B.X[:, ts(i, D)],
                mean[:, i:i + 1], rstd[:, i:i + 1],
                ALU.subtract, ALU.mult), "dve")
        B._l3_xn_tok = txn
        tok_gw = stream_cast(p, B, d["w_l3_gate"], 0, 8, 0, 8, gwbuf, 0, None)
        p.dwait("tensor", tok_gw)
        p.wait("tensor", "dve", B.tok_xnT)
        for cc in range(4):
            for dt in range(ND):
                tm = p.inc(nc.tensor.matmul(
                    pA[0:8, cc * 512:(cc + 1) * 512],
                    gwbuf[:, dt * 8:(dt + 1) * 8],
                    xnt(B, dt, rs(cc * 512, (cc + 1) * 512)),
                    start=(dt == 0), stop=(dt == ND - 1)), "pe")
        p.wait("vector", "pe", tm)
        for cc in range(4):
            tv = p.inc(nc.vector.tensor_copy(
                Tb[:, cc * 512:(cc + 1) * 512],
                pA[0:8, cc * 512:(cc + 1) * 512]), "dve")
        p.wait("vector", "dve", tv)
        tv = p.inc(nc.vector.tensor_scalar(Tb[:], Tb[:], 1e-6, None, ALU.max),
                   "dve")
        p.wait("scalar", "dve", tv)
        ta = p.inc(nc.scalar.activation(Tb[:], Tb[:], AF.Ln), "act")
        p.wait("vector", "act", ta)
        p.wait("tensor", "act", ta)

        c_rm = B.colbuf[0:8, 110:111]
        c_nrm = B.colbuf[0:8, 111:112]
        c_se = B.colbuf[0:8, 112:113]
        rm2 = B.colbuf[:, 113:129] if False else None
        for it in range(8):
            tv = p.inc(nc.vector.tensor_reduce(c_rm, Tb[:], AX.X, ALU.max),
                       "dve")
            p.wait("vector", "dve", tv)
            tv = p.inc(nc.vector.tensor_scalar(c_nrm, c_rm, -1.0, None,
                                               ALU.mult), "dve")
            p.wait("scalar", "dve", tv)
            ta = p.inc(nc.scalar.activation(B.scr_bf[0:8, 0:2048], Tb[:],
                                            AF.Exp, bias=c_nrm,
                                            accum_out=c_se), "act")
            p.wait("scalar", "act", ta)
            ta = p.inc(nc.scalar.activation(c_se, c_se, AF.Ln), "act")
            p.wait("vector", "act", ta)
            tv = p.inc(nc.vector.tensor_tensor(c_se, c_se, c_rm, ALU.add),
                       "dve")
            p.wait("vector", "dve", tv)
            tv = p.inc(nc.vector.tensor_scalar(Tb[:], Tb[:], c_se, None,
                                               ALU.subtract), "dve")
            p.wait("tensor", "dve", tv)
            for i in range(NT):
                tp = p.inc(nc.tensor.transpose(
                    pA[:, 2048 + i * 8: 2048 + (i + 1) * 8],
                    Tb[0:8, ts(i, 128)], B.identf[0:8, 0:8]), "pe")
            p.wait("vector", "pe", tp)
            tv = p.inc(nc.vector.tensor_copy(Ttok[:], pA[:, 2048:2176]), "dve")
            p.wait("vector", "dve", tv)
            t3 = Ttok[:].rearrange("p (g e) -> p g e", e=8)
            rm16 = B.scr_f1[:, 0:16]
            sm16 = B.scr_f1[:, 16:32]
            tv = p.inc(nc.vector.tensor_reduce(rm16, t3, AX.X, ALU.max), "dve")
            p.wait("vector", "dve", tv)
            bc = rm16.rearrange("p (g o) -> p g o", o=1).broadcast_to([128, 16, 8])
            tv = p.inc(nc.vector.tensor_tensor(
                B.scr_f2[:, 0:128].rearrange("p (g e) -> p g e", e=8), t3, bc,
                ALU.subtract), "dve")
            p.wait("scalar", "dve", tv)
            ta = p.inc(nc.scalar.activation(B.scr_f2[:, 128:256],
                                            B.scr_f2[:, 0:128], AF.Exp), "act")
            p.wait("vector", "act", ta)
            tv = p.inc(nc.vector.tensor_reduce(
                sm16, B.scr_f2[:, 128:256].rearrange("p (g e) -> p g e", e=8),
                AX.X, ALU.add), "dve")
            p.wait("scalar", "dve", tv)
            ta = p.inc(nc.scalar.activation(sm16, sm16, AF.Ln), "act")
            p.wait("vector", "act", ta)
            tv = p.inc(nc.vector.tensor_tensor(sm16, sm16, rm16, ALU.add),
                       "dve")
            p.wait("vector", "dve", tv)
            bc2 = sm16.rearrange("p (g o) -> p g o", o=1).broadcast_to([128, 16, 8])
            tv = p.inc(nc.vector.tensor_tensor(t3, t3, bc2, ALU.subtract),
                       "dve")
            p.wait("tensor", "dve", tv)
            p.wait("tensor", "dve", tv)
            for i in range(NT):
                tp = p.inc(nc.tensor.transpose(
                    pA[0:8, i * 128:(i + 1) * 128],
                    Ttok[:, i * 8:(i + 1) * 8], B.identf[:, 0:128]), "pe")
            p.wait("vector", "pe", tp)
            for cc in range(4):
                tv = p.inc(nc.vector.tensor_copy(
                    Tb[:, cc * 512:(cc + 1) * 512],
                    pA[0:8, cc * 512:(cc + 1) * 512]), "dve")
            p.wait("vector", "dve", tv)
            p.wait("scalar", "dve", tv)
            p.wait("tensor", "dve", tv)
        ta = p.inc(nc.scalar.activation(G8[:], Tb[:], AF.Exp), "act")
        p.wait("vector", "act", ta)
        p.wait("tensor", "act", ta)
        for i in range(NT):
            tp = p.inc(nc.tensor.transpose(
                pA[:, 2048 + i * 8: 2048 + (i + 1) * 8],
                G8[0:8, ts(i, 128)], B.identf[0:8, 0:8]), "pe")
        p.wait("vector", "pe", tp)
        tv = p.inc(nc.vector.tensor_copy(gtok[:], pA[:, 2048:2176]), "dve")
        p.wait("vector", "dve", tv)
        lo = B.colbuf[0:8, 114:115]
        hi = B.colbuf[0:8, 115:116]
        mid = B.colbuf[0:8, 116:117]
        cntc = B.colbuf[0:8, 117:118]
        prd = B.colbuf[0:8, 118:119]
        pm1 = B.colbuf[0:8, 119:120]
        t1c = B.colbuf[0:8, 120:121]
        tv = p.inc(nc.vector.memset(lo, 0.0), "dve")
        tv = p.inc(nc.vector.memset(hi, 1.125), "dve")
        p.wait("vector", "dve", tv)
        for bi in range(42):
            tv = p.inc(nc.vector.tensor_tensor(mid, lo, hi, ALU.add), "dve")
            p.wait("vector", "dve", tv)
            tv = p.inc(nc.vector.tensor_scalar(mid, mid, 0.5, None, ALU.mult),
                       "dve")
            p.wait("vector", "dve", tv)
            tv = p.inc(nc.vector.tensor_scalar(B.scr_bf[0:8, 0:2048], G8[:],
                                               mid, None, ALU.is_ge), "dve")
            p.wait("vector", "dve", tv)
            tv = p.inc(nc.vector.tensor_reduce(cntc, B.scr_bf[0:8, 0:2048],
                                               AX.X, ALU.add), "dve")
            p.wait("vector", "dve", tv)
            tv = p.inc(nc.vector.tensor_scalar(prd, cntc, float(M3), None,
                                               ALU.is_ge), "dve")
            p.wait("vector", "dve", tv)
            tv = p.inc(nc.vector.tensor_tensor(t1c, mid, lo, ALU.subtract),
                       "dve")
            p.wait("vector", "dve", tv)
            tv = p.inc(nc.vector.tensor_tensor(t1c, t1c, prd, ALU.mult), "dve")
            p.wait("vector", "dve", tv)
            tv = p.inc(nc.vector.tensor_tensor(lo, lo, t1c, ALU.add), "dve")
            p.wait("vector", "dve", tv)
            tv = p.inc(nc.vector.tensor_scalar(pm1, prd, -1.0, 1.0, ALU.mult,
                                               ALU.add), "dve")
            p.wait("vector", "dve", tv)
            tv = p.inc(nc.vector.tensor_tensor(t1c, mid, hi, ALU.subtract),
                       "dve")
            p.wait("vector", "dve", tv)
            tv = p.inc(nc.vector.tensor_tensor(t1c, t1c, pm1, ALU.mult), "dve")
            p.wait("vector", "dve", tv)
            tv = p.inc(nc.vector.tensor_tensor(hi, hi, t1c, ALU.add), "dve")
            p.wait("vector", "dve", tv)
        p.wait("tensor", "dve", tv)
        tp = p.inc(nc.tensor.transpose(pA[:, 2048:2056][0:1, :],
                                       lo, B.identf[0:8, 0:8]), "pe")
        p.wait("vector", "pe", tp)
        tv = p.inc(nc.vector.tensor_copy(B.scr_f2[0:1, 0:8],
                                         pA[0:1, 2048:2056]), "dve")
        p.wait("tensor", "dve", tv)
        tp = p.inc(nc.tensor.matmul(pA[:, 2048:2056], B.onesf[0:1, :],
                                    B.scr_f2[0:1, 0:8], start=True, stop=True),
                   "pe")
        p.wait("vector", "pe", tp)
        tv = p.inc(nc.vector.tensor_copy(taub[:], pA[:, 2048:2056]), "dve")
        p.wait("vector", "dve", tv)
        for i in range(NT):
            tv = p.inc(nc.vector.tensor_tensor(
                selb[:, i * 8:(i + 1) * 8], gtok[:, i * 8:(i + 1) * 8],
                taub[:], ALU.is_ge), "dve")
        tv = p.inc(nc.vector.tensor_scalar(hardb[:], gtok[:], 0.5, None,
                                           ALU.is_gt), "dve")
        p.wait("vector", "dve", tv)
        acc = B.scr_f1[:, 0:1]
        tcol = B.scr_f1[:, 1:2]
        for i in range(NT):
            tv = p.inc(nc.vector.memset(acc, 1.0), "dve")
            p.wait("vector", "dve", tv)
            for e in range(7, -1, -1):
                selc = selb[:, i * 8 + e: i * 8 + e + 1]
                tv = p.inc(nc.vector.tensor_tensor(
                    Gb[:, i * 8 + e: i * 8 + e + 1], selc, acc, ALU.mult),
                    "dve")
                p.wait("vector", "dve", tv)
                if e:
                    tv = p.inc(nc.vector.tensor_tensor(tcol, selc, acc,
                                                       ALU.mult), "dve")
                    p.wait("vector", "dve", tv)
                    tv = p.inc(nc.vector.tensor_tensor(acc, acc, tcol,
                                                       ALU.subtract), "dve")
                    p.wait("vector", "dve", tv)
        tv = p.inc(nc.vector.tensor_tensor(Gb[:], Gb[:], hardb[:], ALU.mult),
                   "dve")
        p.wait("vector", "dve", tv)
        tv = p.inc(nc.vector.tensor_copy(Gbf[:], Gb[:]), "dve")
        p.wait("tensor", "dve", tv)
        for i in range(NT):
            for j in range(i + 1):
                lhs = B.triu if j == i else B.ones
                tm = p.inc(nc.tensor.matmul(
                    pA[:, 2176:2184], lhs, Gbf[:, j * 8:(j + 1) * 8],
                    start=(j == 0), stop=(j == i)), "pe")
            p.wait("vector", "pe", tm)
            tv = p.inc(nc.vector.tensor_tensor(
                rankb[:, i * 8:(i + 1) * 8], pA[:, 2176:2184],
                Gb[:, i * 8:(i + 1) * 8], ALU.mult), "dve")
            p.wait("vector", "dve", tv)
            tv = p.inc(nc.vector.tensor_scalar(
                rankb[:, i * 8:(i + 1) * 8], rankb[:, i * 8:(i + 1) * 8],
                -1.0, None, ALU.add), "dve")
            p.wait("tensor", "dve", tv)
        p.wait("vector", "dve", tv)
        tok_sc = 0
        for e in range(NE):
            for i in range(NT):
                tv = p.inc(nc.vector.tensor_scalar(
                    PeT[:, i * 256:(i + 1) * 256], B.iota256[:],
                    rankb[:, i * 8 + e: i * 8 + e + 1], None, ALU.is_equal),
                    "dve")
            tok_pet = tv
            p.wait("tensor", "dve", tok_pet)
            for dt in range(ND):
                p.wait("tensor", "dve", B._l3_xn_tok)
                for i in range(NT):
                    tm = p.inc(nc.tensor.matmul(
                        pY[:, 0:256],
                        B.arena[:, i * 1024 + dt * 128: i * 1024 + (dt + 1) * 128],
                        PeT[:, i * 256:(i + 1) * 256],
                        start=(i == 0), stop=(i == NT - 1)), "pe")
                p.wait("vector", "pe", tm)
                tv = p.inc(nc.vector.tensor_copy(
                    ReT[:, dt * 256:(dt + 1) * 256], pY[:, 0:256]), "dve")
                p.wait("tensor", "dve", tv)
            for half in range(2):
                tok_we = stream_cast(p, B, d["w_l3_experts"][e], half * 512, 4,
                                     0, 1024, webuf, 0, None)
                p.dwait("tensor", tok_we)
                for rt in range(2):
                    prt = pY if rt == 0 else pA
                    for dth in range(4):
                        dt = half * 4 + dth
                        for oc in range(2):
                            tm = p.inc(nc.tensor.matmul(
                                prt[:, oc * 512:(oc + 1) * 512],
                                ReT[:, dt * 256 + rt * 128:
                                    dt * 256 + rt * 128 + 128],
                                webuf[:, dth * 1024 + oc * 512:
                                      dth * 1024 + (oc + 1) * 512],
                                start=(dt == 0), stop=(dt == ND - 1)), "pe")
            p.wait("vector", "pe", tm)
            for rt in range(2):
                prt = pY if rt == 0 else pA
                for oc in range(2):
                    tv = p.inc(nc.vector.tensor_copy(
                        Ye[:, rt * 1024 + oc * 512: rt * 1024 + (oc + 1) * 512],
                        prt[:, oc * 512:(oc + 1) * 512]), "dve")
            p.wait("tensor", "dve", tv)
            for i in range(NT):
                p.wait("tensor", "dve", tok_sc)
                for rt in range(2):
                    tp = p.inc(nc.tensor.transpose(
                        pB16[:, rt * 128:(rt + 1) * 128],
                        PeT[:, i * 256 + rt * 128: i * 256 + (rt + 1) * 128],
                        B.ident), "pe")
                p.wait("vector", "pe", tp)
                tv = p.inc(nc.vector.tensor_copy(PeL[:, 0:256].rearrange(
                    "p (a c) -> p a c", c=128), pB16[:, 0:256].rearrange(
                    "p (a c) -> p a c", c=128)), "dve")
                p.wait("tensor", "dve", tv)
                for oc in range(2):
                    for rt in range(2):
                        tm = p.inc(nc.tensor.matmul(
                            pY[:, oc * 512:(oc + 1) * 512],
                            PeL[:, rt * 128:(rt + 1) * 128],
                            Ye[:, rt * 1024 + oc * 512: rt * 1024 + (oc + 1) * 512],
                            start=(rt == 0), stop=(rt == 1)), "pe")
                p.wait("vector", "pe", tm)
                for oc in range(2):
                    xs = rs(i * D + oc * 512, i * D + (oc + 1) * 512)
                    tok_sc = p.inc(nc.vector.tensor_tensor(
                        B.X[:, xs], B.X[:, xs],
                        pY[:, oc * 512:(oc + 1) * 512], ALU.add), "dve")
                p.wait("tensor", "dve", tok_sc)
    p.barrier_into(("vector", "scalar", "tensor", "gpsimd", "sync"))


def emit_l3_attn(p, B):
    nc = p.nc
    d = B.d
    scale = float(1.0 / np.sqrt(128.0))
    with (
        nc.sbuf_tensor(uname("a3_qt"), [128, 2048], BF16) as QT,
        nc.sbuf_tensor(uname("a3_kt"), [128, 2048], BF16) as KT1,
        nc.sbuf_tensor(uname("a3_vtf"), [128, 2048], F32) as VTf,
        nc.sbuf_tensor(uname("a3_vtok"), [128, NT * 128], BF16) as vtok,
        nc.sbuf_tensor(uname("a3_attnT"), [128, 1024], BF16) as attnT,
        nc.sbuf_tensor(uname("a3_pet"), [128, 2048], BF16) as peT,
        nc.sbuf_tensor(uname("a3_pe1"), [128, 1], F32) as pe1c,
        nc.sbuf_tensor(uname("a3_wbuf"), [128, 2048], BF16) as wbuf,
        nc.psum_tensor([128, 2048], F32) as pS,
        nc.psum_tensor([128, 512], F32) as pO,
        nc.psum_tensor([128, 512], F32) as pSum,
    ):
        gB3 = p.dgrp()
        p.dma("sync", peT[:], d["c_l3_peT"], gB3)
        p.dma("sync", pe1c[:], d["c_l3_pe1"], gB3)
        p.dwait("vector", gB3)
        p.dwait("scalar", gB3)
        gk = stream_cast(p, B, d["w_l3_kT"], 0, 8, 0, 128, wbuf, 0, None)
        gv = stream_cast(p, B, d["w_l3_vT"], 0, 8, 0, 128, wbuf, 1024, None)
        p.dwait("tensor", gk)
        p.dwait("tensor", gv)
        p.wait("tensor", "dve", B.tok_xnT)
        tv = 0
        for w in range(2):
            for cc in range(4):
                for dt in range(ND):
                    tm = p.inc(nc.tensor.matmul(
                        pS[:, cc * 512:(cc + 1) * 512],
                        wbuf[:, w * 1024 + dt * 128: w * 1024 + (dt + 1) * 128],
                        xnt(B, dt, rs(cc * 512, (cc + 1) * 512)),
                        start=(dt == 0), stop=(dt == ND - 1)), "pe")
                if w == 0:
                    p.wait("scalar", "pe", tm)
                    tv = p.inc(nc.scalar.activation(
                        KT1[:, cc * 512:(cc + 1) * 512],
                        pS[:, cc * 512:(cc + 1) * 512], AF.Identity,
                        bias=pe1c[:]), "act")
                else:
                    p.wait("vector", "pe", tm)
                    tv = p.inc(nc.vector.tensor_copy(
                        VTf[:, cc * 512:(cc + 1) * 512],
                        pS[:, cc * 512:(cc + 1) * 512]), "dve")
            if w == 0:
                p.wait("tensor", "act", tv)
            else:
                p.wait("tensor", "dve", tv)
        tok_kv = tv
        tvt = emit_vtok(p, B, VTf, vtok, pS, 1, 128, 128, tok_kv)
        p.wait("tensor", "dve", tvt)
        for h in range(8):
            tok_wq = stream_cast(p, B, d["w_l3_qT"], 0, 8, h * 128, 128, wbuf,
                                 0, None)
            p.dwait("tensor", tok_wq)
            p.wait("tensor", "dve", B._tok_ctx_evac)
            for cc in range(4):
                for dt in range(ND):
                    tm = p.inc(nc.tensor.matmul(
                        pS[:, cc * 512:(cc + 1) * 512],
                        wbuf[:, dt * 128:(dt + 1) * 128],
                        xnt(B, dt, rs(cc * 512, (cc + 1) * 512)),
                        start=(dt == 0), stop=(dt == ND - 1)), "pe")
                p.wait("vector", "pe", tm)
                tv = p.inc(nc.vector.tensor_tensor(
                    QT[:, cc * 512:(cc + 1) * 512],
                    pS[:, cc * 512:(cc + 1) * 512],
                    peT[:, cc * 512:(cc + 1) * 512], ALU.add), "dve")
            tok_q = tv
            p.wait("tensor", "dve", tok_q)
            tok_qkv_pe = p.cnt["pe"]
            tnb = emit_negB_full(p, B, QT, KT1, pS, scale, 100, tok_qkv_pe,
                                 tok_q)
            p.wait("scalar", "dve", tnb)
            p.wait("tensor", "dve", tnb)
            tok_exp = {}
            tok_av3 = {}
            for c in range(NC2):
                q0 = c * 512
                jmax = 4 * c + 3
                t_prev = B._tok_ctx_evac

                def emit_av3(j, par, qlo, tmask):
                    p.wait("tensor", "act", tok_exp[(par, j)])
                    if tmask:
                        p.wait("tensor", "dve", tmask)
                    ta = p.inc(nc.tensor.matmul(
                        pO[0:128, qlo:512],
                        vtok[:, j * 128:(j + 1) * 128],
                        attnT[:, par * 512 + qlo: par * 512 + 512],
                        start=(j == 0), stop=(j == jmax)), "pe")
                    ta = p.inc(nc.tensor.matmul(
                        pSum[0:1, qlo:512],
                        B.ones[:, 0:1],
                        attnT[:, par * 512 + qlo: par * 512 + 512],
                        start=(j == 0), stop=(j == jmax)), "pe")
                    tok_av3[(par, j)] = ta
                    return ta

                pend = None
                for j in range(jmax + 1):
                    par = j % 2
                    qlo = max(0, 128 * j - q0)
                    p.wait("tensor", "act", tok_exp.get((par, j - 2), 0))
                    if j == 0:
                        p.wait("tensor", "dve", t_prev)
                    diag = 128 * j >= q0
                    tm = p.inc(nc.tensor.matmul(
                        pS[:, par * 512 + qlo: par * 512 + 512],
                        KT1[:, ts(j, 128)],
                        QT[:, rs(q0 + qlo, q0 + 512)],
                        start=True, stop=not diag), "pe")
                    if diag:
                        tm = p.inc(nc.tensor.matmul(
                            pS[:, par * 512 + qlo: par * 512 + qlo + 128],
                            B.ident, B.negtri,
                            start=False, stop=True), "pe")
                    tok_s = tm
                    if pend is not None:
                        emit_av3(*pend)
                    p.wait("scalar", "pe", tok_s)
                    p.wait("scalar", "pe", tok_av3.get((par, j - 2), 0))
                    te = p.inc(nc.scalar.activation(
                        attnT[:, par * 512 + qlo: par * 512 + 512],
                        pS[:, par * 512 + qlo: par * 512 + 512],
                        AF.Exp, scale=scale,
                        bias=B.colbuf[:, 100:101]), "act")
                    tok_exp[(par, j)] = te
                    pend = (j, par, qlo, 0)
                ta = emit_av3(*pend)
                p.wait("vector", "pe", ta)
                tv = p.inc(nc.vector.reciprocal(B.scr_f2[0:1, 0:512],
                                                pSum[0:1, 0:512]), "dve")
                p.wait("vector", "dve", tv)
                tv = p.inc(nc.vector.tensor_copy(B.scr_f1[:, 0:512],
                                                 pO[:, 0:512]), "dve")
                p.wait("tensor", "dve", tv)
                p.wait("tensor", "act", tok_exp[(jmax % 2, jmax)])
                tb = p.inc(nc.tensor.matmul(
                    pS[:, 0:512], B.onesf[0:1, 0:128], B.scr_f2[0:1, 0:512],
                    start=True, stop=True), "pe")
                p.wait("vector", "pe", tb)
                tv = p.inc(nc.vector.tensor_tensor(
                    B.scr_f1[:, 0:512], B.scr_f1[:, 0:512], pS[:, 0:512],
                    ALU.mult), "dve")
                p.wait("vector", "dve", tv)
                tv = p.inc(nc.vector.tensor_copy(
                    B.arena[:, h * 2048 + c * 512: h * 2048 + (c + 1) * 512],
                    B.scr_f1[:, 0:512]), "dve")
                B._tok_ctx_evac = p.cnt["dve"]
        tok_x = 0
        for quarter in range(4):
            tok_wp = stream_cast(p, B, d["w_l3_oT"], quarter * 256, 2, 0,
                                 1024, wbuf, 0, None)
            p.dwait("tensor", tok_wp)
            p.wait("tensor", "dve", B._tok_ctx_evac)
            for i in range(NT):
                for oc in range(2):
                    for hpq in range(2):
                        hp = quarter * 2 + hpq
                        tm = p.inc(nc.tensor.matmul(
                            pS[:, oc * 512: oc * 512 + 512],
                            B.arena[:, hp * 2048 + i * 128: hp * 2048 + i * 128 + 128],
                            wbuf[:, hpq * 1024 + oc * 512: hpq * 1024 + (oc + 1) * 512],
                            start=(hpq == 0), stop=(hpq == 1)), "pe")
                    p.wait("vector", "pe", tm)
                    tok_x = p.inc(nc.vector.tensor_tensor(
                        B.X[:, i * D + oc * 512: i * D + (oc + 1) * 512],
                        B.X[:, i * D + oc * 512: i * D + (oc + 1) * 512],
                        pS[:, oc * 512: oc * 512 + 512], ALU.add), "dve")
                    p.wait("tensor", "dve", tok_x)
    B._tok_ctx_evac = 0
    p.barrier_into(("vector", "scalar", "tensor", "gpsimd", "sync"))



def _run(common, per_core, layer_groups, x0=None):
    x = x0
    for layers in layer_groups:
        nc = build_program(common, layers)
        in_maps = []
        for b in range(8):
            m = {k: v for k, v in common.items()}
            m["x_in"] = per_core[b]["x_in"] if x is None else x[b]
            in_maps.append(m)
        res = run_bass_kernel_spmd(nc, in_maps, core_ids=list(range(8)))
        x = [np.ascontiguousarray(res.results[b]["x_out"]) for b in range(8)]
    return np.stack(x, axis=0)


def kernel(**inputs):
    common, per_core = host_prep(inputs)
    out = _run(common, per_core, [[0, 1, 2, 3]])
    return out.astype(np.float32)
